# version 29
# baseline (speedup 1.0000x reference)
import os
import numpy as np
from contextlib import ExitStack
import concourse.bass as bass
import concourse.mybir as mybir
from concourse.bass_utils import run_bass_kernel_spmd

F32 = mybir.dt.float32
BF16 = mybir.dt.bfloat16
AF = mybir.ActivationFunctionType
ALU = mybir.AluOpType
AX = mybir.AxisListType

NCORE = 8
B = 4
T = 8192
C = 1024
DI = 2816
TK = T // NCORE
W = 512
TE = W + TK
ROWS = B * TK
CCH = C // 128
ICH = DI // 128
NEG_BIG = -1e30
EPS = 1e-5

SIM_DRAINS = os.environ.get("BASS_SIM_FP64") == "1"


def dr(eng):
    eng.drain()


def drh(eng):
    eng.drain()


def drs(eng):
    if SIM_DRAINS:
        eng.drain()


class Ctr:
    def __init__(self, sem):
        self.sem = sem
        self.v = 0

    def start(self, eng, dst, src):
        eng.dma_start(dst, src).then_inc(self.sem, 16)
        self.v += 16

    def wait(self, eng):
        eng.wait_ge(self.sem, self.v)


def make_identity(nc, ident):
    nc.gpsimd.memset(ident, 0.0)
    nc.gpsimd.drain()
    nc.gpsimd.affine_select(
        out=ident, in_=ident, compare_op=ALU.not_equal, fill=1.0,
        base=0, pattern=[[-1, ident.shape[0]]], channel_multiplier=1)


def emit_qd_head(ve, srcs, qm, qs, qr, qsh):
    drs(ve)
    for X, c0, nb in srcs:
        ve.tensor_reduce(qm[:, c0:c0 + nb],
                         X.rearrange("p (nb w) -> p nb w", w=16),
                         axis=AX.X, op=ALU.max, apply_absolute_value=True)
    a = srcs[0][1]
    z = srcs[-1][1] + srcs[-1][2]
    dr(ve)
    ve.tensor_scalar(qs[:, a:z], qm[:, a:z], 1.0 / 6.0, 1e-12,
                     op0=ALU.mult, op1=ALU.max)
    dr(ve)
    ve.reciprocal(qr[:, a:z], qs[:, a:z])
    ve.tensor_scalar(qsh[:, a:z], qs[:, a:z], 0.5, None, op0=ALU.mult)
    dr(ve)


def emit_qd_tail(ve, X, Q, qr, qsh, ta, tacc, tb, nb):
    w = nb * 16
    X3 = X.rearrange("p (nb w) -> p nb w", w=16)
    r_bc = qr.unsqueeze(2).broadcast_to((128, nb, 16))
    sh_bc = qsh.unsqueeze(2).broadcast_to((128, nb, 16))
    av, accv, bv = ta[:, :w], tacc[:, :w], tb[:, :w]
    drs(ve)
    ve.tensor_tensor(av.rearrange("p (nb w) -> p nb w", w=16), X3, r_bc, op=ALU.mult)
    drs(ve)
    ve.tensor_scalar(bv, av, -1.0, None, op0=ALU.mult)
    drs(ve)
    ve.tensor_tensor(av, av, bv, op=ALU.max)
    drs(ve)
    ve.tensor_scalar(accv, av, 0.25, None, op0=ALU.is_gt)
    for tau in (0.75, 1.25, 1.75):
        drs(ve)
        ve.scalar_tensor_tensor(accv, av, tau, accv, op0=ALU.is_gt, op1=ALU.add)
    ve.tensor_scalar(bv, av, 2.5, None, op0=ALU.is_gt)
    drs(ve)
    ve.scalar_tensor_tensor(bv, av, 3.5, bv, op0=ALU.is_gt, op1=ALU.add)
    drs(ve)
    ve.scalar_tensor_tensor(accv, bv, 2.0, accv, op0=ALU.mult, op1=ALU.add)
    drs(ve)
    ve.tensor_scalar(bv, av, 5.0, None, op0=ALU.is_gt)
    drs(ve)
    ve.scalar_tensor_tensor(accv, bv, 4.0, accv, op0=ALU.mult, op1=ALU.add)
    drs(ve)
    ve.tensor_tensor(accv.rearrange("p (nb w) -> p nb w", w=16),
                     accv.rearrange("p (nb w) -> p nb w", w=16), sh_bc, op=ALU.mult)
    ve.tensor_scalar(bv, X, 0.0, None, op0=ALU.is_ge)
    drs(ve)
    ve.tensor_scalar(bv, bv, 2.0, -1.0, op0=ALU.mult, op1=ALU.add)
    drs(ve)
    return ve.tensor_tensor(Q, accv, bv, op=ALU.mult)


def emit_qd_tail_s(ve, AX_, SG, Q, qr, qsh, ta, tacc, tb, nb):
    w = nb * 16
    r_bc = qr.unsqueeze(2).broadcast_to((128, nb, 16))
    sh_bc = qsh.unsqueeze(2).broadcast_to((128, nb, 16))
    av, accv, bv = ta[:, :w], tacc[:, :w], tb[:, :w]
    drs(ve)
    ve.tensor_tensor(av.rearrange("p (nb w) -> p nb w", w=16),
                     AX_.rearrange("p (nb w) -> p nb w", w=16), r_bc,
                     op=ALU.mult)
    drs(ve)
    ve.tensor_scalar(accv, av, 0.25, None, op0=ALU.is_gt)
    for tau in (0.75, 1.25, 1.75):
        drs(ve)
        ve.scalar_tensor_tensor(accv, av, tau, accv, op0=ALU.is_gt, op1=ALU.add)
    ve.tensor_scalar(bv, av, 2.5, None, op0=ALU.is_gt)
    drs(ve)
    ve.scalar_tensor_tensor(bv, av, 3.5, bv, op0=ALU.is_gt, op1=ALU.add)
    drs(ve)
    ve.scalar_tensor_tensor(accv, bv, 2.0, accv, op0=ALU.mult, op1=ALU.add)
    drs(ve)
    ve.tensor_scalar(bv, av, 5.0, None, op0=ALU.is_gt)
    drs(ve)
    ve.scalar_tensor_tensor(accv, bv, 4.0, accv, op0=ALU.mult, op1=ALU.add)
    drs(ve)
    ve.tensor_tensor(accv.rearrange("p (nb w) -> p nb w", w=16),
                     accv.rearrange("p (nb w) -> p nb w", w=16), sh_bc,
                     op=ALU.mult)
    drs(ve)
    return ve.tensor_tensor(Q, accv, SG, op=ALU.mult)


def emit_qd(ve, X, Q, qm, qs, qr, qsh, ta, tacc, tb, nb):
    emit_qd_head(ve, [(X, 0, nb)], qm, qs, qr, qsh)
    return emit_qd_tail(ve, X, Q, qr[:, 0:nb], qsh[:, 0:nb], ta, tacc, tb, nb)


def build_fused():
    nc = bass.Bass("TRN2", target_bir_lowering=False)
    xa = nc.dram_tensor("xa", [B * TE, C], F32, kind="ExternalInput").ap()
    xb = nc.dram_tensor("xb", [B, C], F32, kind="ExternalInput").ap()
    bmask_d = nc.dram_tensor("bmask", [128, 1], F32, kind="ExternalInput").ap()
    wkT = nc.dram_tensor("wkT", [C, C], F32, kind="ExternalInput").ap()
    wvT = nc.dram_tensor("wvT", [C, C], F32, kind="ExternalInput").ap()
    wrT = nc.dram_tensor("wrT", [C, C], F32, kind="ExternalInput").ap()
    tmH = nc.dram_tensor("tmH", [128, 3 * CCH], F32, kind="ExternalInput").ap()
    tdH = nc.dram_tensor("tdH", [128, CCH], F32, kind="ExternalInput").ap()
    tfH = nc.dram_tensor("tfH", [128, CCH], F32, kind="ExternalInput").ap()
    cvF_d = nc.dram_tensor("cvF", [128, 4 * CCH], F32, kind="ExternalInput").ap()
    wmod_d = nc.dram_tensor("wmod", [C, 6 * C], F32, kind="ExternalInput").ap()
    woT = nc.dram_tensor("woT", [C, C], F32, kind="ExternalInput").ap()
    wg_d = nc.dram_tensor("wg", [DI, C], F32, kind="ExternalInput").ap()
    wu_d = nc.dram_tensor("wu", [DI, C], F32, kind="ExternalInput").ap()
    wd_d = nc.dram_tensor("wd", [C, DI], F32, kind="ExternalInput").ap()
    outo = nc.dram_tensor("outo", [ROWS, C], F32, kind="ExternalOutput").ap()
    kvscr = nc.dram_tensor("kvscr", [2, CCH, 128, B * TE], F32, kind="Internal").ap()
    srscr = nc.dram_tensor("srscr", [C, ROWS], F32, kind="Internal").ap()
    yscr = nc.dram_tensor("yscr", [CCH, 128, ROWS], F32, kind="Internal").ap()
    x2scr = nc.dram_tensor("x2scr", [ROWS, C], F32, kind="Internal").ap()
    hqscr = nc.dram_tensor("hqscr", [32, 128, ICH * 128], BF16, kind="Internal").ap()

    idt = nc.alloc_sbuf_tensor("idt", [128, 128], F32)
    idt4 = nc.alloc_sbuf_tensor("idt4", [B, B], F32)
    idtb = nc.alloc_sbuf_tensor("idtb", [128, 128], BF16)
    epsv = nc.alloc_sbuf_tensor("epsv", [128, 1], F32)
    ones1 = nc.alloc_sbuf_tensor("ones1", [1, 128], F32)
    bmask = nc.alloc_sbuf_tensor("bmaskT", [128, 1], F32)
    scf = nc.alloc_sbuf_tensor("scf", [128, 4 * CCH], F32)
    scf_bf = nc.alloc_sbuf_tensor("scf_bf", [128, 4 * CCH], BF16)
    modF = nc.alloc_sbuf_tensor("modF", [128, 6 * CCH * 4], F32)
    ca1F = nc.alloc_sbuf_tensor("ca1F", [128, CCH * 4], F32)
    cb1F = nc.alloc_sbuf_tensor("cb1F", [128, CCH * 4], F32)
    gaF = nc.alloc_sbuf_tensor("gaF", [128, CCH * 4], F32)
    gbF = nc.alloc_sbuf_tensor("gbF", [128, CCH * 4], F32)
    sgt = nc.alloc_sbuf_tensor("sgt", [128, CCH * 4], F32)
    tmF = nc.alloc_sbuf_tensor("tmF", [128, 3 * CCH], F32)
    tm1F = nc.alloc_sbuf_tensor("tm1F", [128, 3 * CCH], F32)
    wF = nc.alloc_sbuf_tensor("wF", [128, CCH], F32)
    uF = nc.alloc_sbuf_tensor("uF", [128, CCH], F32)
    vrow = nc.alloc_sbuf_tensor("vrow", [1, C], F32)
    st6 = nc.alloc_sbuf_tensor("st6", [128, 12], F32)
    mv = nc.alloc_sbuf_tensor("mv", [128, 2], F32)
    sd = nc.alloc_sbuf_tensor("sd", [128, 1], F32)
    sinv = nc.alloc_sbuf_tensor("sinv", [128, 1], F32)
    nbv = nc.alloc_sbuf_tensor("nbv", [128, 1], F32)
    qd_m = nc.alloc_sbuf_tensor("qd_m", [128, 240], F32)
    qd_s = nc.alloc_sbuf_tensor("qd_s", [128, 240], F32)
    qd_r = nc.alloc_sbuf_tensor("qd_r", [128, 240], F32)
    qd_sh = nc.alloc_sbuf_tensor("qd_sh", [128, 240], F32)

    ps = [nc.alloc_psum_tensor(f"ps{i}", [128, 512], F32) for i in range(4)]
    psb = [nc.alloc_psum_tensor(f"psb{i}", [128, 512], BF16) for i in range(2)]
    pmod = nc.alloc_psum_tensor("pmod", [128, 192], F32)

    dma = Ctr(nc.alloc_semaphore("dmas"))
    dmo = Ctr(nc.alloc_semaphore("dmao"))

    def ln_rowstats(xt, p=128):
        with nc.Block() as blk:
            @blk.vector
            def _(ve):
                ve.bn_stats(st6[0:p, 0:6], xt[:, 0:512])
                ve.bn_stats(st6[0:p, 6:12], xt[:, 512:1024])
                dr(ve)
                ve.bn_aggr(mv[0:p, :], st6[0:p, :])
        with nc.Block() as blk:
            @blk.scalar
            def _(sc):
                sc.activation(sd[0:p, :], mv[0:p, 1:2], AF.Sqrt, bias=epsv[0:p, 0:1])
        with nc.Block() as blk:
            @blk.vector
            def _(ve):
                ve.reciprocal(sinv[0:p, :], sd[0:p, :])
                dr(ve)
                ve.scalar_tensor_tensor(nbv[0:p, :], mv[0:p, 0:1], -1.0, sinv[0:p, :],
                                        op0=ALU.mult, op1=ALU.mult)

    with nc.Block() as blk:
        @blk.gpsimd
        def _(g):
            make_identity(nc, idt[:])
            make_identity(nc, idt4[:])
            make_identity(nc, idtb[:])
            g.memset(epsv[:], EPS)
            g.memset(ones1[:], 1.0)

        @blk.sync
        def _(sync):
            dma.start(sync, tmF[:], tmH)
            dma.start(sync, wF[:], tdH)
            dma.start(sync, uF[:], tfH)
            dma.start(sync, bmask[:], bmask_d)
            dma.start(sync, scf[:], cvF_d)
            dma.wait(sync)

    with nc.Block() as blk:
        @blk.vector
        def _(ve):
            ve.tensor_scalar(tm1F[:], tmF[:], -1.0, 1.0, op0=ALU.mult, op1=ALU.add)

        @blk.scalar
        def _(sc):
            sc.activation(wF[:], wF[:], AF.Exp)
            dr(sc)
            sc.activation(scf[:], scf[:], AF.Silu)

    with nc.Block() as blk:
        @blk.vector
        def _(ve):
            ve.tensor_scalar(wF[:], wF[:], -1.0, None, op0=ALU.mult)
            ve.tensor_copy(scf_bf[:], scf[:])

    with nc.sbuf_tensor("wmt", [128, 6 * C], F32) as wmt_h, \
         nc.sbuf_tensor("wmb", [128, 6 * C], BF16) as wmb_h:
        wmt, wmb = wmt_h.ap(), wmb_h.ap()
        for ci in range(CCH):
            with nc.Block() as blk:
                @blk.sync
                def _(sync, ci=ci):
                    dma.start(sync, wmt, wmod_d[ci * 128:(ci + 1) * 128, :])
                    dma.wait(sync)
            with nc.Block() as blk:
                @blk.vector
                def _(ve):
                    ve.tensor_copy(wmb, wmt)
            with nc.Block() as blk:
                @blk.tensor
                def _(te, ci=ci):
                    for oc in range(48):
                        te.matmul(pmod[:, oc * 4:(oc + 1) * 4],
                                  wmb[:, oc * 128:(oc + 1) * 128],
                                  scf_bf[:, ci * 4:(ci + 1) * 4],
                                  start=True, stop=True)
            with nc.Block() as blk:
                @blk.vector
                def _(ve, ci=ci):
                    if ci == 0:
                        ve.tensor_copy(modF[:], pmod[:])
                    else:
                        ve.tensor_tensor(modF[:], modF[:], pmod[:], op=ALU.add)

    with nc.Block() as blk:
        @blk.vector
        def _(ve):
            ve.tensor_scalar(ca1F[:], modF[:, 32:64], 1.0, None, op0=ALU.add)
            ve.tensor_scalar(cb1F[:], modF[:, 128:160], 1.0, None, op0=ALU.add)
            for (s0, s1), dstF in (((64, 96), gaF), ((160, 192), gbF)):
                dr(ve)
                ve.tensor_scalar(sgt[:], modF[:, s0:s1], 0.0, None, op0=ALU.is_ge)
                dr(ve)
                ve.tensor_scalar(sgt[:], sgt[:], 2.0, -1.0, op0=ALU.mult, op1=ALU.add)
                ve.tensor_scalar(dstF[:], modF[:, s0:s1], -1.0, None, op0=ALU.mult)
                dr(ve)
                ve.tensor_tensor(dstF[:], dstF[:], modF[:, s0:s1], op=ALU.max)
                dr(ve)
                ve.tensor_scalar(dstF[:], dstF[:], 0.25, None, op0=ALU.max)
                dr(ve)
                ve.tensor_tensor(dstF[:], dstF[:], sgt[:], op=ALU.mult)

    def pmm(ng):
        return ps[ng % 2][:]

    with ExitStack() as stk:
        def sb(name, shape, dt):
            return stk.enter_context(nc.sbuf_tensor(name, shape, dt)).ap()
        wkb = sb("wkb", [128, CCH * C], BF16)
        wvb = sb("wvb", [128, CCH * C], BF16)
        wrb = sb("wrb", [128, CCH * C], BF16)
        h1F = sb("h1F", [128, CCH * (TE + 1)], F32)
        xkb = sb("xkb", [128, CCH * TE], BF16)
        xvb = sb("xvb", [128, CCH * TE], BF16)
        xrb = sb("xrb", [128, CCH * TK], BF16)
        xt2 = [sb(f"xt{i}", [128, C], F32) for i in range(3)]
        t12 = xt2
        st62 = [sb(f"st6_{i}", [128, 12], F32) for i in range(2)]
        mv2 = [sb(f"mv_{i}", [128, 2], F32) for i in range(2)]
        sd2 = [sb(f"sd_{i}", [128, 1], F32) for i in range(2)]
        si2 = [sb(f"si_{i}", [128, 1], F32) for i in range(2)]
        nb2 = [sb(f"nb_{i}", [128, 1], F32) for i in range(2)]
        pre2 = [sb(f"pre_{i}", [128, TE], F32) for i in range(2)]
        kv2 = [sb(f"kv_{i}", [128, 512], F32) for i in range(2)]
        xbt = sb("xbt", [B, C], F32)
        t1b = xbt
        hbF = sb("hbF", [128, CCH * 4], F32)

        if True:
            wpair = [xt2[0], xt2[1]]
            sFL = [nc.alloc_semaphore(f"p1FL{i}") for i in range(2)]
            sFC = nc.alloc_semaphore("p1FC")
            fitems = [(wdram, wbuf, ci)
                      for wdram, wbuf in ((wkT, wkb), (wvT, wvb), (wrT, wrb))
                      for ci in range(CCH)]
            with nc.Block() as blk:
                @blk.sync
                def _(sync):
                    for i, (wdram, wbuf, ci) in enumerate(fitems):
                        if i % 2 == 1:
                            continue
                        if i >= 2:
                            sync.wait_ge(sFC, i - 1)
                        sync.dma_start(wpair[0],
                                       wdram[ci * 128:(ci + 1) * 128, :]
                                       ).then_inc(sFL[0], 16)

                @blk.gpsimd
                def _(g_):
                    for i, (wdram, wbuf, ci) in enumerate(fitems):
                        if i % 2 == 0:
                            continue
                        if i >= 2:
                            g_.wait_ge(sFC, i - 1)
                        g_.dma_start(wpair[1],
                                     wdram[ci * 128:(ci + 1) * 128, :]
                                     ).then_inc(sFL[1], 16)
                        g_.wait_ge(sFL[1], 16 * (i // 2 + 1))

                @blk.vector
                def _(ve):
                    for i, (wdram, wbuf, ci) in enumerate(fitems):
                        ve.wait_ge(sFL[i % 2], 16 * (i // 2 + 1))
                        ve.tensor_copy(wbuf[:, ci * C:(ci + 1) * C],
                                       wpair[i % 2]).then_inc(sFC, 1)

        with nc.Block() as blk:
            @blk.sync
            def _(sync):
                dma.start(sync, xbt, xb)
                dma.wait(sync)
        with nc.Block() as blk:
            @blk.vector
            def _(ve):
                ve.bn_stats(st62[0][0:B, 0:6], xbt[:, 0:512])
                ve.bn_stats(st62[0][0:B, 6:12], xbt[:, 512:1024])
                dr(ve)
                ve.bn_aggr(mv2[0][0:B, :], st62[0][0:B, :])
        with nc.Block() as blk:
            @blk.scalar
            def _(sc):
                sc.activation(sd2[0][0:B, :], mv2[0][0:B, 1:2], AF.Sqrt,
                              bias=epsv[0:B, 0:1])
        with nc.Block() as blk:
            @blk.vector
            def _(ve):
                ve.reciprocal(si2[0][0:B, :], sd2[0][0:B, :])
                dr(ve)
                ve.scalar_tensor_tensor(nb2[0][0:B, :], mv2[0][0:B, 0:1], -1.0,
                                        si2[0][0:B, :], op0=ALU.mult, op1=ALU.mult)
        with nc.Block() as blk:
            @blk.scalar
            def _(sc):
                sc.activation(t1b, xbt, AF.Identity,
                              bias=nb2[0][0:B, 0:1], scale=si2[0][0:B, 0:1])
        with nc.Block() as blk:
            @blk.tensor
            def _(te):
                for ci in range(CCH):
                    te.transpose(ps[0][:, ci * 4:(ci + 1) * 4],
                                 t1b[:, ci * 128:(ci + 1) * 128], idt4[:])
        with nc.Block() as blk:
            @blk.vector
            def _(ve):
                ve.tensor_copy(hbF[:], ps[0][:, 0:32])
        with nc.Block() as blk:
            @blk.vector
            def _(ve):
                for ci in range(CCH):
                    for b in range(B):
                        col = ci * 4 + b
                        ve.scalar_tensor_tensor(
                            hbF[:, col:col + 1], hbF[:, col:col + 1],
                            ca1F[:, col:col + 1], modF[:, col:col + 1],
                            op0=ALU.mult, op1=ALU.add)
                dr(ve)
                ve.tensor_scalar(hbF[:], hbF[:], bmask[:, 0:1], None, op0=ALU.mult)

        sX = nc.alloc_semaphore("p1X")
        sBN = nc.alloc_semaphore("p1BN")
        sSQ = nc.alloc_semaphore("p1SQ")
        sNB = nc.alloc_semaphore("p1NB")
        sT1 = nc.alloc_semaphore("p1T1")
        sTR = nc.alloc_semaphore("p1TR")
        sEV = nc.alloc_semaphore("p1EV")
        sMD = nc.alloc_semaphore("p1MD")
        sMK = nc.alloc_semaphore("p1MK")
        sPR = nc.alloc_semaphore("p1PR")
        sMX = nc.alloc_semaphore("p1MX")
        sMM = nc.alloc_semaphore("p1MM")
        sEC = nc.alloc_semaphore("p1EC")
        kos = [nc.alloc_semaphore(f"p1KO{i}") for i in range(2)]
        NR = TE // 128
        NMIX = 3 * CCH
        NG = 2 * 3 * CCH + 2 * CCH

        groups = []
        for pi, (wbuf, xmb, tw) in enumerate(((wkb, xkb, TE), (wvb, xvb, TE),
                                              (wrb, xrb, TK))):
            for d in range(CCH):
                for th in range(tw // 512):
                    groups.append((pi, wbuf, xmb, tw, d, th))
        assert len(groups) == NG

        with nc.Block() as blk:
            @blk.gpsimd
            def _(g_):
                for b in range(B):
                    for r in range(NR):
                        n = b * NR + r
                        if n >= 3:
                            g_.wait_ge(sTR, n - 2)
                        g_.dma_start(xt2[n % 3][:],
                                     xa[(b * TE + r * 128):(b * TE + r * 128 + 128), :]
                                     ).then_inc(sX, 16)
                        g_.wait_ge(sX, 16 * (n + 1))

            @blk.sync
            def _(sync):
                for b in range(B):
                    for g, (pi, wbuf, xmb, tw, d, th) in enumerate(groups):
                        ng = b * NG + g
                        sync.wait_ge(sEC, ng + 1)
                        if pi < 2:
                            off2 = b * TE + th * 512
                            sync.dma_start(kvscr[pi, d, :, off2:off2 + 512],
                                           kv2[ng % 2][:]).then_inc(kos[ng % 2], 16)
                        else:
                            off2 = b * TK + th * 512
                            sync.dma_start(srscr[d * 128:(d + 1) * 128,
                                                 off2:off2 + 512],
                                           kv2[ng % 2][:]).then_inc(kos[ng % 2], 16)
                sync.wait_ge(kos[0], 16 * (B * NG // 2))
                sync.wait_ge(kos[1], 16 * (B * NG // 2))

            @blk.vector
            def _(ve):
                def bn(b, r):
                    n = b * NR + r
                    p = n % 2
                    ve.wait_ge(sX, 16 * (n + 1))
                    ve.bn_stats(st62[p][:, 0:6], xt2[n % 3][:, 0:512])
                    ve.bn_stats(st62[p][:, 6:12], xt2[n % 3][:, 512:1024])
                    dr(ve)
                    ve.bn_aggr(mv2[p][:], st62[p][:]).then_inc(sBN, 1)

                def recip(b, r):
                    n = b * NR + r
                    p = n % 2
                    ve.wait_ge(sSQ, n + 1)
                    if n >= 2:
                        ve.wait_ge(sT1, n - 1)
                    ve.reciprocal(si2[p][:], sd2[p][:])
                    dr(ve)
                    ve.scalar_tensor_tensor(nb2[p][:], mv2[p][:, 0:1], -1.0,
                                            si2[p][:], op0=ALU.mult, op1=ALU.mult
                                            ).then_inc(sNB, 1)

                def evac(b, r):
                    n = b * NR + r
                    ve.wait_ge(sTR, n + 1)
                    h3 = h1F.rearrange("p (ci x) -> p ci x", ci=CCH)
                    for g in range(2):
                        pt = ps[2] if g == 0 else ps[3]
                        dst = h3[:, g * 4:(g + 1) * 4, 1 + r * 128:1 + (r + 1) * 128]
                        inst = ve.tensor_copy(dst,
                                              pt[:].rearrange("p (ci x) -> p ci x", ci=4))
                    inst.then_inc(sEV, 1)

                for b in range(B):
                    if b > 0:
                        ve.wait_ge(sPR, NMIX * b)
                    bn(b, 0)
                    bn(b, 1)
                    for r in range(NR):
                        recip(b, r)
                        if r + 2 < NR:
                            bn(b, r + 2)
                        evac(b, r)
                    for ci in range(CCH):
                        ve.tensor_copy(h1F[:, ci * (TE + 1):ci * (TE + 1) + 1],
                                       hbF[:, ci * 4 + b:ci * 4 + b + 1])
                    ve.wait_ge(sMD, b + 1)
                    for ci in range(CCH):
                        inst = ve.tensor_scalar(
                            h1F[:, ci * (TE + 1) + W:ci * (TE + 1) + W + 1],
                            h1F[:, ci * (TE + 1) + W:ci * (TE + 1) + W + 1],
                            bmask[:, 0:1], None, op0=ALU.mult)
                    inst.then_inc(sMK, 1)
                    if b > 0:
                        ve.wait_ge(sMM, NG * b)
                    for mi, xmb, tw in ((0, xkb, TE), (1, xvb, TE), (2, xrb, TK)):
                        off = 0 if tw == TE else W
                        for ci in range(CCH):
                            i = b * NMIX + mi * CCH + ci
                            ve.wait_ge(sPR, i + 1)
                            ve.scalar_tensor_tensor(
                                xmb[:, ci * tw:ci * tw + tw],
                                h1F[:, ci * (TE + 1) + off + 1:ci * (TE + 1) + off + tw + 1],
                                tmF[:, mi * CCH + ci:mi * CCH + ci + 1],
                                pre2[i % 2][:, 0:tw], op0=ALU.mult, op1=ALU.add
                                ).then_inc(sMX, 1)

            @blk.scalar
            def _(sc):
                def sqt(b2, r):
                    n = b2 * NR + r
                    p = n % 2
                    sc.wait_ge(sBN, n + 1)
                    sc.activation(sd2[p][:], mv2[p][:, 1:2], AF.Sqrt,
                                  bias=epsv[:, 0:1]).then_inc(sSQ, 1)
                    sc.wait_ge(sNB, n + 1)
                    if n >= 3:
                        sc.wait_ge(sTR, n - 2)
                    sc.activation(t12[n % 3][:], xt2[n % 3][:], AF.Identity,
                                  bias=nb2[p][:, 0:1], scale=si2[p][:, 0:1]
                                  ).then_inc(sT1, 1)

                for b in range(B):
                    if b == 0:
                        for r in range(NR):
                            sqt(0, r)
                    sc.wait_ge(sEV, NR * (b + 1))
                    for ci in range(CCH):
                        col = ci * 4 + b
                        inst = sc.activation(
                            h1F[:, ci * (TE + 1) + 1:(ci + 1) * (TE + 1)],
                            h1F[:, ci * (TE + 1) + 1:(ci + 1) * (TE + 1)],
                            AF.Identity, bias=modF[:, col:col + 1],
                            scale=ca1F[:, col:col + 1])
                    inst.then_inc(sMD, 1)
                    sc.wait_ge(sMK, b + 1)
                    for mi, tw in ((0, TE), (1, TE), (2, TK)):
                        off = 0 if tw == TE else W
                        for ci in range(CCH):
                            i = b * NMIX + mi * CCH + ci
                            if i >= 2:
                                sc.wait_ge(sMX, i - 1)
                            sc.activation(
                                pre2[i % 2][:, 0:tw],
                                h1F[:, ci * (TE + 1) + off:ci * (TE + 1) + off + tw],
                                AF.Identity,
                                scale=tm1F[:, mi * CCH + ci:mi * CCH + ci + 1]
                                ).then_inc(sPR, 1)
                    for g, (pi, wbuf, xmb, tw, d, th) in enumerate(groups):
                        ng = b * NG + g
                        sc.wait_ge(sMM, ng + 1)
                        if ng >= 2:
                            sc.wait_ge(kos[ng % 2], 16 * (ng // 2))
                        if pi < 2:
                            sc.activation(kv2[ng % 2][:], pmm(ng), AF.Identity
                                          ).then_inc(sEC, 1)
                        else:
                            sc.activation(kv2[ng % 2][:], pmm(ng), AF.Sigmoid
                                          ).then_inc(sEC, 1)
                        if b + 1 < B and g >= 1 and (g - 1) % 5 == 0 \
                                and (g - 1) // 5 < NR:
                            sqt(b + 1, (g - 1) // 5)

            @blk.tensor
            def _(te):
                def trow(b2, r):
                    n = b2 * NR + r
                    te.wait_ge(sT1, n + 1)
                    if n >= 1:
                        te.wait_ge(sEV, n)
                    for ci in range(CCH):
                        pt = ps[2] if ci < 4 else ps[3]
                        inst = te.transpose(
                            pt[:, (ci % 4) * 128:(ci % 4 + 1) * 128],
                            t12[n % 3][:, ci * 128:(ci + 1) * 128], idt[:])
                    inst.then_inc(sTR, 1)

                for r in range(NR):
                    trow(0, r)
                for b in range(B):
                    for g, (pi, wbuf, xmb, tw, d, th) in enumerate(groups):
                        ng = b * NG + g
                        te.wait_ge(sMX, b * NMIX + 8 * (pi + 1))
                        if ng >= 2:
                            te.wait_ge(sEC, ng - 1)
                        pm = pmm(ng)
                        for ci in range(CCH):
                            inst = te.matmul(
                                pm,
                                wbuf[:, ci * C + d * 128:ci * C + (d + 1) * 128],
                                xmb[:, ci * tw + th * 512:ci * tw + th * 512 + 512],
                                start=(ci == 0), stop=(ci == CCH - 1))
                        inst.then_inc(sMM, 1)
                        if b + 1 < B and g >= 2 and (g - 2) % 5 == 0 \
                                and (g - 2) // 5 < NR:
                            trow(b + 1, (g - 2) // 5)
    with ExitStack() as stk:
        def sb(name, shape, dt):
            return stk.enter_context(nc.sbuf_tensor(name, shape, dt)).ap()
        kt = [sb(f"kt{i}", [128, TE], F32) for i in range(2)]
        vt = [sb(f"vt{i}", [128, TE], F32) for i in range(2)]
        ppb = [sb(f"ppb{i}", [128, TE + 1], F32) for i in range(2)]
        aab = [sb(f"aab{i}", [128, TE + 1], F32) for i in range(2)]
        bbb = [sb(f"bbb{i}", [128, TE + 1], F32) for i in range(2)]
        e1b = [sb(f"e1b{i}", [128, TE], F32) for i in range(2)]
        e2b = [sb(f"e2b{i}", [128, TE], F32) for i in range(2)]
        d1 = [sb(f"d1_{i}", [128, TE], F32) for i in range(2)]
        gar = [sb(f"gar{i}", [128, TK], F32) for i in range(2)]
        gv = [sb(f"gv{i}", [128, TK], F32) for i in range(2)]
        num = [sb(f"num{i}", [128, TK], F32) for i in range(2)]
        den = [sb(f"den{i}", [128, TK], F32) for i in range(2)]
        yv = [sb(f"yv{i}", [128, TK], F32) for i in range(2)]

        lds = [nc.alloc_semaphore(f"p2LD{i}") for i in range(2)]
        sMS = nc.alloc_semaphore("p2MS")
        sVA = nc.alloc_semaphore("p2VA")
        sSE = nc.alloc_semaphore("p2SE")
        sVN = nc.alloc_semaphore("p2VN")
        sSD = nc.alloc_semaphore("p2SD")
        sVY = nc.alloc_semaphore("p2VY")
        sYO = nc.alloc_semaphore("p2YO")
        NIT = B * CCH

        def it_bc(i):
            return i // CCH, i % CCH

        with nc.Block() as blk:
            @blk.sync
            def _(sync):
                for i in range(NIT):
                    b, ci = it_bc(i)
                    p = i % 2
                    if i >= 2:
                        sync.wait_ge(sVN, i - 1)
                        sync.wait_ge(lds[p], 32 * (i // 2))
                    off = b * TE
                    sync.dma_start(kt[p][:], kvscr[0, ci, :, off:off + TE]
                                   ).then_inc(lds[p], 16)
                    sync.dma_start(vt[p][:], kvscr[1, ci, :, off:off + TE]
                                   ).then_inc(lds[p], 16)
                    if i >= 1:
                        bo, cio = it_bc(i - 1)
                        sync.wait_ge(sVY, i)
                        if i >= 2:
                            sync.wait_ge(sYO, 16 * (i - 1))
                        sync.dma_start(yscr[cio, :, bo * TK:(bo + 1) * TK],
                                       yv[(i - 1) % 2][:]).then_inc(sYO, 16)
                bo, cio = it_bc(NIT - 1)
                sync.wait_ge(sVY, NIT)
                sync.wait_ge(sYO, 16 * (NIT - 1))
                sync.dma_start(yscr[cio, :, bo * TK:(bo + 1) * TK],
                               yv[(NIT - 1) % 2][:]).then_inc(sYO, 16)
                sync.wait_ge(sYO, 16 * NIT)

            @blk.gpsimd
            def _(g):
                for i in range(NIT):
                    p = i % 2
                    if i >= 2:
                        g.wait_ge(sVN, i - 1)
                    g.memset(ppb[p][:, 0:1], NEG_BIG)
                    g.memset(aab[p][:, 0:1], 0.0)
                    g.memset(bbb[p][:, 0:1], 0.0)
                    g.drain()
                    g.sem_inc(sMS, 1)

            @blk.vector
            def _(ve):
                def S1(i):
                    b, ci = it_bc(i)
                    p = i % 2
                    ve.wait_ge(lds[p], 32 * (i // 2 + 1))
                    ve.wait_ge(sMS, i + 1)
                    ve.tensor_tensor_scan(
                        ppb[p][:, 1:TE + 1],
                        wF[:, ci:ci + 1].broadcast_to((128, TE)), kt[p][:],
                        ppb[p][:, 0:1], op0=ALU.add, op1=ALU.max)
                    drh(ve)
                    ve.tensor_tensor(e2b[p][:], kt[p][:], ppb[p][:, 1:TE + 1],
                                     op=ALU.subtract)
                    ve.tensor_tensor(e1b[p][:], ppb[p][:, 0:TE], ppb[p][:, 1:TE + 1],
                                     op=ALU.subtract)
                    ve.scalar_tensor_tensor(gar[p][:], kt[p][:, W:TE],
                                            uF[:, ci:ci + 1], ppb[p][:, W:TE],
                                            op0=ALU.add, op1=ALU.subtract
                                            ).then_inc(sVA, 1)

                def S2(i):
                    p = i % 2
                    ve.wait_ge(sSE, i + 1)
                    ve.tensor_tensor(d1[p][:], e2b[p][:], vt[p][:], op=ALU.mult)
                    ve.tensor_tensor(gv[p][:], gar[p][:], vt[p][:, W:TE], op=ALU.mult)
                    drs(ve)
                    ve.tensor_scalar(d1[p][:, 0:W], d1[p][:, 0:W], bmask[:, 0:1],
                                     None, op0=ALU.mult)
                    ve.tensor_scalar(e2b[p][:, 0:W], e2b[p][:, 0:W], bmask[:, 0:1],
                                     None, op0=ALU.mult)
                    drh(ve)
                    ve.tensor_tensor_scan(aab[p][:, 1:TE + 1], e1b[p][:], d1[p][:],
                                          aab[p][:, 0:1], op0=ALU.mult, op1=ALU.add)
                    ve.tensor_tensor_scan(bbb[p][:, 1:TE + 1], e1b[p][:], e2b[p][:],
                                          bbb[p][:, 0:1], op0=ALU.mult, op1=ALU.add)
                    drh(ve)
                    ve.tensor_tensor(num[p][:], aab[p][:, W:TE], gv[p][:], op=ALU.add)
                    ve.tensor_tensor(den[p][:], bbb[p][:, W:TE], gar[p][:], op=ALU.add
                                     ).then_inc(sVN, 1)

                def S3(i):
                    p = i % 2
                    ve.wait_ge(sSD, i + 1)
                    if i >= 2:
                        ve.wait_ge(sYO, 16 * (i - 1))
                    ve.tensor_tensor(yv[p][:], num[p][:], den[p][:], op=ALU.mult
                                     ).then_inc(sVY, 1)

                S1(0)
                for i in range(NIT):
                    S2(i)
                    if i + 1 < NIT:
                        S1(i + 1)
                    S3(i)

            @blk.scalar
            def _(sc):
                for i in range(NIT):
                    b, ci = it_bc(i)
                    p = i % 2
                    sc.wait_ge(sVA, i + 1)
                    sc.activation(e2b[p][:], e2b[p][:], AF.Exp)
                    sc.activation(e1b[p][:], e1b[p][:], AF.Exp, bias=wF[:, ci:ci + 1])
                    sc.activation(gar[p][:], gar[p][:], AF.Exp).then_inc(sSE, 1)
                    sc.wait_ge(sVN, i + 1)
                    sc.activation(den[p][:], den[p][:], AF.Ln)
                    dr(sc)
                    sc.activation(den[p][:], den[p][:], AF.Exp, scale=-1.0
                                  ).then_inc(sSD, 1)

    with ExitStack() as stk0:
        def sbw(name, shape, dt):
            return stk0.enter_context(nc.sbuf_tensor(name, shape, dt)).ap()
        wgqT = sbw("wgqT", [128, CCH * DI], BF16)
        wuqT = sbw("wuqT", [128, CCH * DI], BF16)
        ps4 = nc.alloc_psum_tensor("ps4", [128, 512], F32)

        with ExitStack() as stk1:
            def sb1(name, shape, dt):
                return stk1.enter_context(nc.sbuf_tensor(name, shape, dt)).ap()
            wscr1 = [sb1(f"wscr1_{i}", [128, C], F32) for i in range(4)]
            axs1 = [sb1(f"axs1_{i}", [128, C], F32) for i in range(4)]
            sgs1 = [sb1(f"sgs1_{i}", [128, C], F32) for i in range(4)]
            ta1 = sb1("qta1", [128, C], F32)
            tacc1 = sb1("qtacc1", [128, C], F32)
            tb1 = sb1("qtb1", [128, C], F32)
            qtile1 = [sb1(f"qtile1_{i}", [128, C], BF16) for i in range(4)]
            qdl_m = sb1("qdl_m", [128, 256], F32)
            qdl_s = sb1("qdl_s", [128, 256], F32)
            qdl_r = sb1("qdl_r", [128, 256], F32)
            qdl_sh = sb1("qdl_sh", [128, 256], F32)
            sWL = [nc.alloc_semaphore(f"p3WL{i}") for i in range(2)]
            sAS = nc.alloc_semaphore("p3AS")
            sQ1 = nc.alloc_semaphore("p3Q1")
            sT3 = nc.alloc_semaphore("p3T3")
            sE3 = nc.alloc_semaphore("p3E3")
            wjobs = [(wg_d, wgqT, wch) for wch in range(ICH)] + \
                    [(wu_d, wuqT, wch) for wch in range(ICH)]
            NWJ = len(wjobs)

            with nc.Block() as blk:
                @blk.gpsimd
                def _(g_):
                    for i, (wdram, wdst, wch) in enumerate(wjobs):
                        if i >= 4:
                            g_.wait_ge(sQ1, i - 3)
                        g_.dma_start(wscr1[i % 4], wdram[wch * 128:(wch + 1) * 128, :]
                                     ).then_inc(sWL[i % 2], 16)
                        g_.wait_ge(sWL[i % 2], 16 * (i // 2 + 1))

                @blk.scalar
                def _(sc):
                    for i, (wdram, wdst, wch) in enumerate(wjobs):
                        sc.wait_ge(sWL[i % 2], 16 * (i // 2 + 1))
                        if i >= 4:
                            sc.wait_ge(sQ1, i - 3)
                        sc.activation(axs1[i % 4][:], wscr1[i % 4][:], AF.Abs)
                        sc.activation(sgs1[i % 4][:], wscr1[i % 4][:], AF.Sign
                                      ).then_inc(sAS, 1)
                        if i >= 1:
                            _, wdst_p, wch_p = wjobs[i - 1]
                            sc.wait_ge(sT3, i)
                            d3 = wdst_p.rearrange("p (ci x) -> p ci x", ci=CCH)
                            for g in range(2):
                                inst = sc.activation(
                                    d3[:, g * 4:(g + 1) * 4,
                                       wch_p * 128:(wch_p + 1) * 128],
                                    psb[g][:].rearrange("p (ci x) -> p ci x",
                                                        ci=4), AF.Copy)
                            inst.then_inc(sE3, 1)
                    _, wdst_p, wch_p = wjobs[NWJ - 1]
                    sc.wait_ge(sT3, NWJ)
                    d3 = wdst_p.rearrange("p (ci x) -> p ci x", ci=CCH)
                    for g in range(2):
                        inst = sc.activation(
                            d3[:, g * 4:(g + 1) * 4,
                               wch_p * 128:(wch_p + 1) * 128],
                            psb[g][:].rearrange("p (ci x) -> p ci x", ci=4),
                            AF.Copy)
                    inst.then_inc(sE3, 1)

                @blk.vector
                def _(ve):
                    for g4 in range(NWJ // 4):
                        j0 = 4 * g4
                        for h in range(2):
                            ve.wait_ge(sWL[h], 16 * (j0 // 2 + 2))
                        if g4 >= 1:
                            ve.wait_ge(sT3, j0)
                        emit_qd_head(ve, [(wscr1[k], 64 * k, 64)
                                          for k in range(4)],
                                     qdl_m, qdl_s, qdl_r, qdl_sh)
                        for k in range(4):
                            j = j0 + k
                            ve.wait_ge(sAS, j + 1)
                            inst = emit_qd_tail_s(
                                ve, axs1[k], sgs1[k], qtile1[k],
                                qdl_r[:, 64 * k:64 * k + 64],
                                qdl_sh[:, 64 * k:64 * k + 64],
                                ta1, tacc1, tb1, C // 16)
                            inst.then_inc(sQ1, 1)

                @blk.tensor
                def _(te):
                    for i in range(NWJ):
                        te.wait_ge(sQ1, i + 1)
                        if i >= 1:
                            te.wait_ge(sE3, i)
                        for g in range(2):
                            for k in range(4):
                                ci = g * 4 + k
                                inst = te.transpose(
                                    psb[g][:, k * 128:(k + 1) * 128],
                                    qtile1[i % 4][:, ci * 128:(ci + 1) * 128],
                                    idtb[:])
                        inst.then_inc(sT3, 1)

        with ExitStack() as stk2:
            def sb2(name, shape, dt):
                return stk2.enter_context(nc.sbuf_tensor(name, shape, dt)).ap()
            wob = sb2("wob", [128, CCH * C], BF16)
            ga_bc = sb2("ga_bc", [128, C], F32)
            cb1_bc = sb2("cb1_bc", [128, C], F32)
            sb_bc = sb2("sb_bc", [128, C], F32)
            yt0 = sb2("yt0", [128, CCH * 128], F32)
            srx0 = sb2("srx0", [128, CCH * 128], F32)
            yt = [yt0, yt0]
            srx = [srx0, srx0]
            xrow = [sb2(f"xrow{i}", [128, C], F32) for i in range(2)]
            rwkv = sb2("rwkv", [128, CCH * 128], BF16)
            x2t = sb2("x2t", [128, C], F32)
            h2t = sb2("h2t", [128, C], F32)
            hqt = sb2("hqt", [128, C], BF16)
            hqF = sb2("hqF", [128, CCH * 128], BF16)
            atmp = sb2("atmp", [128, 512], F32)
            sg3 = [sb2(f"sg512_{i}", [128, 512], F32) for i in range(3)]
            hh = sb2("hh", [128, DI], F32)
            hhqt = sb2("hhqt", [128, DI], BF16)
            hhqF0 = sb2("hhqF0", [128, ICH * 128], BF16)
            hhqF = [hhqF0, hhqF0]
            HV = 1536
            ta2 = sb2("qta2", [128, HV], F32)
            tacc2 = sb2("qtacc2", [128, HV], F32)
            tb2 = sb2("qtb2", [128, HV], F32)
            GQ = (DI - HV) // 2

            wscr2 = hh[:, 0:C]
            for ci in range(CCH):
                with nc.Block() as blk:
                    @blk.sync
                    def _(sync, ci=ci):
                        dma.start(sync, wscr2, woT[ci * 128:(ci + 1) * 128, :])
                        dma.wait(sync)
                with nc.Block() as blk:
                    @blk.vector
                    def _(ve, ci=ci):
                        ve.tensor_copy(wob[:, ci * C:(ci + 1) * C], wscr2)

            sLD3 = [nc.alloc_semaphore(f"p3LD{i}") for i in range(2)]
            sAF3 = nc.alloc_semaphore("p3AF3")
            sABh = nc.alloc_semaphore("p3ABh")
            sABq = nc.alloc_semaphore("p3ABq")
            sVR = nc.alloc_semaphore("p3VR")
            sRP = nc.alloc_semaphore("p3RP")
            sRB = nc.alloc_semaphore("p3RB")
            sRW = nc.alloc_semaphore("p3RW")
            sWO = nc.alloc_semaphore("p3WO")
            sX2 = nc.alloc_semaphore("p3X2")
            sXO = nc.alloc_semaphore("p3XO")
            sBN2 = nc.alloc_semaphore("p3BN")
            sSQ2 = nc.alloc_semaphore("p3SQ")
            sNB2 = nc.alloc_semaphore("p3NB")
            sAF2 = nc.alloc_semaphore("p3AF")
            sQH = nc.alloc_semaphore("p3QH")
            sTH = nc.alloc_semaphore("p3TH")
            sHF = nc.alloc_semaphore("p3HF")
            sGU = nc.alloc_semaphore("p3GU")
            sSL = nc.alloc_semaphore("p3SL")
            sHH = nc.alloc_semaphore("p3HH")
            sQV = nc.alloc_semaphore("p3QV")
            sTH2 = nc.alloc_semaphore("p3TH2")
            sHQ = nc.alloc_semaphore("p3HQ")
            sHO = nc.alloc_semaphore("p3HO")

            with nc.Block() as blk:
                @blk.gpsimd
                def _(g_):
                    for t in range(32):
                        b = t // 8
                        p = t % 2
                        row0 = t * 128
                        xrow0 = b * TE + W + (t % 8) * 128
                        if t % 8 == 0:
                            for vi, (srcF, soff) in enumerate(
                                    ((gaF, 0), (cb1F, 0), (modF, 96))):
                                if b > 0 or vi > 0:
                                    g_.wait_ge(sRP, 6 * b + 2 * vi)
                                for ci in range(CCH):
                                    col = soff + ci * 4 + b
                                    g_.dma_start(
                                        vrow[0:1, ci * 128:(ci + 1) * 128],
                                        srcF[:, col:col + 1]).then_inc(sVR, 16)
                                g_.wait_ge(sVR, 16 * CCH * (3 * b + vi + 1))
                        if t >= 1:
                            g_.wait_ge(sRW, t)
                        if t >= 2:
                            g_.wait_ge(sX2, 2 * (t - 1))
                        for ci in range(CCH):
                            g_.dma_start(yt[p][:, ci * 128:(ci + 1) * 128],
                                         yscr[ci, :, row0:row0 + 128]
                                         ).then_inc(sLD3[p], 16)
                            g_.dma_start(srx[p][:, ci * 128:(ci + 1) * 128],
                                         srscr[ci * 128:(ci + 1) * 128,
                                               row0:row0 + 128]
                                         ).then_inc(sLD3[p], 16)
                        g_.dma_start(xrow[p][:], xa[xrow0:xrow0 + 128, :]
                                     ).then_inc(sLD3[p], 16)
                        g_.wait_ge(sLD3[p], 17 * 16 * (t // 2 + 1))

                @blk.tensor
                def _(te):
                    for t in range(32):
                        b = t // 8
                        if t % 8 == 0:
                            if t >= 8:
                                te.wait_ge(sHH, 6 * t)
                            for vi in range(3):
                                te.wait_ge(sVR, 16 * CCH * (3 * b + vi + 1))
                                for half in range(2):
                                    k2 = 2 * (3 * b + vi) + half
                                    if k2 >= 1:
                                        te.wait_ge(sRB, k2)
                                    te.matmul(ps4[:], ones1[0:1, :],
                                              vrow[0:1, half * 512:(half + 1) * 512],
                                              start=True, stop=True
                                              ).then_inc(sRP, 1)
                        te.wait_ge(sRW, t + 1)
                        if t >= 1:
                            te.wait_ge(sX2, 2 * t)
                            te.wait_ge(sSL, 6 * t - 1)
                        for ch in range(2):
                            for ci in range(CCH):
                                inst = te.matmul(
                                    ps[ch][:],
                                    rwkv[:, ci * 128:(ci + 1) * 128],
                                    wob[:, ci * C + ch * 512:ci * C + ch * 512 + 512],
                                    start=(ci == 0), stop=(ci == CCH - 1))
                            inst.then_inc(sWO, 1)
                        te.wait_ge(sQH, t + 1)
                        if t >= 2:
                            te.wait_ge(sHQ, 3 * (t - 1))
                        for g in range(2):
                            for k in range(4):
                                ci = g * 4 + k
                                inst = te.transpose(
                                    psb[g][:, k * 128:(k + 1) * 128],
                                    hqt[:, ci * 128:(ci + 1) * 128], idtb[:])
                        inst.then_inc(sTH, 1)
                        te.wait_ge(sHF, t + 1)
                        te.wait_ge(sX2, 2 * t + 2)
                        for ic in range(6):
                            n = 512 if ic < 5 else 256
                            pg = ps[(ic % 2) * 2]
                            pu = (ps[1], ps[3], ps4)[ic % 3]
                            if 6 * t + ic - 1 >= 1:
                                te.wait_ge(sSL, 6 * t + ic - 1)
                            if 6 * t + ic - 2 >= 1:
                                te.wait_ge(sHH, 6 * t + ic - 2)
                            if t % 8 == 0 and t >= 0 and ic == 2:
                                te.wait_ge(sRB, 6 * (t // 8 + 1))
                            for ci in range(CCH):
                                lhs = hqF[:, ci * 128:(ci + 1) * 128]
                                te.matmul(pg[:, 0:n], lhs,
                                          wgqT[:, ci * DI + ic * 512:ci * DI + ic * 512 + n],
                                          start=(ci == 0), stop=(ci == CCH - 1))
                                inst = te.matmul(
                                    pu[:, 0:n], lhs,
                                    wuqT[:, ci * DI + ic * 512:ci * DI + ic * 512 + n],
                                    start=(ci == 0), stop=(ci == CCH - 1))
                            inst.then_inc(sGU, 1)
                        if t >= 1:
                            te.wait_ge(sQV, t)
                            te.wait_ge(sHF, t + 1)
                            for gi, g0 in enumerate(range(0, ICH, 8)):
                                gn = min(8, ICH - g0)
                                if gi >= 1:
                                    te.wait_ge(sHQ, 3 * (t - 1) + gi)
                                for k in range(gn):
                                    inst = te.transpose(
                                        psb[k // 4][:, (k % 4) * 128:
                                                    (k % 4 + 1) * 128],
                                        hhqt[:, (g0 + k) * 128:(g0 + k + 1) * 128],
                                        idtb[:])
                                inst.then_inc(sTH2, 1)
                    te.wait_ge(sQV, 32)
                    te.wait_ge(sHQ, 3 * 31)
                    for gi, g0 in enumerate(range(0, ICH, 8)):
                        gn = min(8, ICH - g0)
                        if gi >= 1:
                            te.wait_ge(sHQ, 3 * 31 + gi)
                        for k in range(gn):
                            inst = te.transpose(
                                psb[k // 4][:, (k % 4) * 128:(k % 4 + 1) * 128],
                                hhqt[:, (g0 + k) * 128:(g0 + k + 1) * 128],
                                idtb[:])
                        inst.then_inc(sTH2, 1)

                @blk.vector
                def _(ve):
                    for t in range(32):
                        b = t // 8
                        p = t % 2
                        if t % 8 == 0:
                            for vi, bc in enumerate((ga_bc, cb1_bc, sb_bc)):
                                for half in range(2):
                                    ve.wait_ge(sRP, 2 * (3 * b + vi) + half + 1)
                                    ve.tensor_copy(bc[:, half * 512:(half + 1) * 512],
                                                   ps4[:]).then_inc(sRB, 1)
                        ve.wait_ge(sLD3[p], 17 * 16 * (t // 2 + 1))
                        if t >= 1:
                            ve.wait_ge(sWO, 2 * t)
                        ve.tensor_tensor(rwkv, srx[p], yt[p], op=ALU.mult
                                         ).then_inc(sRW, 1)
                        for ch in range(2):
                            ve.wait_ge(sWO, 2 * t + ch + 1)
                            if t >= 1 and ch == 0:
                                ve.wait_ge(sXO, 16 * t)
                            drs(ve)
                            ve.tensor_tensor(atmp[:], ps[ch][:],
                                             ga_bc[:, ch * 512:(ch + 1) * 512],
                                             op=ALU.mult)
                            drs(ve)
                            ve.tensor_tensor(x2t[:, ch * 512:(ch + 1) * 512],
                                             atmp[:],
                                             xrow[p][:, ch * 512:(ch + 1) * 512],
                                             op=ALU.add).then_inc(sX2, 1)
                        drs(ve)
                        ve.bn_stats(st6[:, 0:6], x2t[:, 0:512])
                        ve.bn_stats(st6[:, 6:12], x2t[:, 512:1024])
                        dr(ve)
                        ve.bn_aggr(mv[:], st6[:]).then_inc(sBN2, 1)
                        ve.wait_ge(sSQ2, t + 1)
                        ve.reciprocal(sinv[:], sd[:])
                        dr(ve)
                        ve.scalar_tensor_tensor(nbv[:], mv[:, 0:1], -1.0, sinv[:],
                                                op0=ALU.mult, op1=ALU.mult
                                                ).then_inc(sNB2, 1)
                        ve.wait_ge(sAF2, t + 1)
                        ve.tensor_tensor(h2t, h2t, cb1_bc[:], op=ALU.mult)
                        drs(ve)
                        ve.tensor_tensor(h2t, h2t, sb_bc[:], op=ALU.add
                                         ).then_inc(sAF3, 1)
                        ve.wait_ge(sABq, t + 1)
                        if t >= 1:
                            ve.wait_ge(sABh, t)
                        srcs = [(h2t, 0, 64)]
                        if t >= 1:
                            srcs += [(hh[:, 0:HV], 64, HV // 16),
                                     (hh[:, HV:DI], 64 + HV // 16,
                                      (DI - HV) // 16)]
                        emit_qd_head(ve, srcs, qd_m, qd_s, qd_r, qd_sh)
                        inst = emit_qd_tail_s(ve, h2t, hqt, hqt, qd_r[:, 0:64],
                                              qd_sh[:, 0:64], ta2, tacc2, tb2,
                                              C // 16)
                        inst.then_inc(sQH, 1)
                        def mul(ic):
                            n = 512 if ic < 5 else 256
                            pu = (ps[1], ps[3], ps4)[ic % 3]
                            ve.wait_ge(sSL, 6 * t + ic + 1)
                            ve.tensor_tensor(hh[:, ic * 512:ic * 512 + n],
                                             sg3[ic % 3][:, 0:n], pu[:, 0:n],
                                             op=ALU.mult).then_inc(sHH, 1)
                        if t >= 1:
                            emit_qd_tail_s(ve, hh[:, 0:HV], hhqt[:, 0:HV],
                                           hhqt[:, 0:HV],
                                           qd_r[:, 64:64 + HV // 16],
                                           qd_sh[:, 64:64 + HV // 16],
                                           ta2, tacc2, tb2, HV // 16)
                        drs(ve)
                        mul(0)
                        mul(1)
                        mul(2)
                        if t >= 1:
                            inst = emit_qd_tail_s(ve, hh[:, HV:DI],
                                                  hhqt[:, HV:DI], hhqt[:, HV:DI],
                                                  qd_r[:, 64 + HV // 16:240],
                                                  qd_sh[:, 64 + HV // 16:240],
                                                  ta2, tacc2, tb2,
                                                  (DI - HV) // 16)
                            inst.then_inc(sQV, 1)
                        drs(ve)
                        for ic in range(3, 6):
                            mul(ic)
                    ve.wait_ge(sABh, 32)
                    emit_qd_head(ve, [(hh[:, 0:HV], 64, HV // 16),
                                      (hh[:, HV:DI], 64 + HV // 16,
                                       (DI - HV) // 16)],
                                 qd_m, qd_s, qd_r, qd_sh)
                    emit_qd_tail_s(ve, hh[:, 0:HV], hhqt[:, 0:HV],
                                   hhqt[:, 0:HV],
                                   qd_r[:, 64:64 + HV // 16],
                                   qd_sh[:, 64:64 + HV // 16],
                                   ta2, tacc2, tb2, HV // 16)
                    drs(ve)
                    inst = emit_qd_tail_s(ve, hh[:, HV:DI], hhqt[:, HV:DI],
                                          hhqt[:, HV:DI],
                                          qd_r[:, 64 + HV // 16:240],
                                          qd_sh[:, 64 + HV // 16:240],
                                          ta2, tacc2, tb2, (DI - HV) // 16)
                    inst.then_inc(sQV, 1)

                @blk.scalar
                def _(sc):
                    for t in range(32):
                        sc.wait_ge(sBN2, t + 1)
                        sc.activation(sd[:], mv[:, 1:2], AF.Sqrt,
                                      bias=epsv[:, 0:1]).then_inc(sSQ2, 1)
                        sc.wait_ge(sNB2, t + 1)
                        if t >= 1:
                            sc.wait_ge(sQH, t)
                        sc.activation(h2t, x2t, AF.Identity,
                                      bias=nbv[:, 0:1], scale=sinv[:, 0:1]
                                      ).then_inc(sAF2, 1)
                        if t >= 1:
                            sc.wait_ge(sHH, 6 * t)
                            sc.wait_ge(sTH2, 3 * (t - 1))
                            sc.activation(hhqt[:], hh[:], AF.Sign)
                            drs(sc)
                            sc.activation(hh[:], hh[:], AF.Abs).then_inc(sABh, 1)
                        sc.wait_ge(sAF3, t + 1)
                        sc.wait_ge(sTH, t)
                        sc.activation(hqt[:], h2t[:], AF.Sign)
                        drs(sc)
                        sc.activation(h2t[:], h2t[:], AF.Abs).then_inc(sABq, 1)
                        sc.wait_ge(sTH, t + 1)
                        if t >= 1:
                            sc.wait_ge(sGU, 6 * t)
                        sc.activation(hqF[:, 0:512], psb[0][:], AF.Copy)
                        inst = sc.activation(hqF[:, 512:1024], psb[1][:], AF.Copy)
                        inst.then_inc(sHF, 1)
                        for ic in range(6):
                            n = 512 if ic < 5 else 256
                            pg = ps[(ic % 2) * 2]
                            sc.wait_ge(sGU, 6 * t + ic + 1)
                            if 6 * t + ic - 2 >= 1:
                                sc.wait_ge(sHH, 6 * t + ic - 2)
                            sc.activation(sg3[ic % 3][:, 0:n], pg[:, 0:n],
                                          AF.Silu).then_inc(sSL, 1)
                        if t >= 1:
                            for gi, g0 in enumerate(range(0, ICH, 8)):
                                gn = min(8, ICH - g0)
                                sc.wait_ge(sTH2, 3 * (t - 1) + gi + 1)
                                if t >= 2 and gi == 0:
                                    sc.wait_ge(sHO, 16 * (t - 1))
                                for k in range(gn):
                                    inst = sc.activation(
                                        hhqF[(t - 1) % 2][:, (g0 + k) * 128:
                                                          (g0 + k + 1) * 128],
                                        psb[k // 4][:, (k % 4) * 128:
                                                    (k % 4 + 1) * 128], AF.Copy)
                                inst.then_inc(sHQ, 1)
                    sc.wait_ge(sHH, 6 * 32)
                    sc.wait_ge(sTH2, 3 * 31)
                    sc.activation(hhqt[:], hh[:], AF.Sign)
                    drs(sc)
                    sc.activation(hh[:], hh[:], AF.Abs).then_inc(sABh, 1)
                    for gi, g0 in enumerate(range(0, ICH, 8)):
                        gn = min(8, ICH - g0)
                        sc.wait_ge(sTH2, 3 * 31 + gi + 1)
                        if gi == 0:
                            sc.wait_ge(sHO, 16 * 31)
                        for k in range(gn):
                            inst = sc.activation(
                                hhqF[31 % 2][:, (g0 + k) * 128:(g0 + k + 1) * 128],
                                psb[k // 4][:, (k % 4) * 128:(k % 4 + 1) * 128],
                                AF.Copy)
                        inst.then_inc(sHQ, 1)

                @blk.sync
                def _(sync):
                    for t in range(32):
                        p = t % 2
                        row0 = t * 128
                        sync.wait_ge(sX2, 2 * t + 2)
                        if t >= 1:
                            sync.wait_ge(sXO, 16 * t)
                        sync.dma_start(x2scr[row0:row0 + 128, :], x2t
                                       ).then_inc(sXO, 16)
                        sync.wait_ge(sHQ, 3 * (t + 1))
                        if t >= 1:
                            sync.wait_ge(sHO, 16 * t)
                        sync.dma_start(hqscr[t, :, :], hhqF[p]).then_inc(sHO, 16)
                    sync.wait_ge(sXO, 16 * 32)
                    sync.wait_ge(sHO, 16 * 32)

    with ExitStack() as stk3:
        def sb3(name, shape, dt):
            return stk3.enter_context(nc.sbuf_tensor(name, shape, dt)).ap()
        wdqT = sb3("wdqT", [128, ICH * C], BF16)
        ps4b = ps4
        with ExitStack() as stk4:
            def sb4(name, shape, dt):
                return stk4.enter_context(nc.sbuf_tensor(name, shape, dt)).ap()
            wscr3 = [sb4(f"wscr3_{i}", [128, DI], F32) for i in range(2)]
            axs3 = [sb4(f"axs3_{i}", [128, DI], F32) for i in range(2)]
            sgs3 = [sb4(f"sgs3_{i}", [128, DI], F32) for i in range(2)]
            ta3 = sb4("qta3", [128, DI], F32)
            tacc3 = sb4("qtacc3", [128, DI], F32)
            tb3 = sb4("qtb3", [128, DI], F32)
            qtile3 = [sb4(f"qtile3_{i}", [128, DI], BF16) for i in range(2)]
            qdw_m = sb4("qdw_m", [128, 2 * (DI // 16)], F32)
            qdw_s = sb4("qdw_s", [128, 2 * (DI // 16)], F32)
            qdw_r = sb4("qdw_r", [128, 2 * (DI // 16)], F32)
            qdw_sh = sb4("qdw_sh", [128, 2 * (DI // 16)], F32)
            sWL4 = [nc.alloc_semaphore(f"p4WL{i}") for i in range(2)]
            sAS4 = nc.alloc_semaphore("p4AS")
            sQ4 = nc.alloc_semaphore("p4Q4")
            sT4 = nc.alloc_semaphore("p4T4")
            sE4 = nc.alloc_semaphore("p4E4")

            with nc.Block() as blk:
                @blk.gpsimd
                def _(g_):
                    for cc in range(CCH):
                        p = cc % 2
                        if cc >= 2:
                            g_.wait_ge(sQ4, cc - 1)
                        g_.dma_start(wscr3[p], wd_d[cc * 128:(cc + 1) * 128, :]
                                     ).then_inc(sWL4[p], 16)
                        g_.wait_ge(sWL4[p], 16 * (cc // 2 + 1))

                @blk.scalar
                def _(sc):
                    for cc in range(CCH):
                        p = cc % 2
                        sc.wait_ge(sWL4[p], 16 * (cc // 2 + 1))
                        if cc >= 2:
                            sc.wait_ge(sQ4, cc - 1)
                        sc.activation(axs3[p][:], wscr3[p][:], AF.Abs)
                        sc.activation(sgs3[p][:], wscr3[p][:], AF.Sign
                                      ).then_inc(sAS4, 1)
                        for gi, g0 in enumerate(range(0, ICH, 8)):
                            gn = min(8, ICH - g0)
                            sc.wait_ge(sT4, 3 * cc + gi + 1)
                            for k in range(gn):
                                inst = sc.activation(
                                    wdqT[:, (g0 + k) * C + cc * 128:
                                         (g0 + k) * C + (cc + 1) * 128],
                                    psb[k // 4][:, (k % 4) * 128:
                                                (k % 4 + 1) * 128], AF.Copy)
                            inst.then_inc(sE4, 1)

                @blk.vector
                def _(ve):
                    NB3 = DI // 16
                    for c2 in range(CCH // 2):
                        cc0 = 2 * c2
                        ve.wait_ge(sWL4[0], 16 * (c2 + 1))
                        ve.wait_ge(sWL4[1], 16 * (c2 + 1))
                        if c2 >= 1:
                            ve.wait_ge(sT4, 3 * (cc0 - 1))
                        emit_qd_head(ve, [(wscr3[0], 0, NB3),
                                          (wscr3[1], NB3, NB3)],
                                     qdw_m, qdw_s, qdw_r, qdw_sh)
                        for k in range(2):
                            cc = cc0 + k
                            ve.wait_ge(sAS4, cc + 1)
                            if k == 1 and c2 >= 1:
                                ve.wait_ge(sT4, 3 * cc0)
                            inst = emit_qd_tail_s(
                                ve, axs3[k], sgs3[k], qtile3[k],
                                qdw_r[:, k * NB3:(k + 1) * NB3],
                                qdw_sh[:, k * NB3:(k + 1) * NB3],
                                ta3, tacc3, tb3, NB3)
                            inst.then_inc(sQ4, 1)

                @blk.tensor
                def _(te):
                    for cc in range(CCH):
                        p = cc % 2
                        te.wait_ge(sQ4, cc + 1)
                        for gi, g0 in enumerate(range(0, ICH, 8)):
                            gn = min(8, ICH - g0)
                            if 3 * cc + gi >= 1:
                                te.wait_ge(sE4, 3 * cc + gi)
                            for k in range(gn):
                                inst = te.transpose(
                                    psb[k // 4][:, (k % 4) * 128:(k % 4 + 1) * 128],
                                    qtile3[p][:, (g0 + k) * 128:(g0 + k + 1) * 128],
                                    idtb[:])
                            inst.then_inc(sT4, 1)

        with ExitStack() as stk5:
            def sb5(name, shape, dt):
                return stk5.enter_context(nc.sbuf_tensor(name, shape, dt)).ap()
            hhqF2 = [sb5(f"hhqF2_{i}", [128, ICH * 128], BF16) for i in range(2)]
            x2r = [sb5(f"x2r{i}", [128, C], F32) for i in range(2)]
            gb_bc = sb5("gb_bc", [128, C], F32)
            ot = [sb5(f"ot{i}", [128, C], F32) for i in range(2)]
            dtmp = sb5("dtmp", [128, 512], F32)
            sLD5 = [nc.alloc_semaphore(f"p5LD{i}") for i in range(2)]
            sVR5 = nc.alloc_semaphore("p5VR")
            sRP5 = nc.alloc_semaphore("p5RP")
            sRB5 = nc.alloc_semaphore("p5RB")
            sDM = nc.alloc_semaphore("p5DM")
            sDE = nc.alloc_semaphore("p5DE")
            soo = [nc.alloc_semaphore(f"p5OO{i}") for i in range(2)]

            with nc.Block() as blk:
                @blk.gpsimd
                def _(g_):
                    for t in range(32):
                        b = t // 8
                        p = t % 2
                        row0 = t * 128
                        if t % 8 == 0:
                            if b > 0:
                                g_.wait_ge(sRP5, 2 * b)
                            for ci in range(CCH):
                                col = ci * 4 + b
                                g_.dma_start(vrow[0:1, ci * 128:(ci + 1) * 128],
                                             gbF[:, col:col + 1]).then_inc(sVR5, 16)
                            g_.wait_ge(sVR5, 16 * CCH * (b + 1))
                        if t >= 2:
                            g_.wait_ge(sDE, 2 * (t - 1))
                        g_.dma_start(x2r[p][:], x2scr[row0:row0 + 128, :]
                                     ).then_inc(sLD5[p], 16)
                        g_.dma_start(hhqF2[p][:], hqscr[t, :, :]
                                     ).then_inc(sLD5[p], 16)
                        g_.wait_ge(sLD5[p], 32 * (t // 2 + 1))

                @blk.tensor
                def _(te):
                    for t in range(32):
                        b = t // 8
                        p = t % 2
                        if t % 8 == 0:
                            te.wait_ge(sVR5, 16 * CCH * (b + 1))
                            for half in range(2):
                                k2 = 2 * b + half
                                if k2 >= 1:
                                    te.wait_ge(sRB5, k2)
                                te.matmul(ps4b[:], ones1[0:1, :],
                                          vrow[0:1, half * 512:(half + 1) * 512],
                                          start=True, stop=True).then_inc(sRP5, 1)
                        te.wait_ge(sLD5[p], 32 * (t // 2 + 1))
                        for ch in range(2):
                            if 2 * t + ch >= 2:
                                te.wait_ge(sDE, 2 * t + ch - 1)
                            for ici in range(ICH):
                                inst = te.matmul(
                                    ps[ch % 2][:],
                                    hhqF2[p][:, ici * 128:(ici + 1) * 128],
                                    wdqT[:, ici * C + ch * 512:ici * C + ch * 512 + 512],
                                    start=(ici == 0), stop=(ici == ICH - 1))
                            inst.then_inc(sDM, 1)

                @blk.vector
                def _(ve):
                    for t in range(32):
                        b = t // 8
                        p = t % 2
                        if t % 8 == 0:
                            for half in range(2):
                                ve.wait_ge(sRP5, 2 * b + half + 1)
                                ve.tensor_copy(gb_bc[:, half * 512:(half + 1) * 512],
                                               ps4b[:]).then_inc(sRB5, 1)
                        for ch in range(2):
                            ve.wait_ge(sDM, 2 * t + ch + 1)
                            if t >= 2 and ch == 0:
                                ve.wait_ge(soo[t % 2], 16 * (t // 2))
                            drs(ve)
                            ve.tensor_tensor(dtmp[:], ps[ch % 2][:],
                                             gb_bc[:, ch * 512:(ch + 1) * 512],
                                             op=ALU.mult)
                            drs(ve)
                            ve.tensor_tensor(ot[p][:, ch * 512:(ch + 1) * 512],
                                             dtmp[:],
                                             x2r[p][:, ch * 512:(ch + 1) * 512],
                                             op=ALU.add).then_inc(sDE, 1)

                @blk.sync
                def _(sync):
                    for t in range(32):
                        p = t % 2
                        row0 = t * 128
                        sync.wait_ge(sDE, 2 * t + 2)
                        if t >= 2:
                            sync.wait_ge(soo[p], 16 * (t // 2))
                        sync.dma_start(outo[row0:row0 + 128, :], ot[p]
                                       ).then_inc(soo[p], 16)
                    sync.wait_ge(soo[0], 16 * 16)
                    sync.wait_ge(soo[1], 16 * 16)
    return nc



def _fm32(vec):
    return np.ascontiguousarray(vec.reshape(CCH, 128).T)


def kernel(L0_attn_x, L0_attn_time_decay, L0_attn_time_first, L0_attn_time_mix_k,
           L0_attn_time_mix_v, L0_attn_time_mix_r, L0_attn_Wk, L0_attn_Wv,
           L0_attn_Wr, L0_attn_Wo, L0_ffn_w_gate, L0_ffn_w_up, L0_ffn_w_down,
           L0_c, L0_wmod):
    x = np.ascontiguousarray(np.asarray(L0_attn_x, np.float32))
    td = np.asarray(L0_attn_time_decay, np.float32).reshape(C)
    tf = np.asarray(L0_attn_time_first, np.float32).reshape(C)
    tmk = np.asarray(L0_attn_time_mix_k, np.float32).reshape(C)
    tmv = np.asarray(L0_attn_time_mix_v, np.float32).reshape(C)
    tmr = np.asarray(L0_attn_time_mix_r, np.float32).reshape(C)
    Wk = np.asarray(L0_attn_Wk, np.float32)
    Wv = np.asarray(L0_attn_Wv, np.float32)
    Wr = np.asarray(L0_attn_Wr, np.float32)
    Wo = np.asarray(L0_attn_Wo, np.float32)
    wg = np.ascontiguousarray(np.asarray(L0_ffn_w_gate, np.float32))
    wu = np.ascontiguousarray(np.asarray(L0_ffn_w_up, np.float32))
    wd = np.ascontiguousarray(np.asarray(L0_ffn_w_down, np.float32))
    cv = np.asarray(L0_c, np.float32)
    wmod = np.ascontiguousarray(np.asarray(L0_wmod, np.float32))

    wkT = np.ascontiguousarray(Wk.T)
    wvT = np.ascontiguousarray(Wv.T)
    wrT = np.ascontiguousarray(Wr.T)
    woT = np.ascontiguousarray(Wo.T)
    tmH = np.ascontiguousarray(
        np.stack([_fm32(tmk), _fm32(tmv), _fm32(tmr)], axis=1).reshape(128, 3 * CCH))
    tdH = _fm32(td)
    tfH = _fm32(tf)
    cvF = np.ascontiguousarray(
        cv.reshape(B, CCH, 128).transpose(2, 1, 0).reshape(128, 4 * CCH))

    nc = build_fused()
    in_maps = []
    for j in range(NCORE):
        t0 = j * TK
        xaj = np.zeros((B, TE, C), np.float32)
        lo = max(0, t0 - W)
        xaj[:, W - (t0 - lo):W, :] = x[:, lo:t0, :]
        xaj[:, W:, :] = x[:, t0:t0 + TK, :]
        xaj = np.ascontiguousarray(xaj.reshape(B * TE, C))
        if j > 0:
            xbj = np.ascontiguousarray(x[:, t0 - W - 1, :])
            bm = np.ones((128, 1), np.float32)
        else:
            xbj = np.zeros((B, C), np.float32)
            bm = np.zeros((128, 1), np.float32)
        in_maps.append({"xa": xaj, "xb": xbj, "bmask": bm, "wkT": wkT, "wvT": wvT,
                        "wrT": wrT, "tmH": tmH, "tdH": tdH, "tfH": tfH,
                        "cvF": cvF, "wmod": wmod, "woT": woT,
                        "wg": wg, "wu": wu, "wd": wd})
    res = run_bass_kernel_spmd(nc, in_maps, core_ids=list(range(NCORE)))

    r0 = np.asarray(res.results[0]["outo"])
    out = np.empty((B, T, C), r0.dtype)
    for j in range(NCORE):
        out[:, j * TK:(j + 1) * TK, :] = res.results[j]["outo"].reshape(B, TK, C)
    return out


# version 30
# speedup vs baseline: 1.0132x; 1.0132x over previous
import os
import numpy as np
from contextlib import ExitStack
import concourse.bass as bass
import concourse.mybir as mybir
from concourse.bass_utils import run_bass_kernel_spmd

F32 = mybir.dt.float32
BF16 = mybir.dt.bfloat16
AF = mybir.ActivationFunctionType
ALU = mybir.AluOpType
AX = mybir.AxisListType

NCORE = 8
B = 4
T = 8192
C = 1024
DI = 2816
TK = T // NCORE
W = 512
TE = W + TK
ROWS = B * TK
CCH = C // 128
ICH = DI // 128
NEG_BIG = -1e30
EPS = 1e-5

SIM_DRAINS = os.environ.get("BASS_SIM_FP64") == "1"


def dr(eng):
    eng.drain()


def drh(eng):
    eng.drain()


def drs(eng):
    if SIM_DRAINS:
        eng.drain()


class Ctr:
    def __init__(self, sem):
        self.sem = sem
        self.v = 0

    def start(self, eng, dst, src):
        eng.dma_start(dst, src).then_inc(self.sem, 16)
        self.v += 16

    def wait(self, eng):
        eng.wait_ge(self.sem, self.v)


def make_identity(nc, ident):
    nc.gpsimd.memset(ident, 0.0)
    nc.gpsimd.drain()
    nc.gpsimd.affine_select(
        out=ident, in_=ident, compare_op=ALU.not_equal, fill=1.0,
        base=0, pattern=[[-1, ident.shape[0]]], channel_multiplier=1)


def emit_qd_head(ve, srcs, qm, qs, qr, qsh):
    drs(ve)
    for X, c0, nb in srcs:
        ve.tensor_reduce(qm[:, c0:c0 + nb],
                         X.rearrange("p (nb w) -> p nb w", w=16),
                         axis=AX.X, op=ALU.max, apply_absolute_value=True)
    a = srcs[0][1]
    z = srcs[-1][1] + srcs[-1][2]
    dr(ve)
    ve.reciprocal(qs[:, a:z], qm[:, a:z])
    dr(ve)
    ve.tensor_scalar(qr[:, a:z], qs[:, a:z], 6.0, 1e12,
                     op0=ALU.mult, op1=ALU.min)
    ve.tensor_scalar(qsh[:, a:z], qm[:, a:z], 1.0 / 12.0, 5e-13,
                     op0=ALU.mult, op1=ALU.max)
    drs(ve)


def emit_qd_tail(ve, X, Q, qr, qsh, ta, tacc, tb, nb):
    w = nb * 16
    X3 = X.rearrange("p (nb w) -> p nb w", w=16)
    r_bc = qr.unsqueeze(2).broadcast_to((128, nb, 16))
    sh_bc = qsh.unsqueeze(2).broadcast_to((128, nb, 16))
    av, accv, bv = ta[:, :w], tacc[:, :w], tb[:, :w]
    drs(ve)
    ve.tensor_tensor(av.rearrange("p (nb w) -> p nb w", w=16), X3, r_bc, op=ALU.mult)
    drs(ve)
    ve.tensor_scalar(bv, av, -1.0, None, op0=ALU.mult)
    drs(ve)
    ve.tensor_tensor(av, av, bv, op=ALU.max)
    drs(ve)
    ve.tensor_scalar(accv, av, 0.25, None, op0=ALU.is_gt)
    for tau in (0.75, 1.25, 1.75):
        drs(ve)
        ve.scalar_tensor_tensor(accv, av, tau, accv, op0=ALU.is_gt, op1=ALU.add)
    ve.tensor_scalar(bv, av, 2.5, None, op0=ALU.is_gt)
    drs(ve)
    ve.scalar_tensor_tensor(bv, av, 3.5, bv, op0=ALU.is_gt, op1=ALU.add)
    drs(ve)
    ve.scalar_tensor_tensor(accv, bv, 2.0, accv, op0=ALU.mult, op1=ALU.add)
    drs(ve)
    ve.tensor_scalar(bv, av, 5.0, None, op0=ALU.is_gt)
    drs(ve)
    ve.scalar_tensor_tensor(accv, bv, 4.0, accv, op0=ALU.mult, op1=ALU.add)
    drs(ve)
    ve.tensor_tensor(accv.rearrange("p (nb w) -> p nb w", w=16),
                     accv.rearrange("p (nb w) -> p nb w", w=16), sh_bc, op=ALU.mult)
    ve.tensor_scalar(bv, X, 0.0, None, op0=ALU.is_ge)
    drs(ve)
    ve.tensor_scalar(bv, bv, 2.0, -1.0, op0=ALU.mult, op1=ALU.add)
    drs(ve)
    return ve.tensor_tensor(Q, accv, bv, op=ALU.mult)


def emit_qd_tail_s(ve, AX_, SG, Q, qr, qsh, ta, tacc, tb, nb):
    w = nb * 16
    r_bc = qr.unsqueeze(2).broadcast_to((128, nb, 16))
    sh_bc = qsh.unsqueeze(2).broadcast_to((128, nb, 16))
    av, accv, bv = ta[:, :w], tacc[:, :w], tb[:, :w]
    drs(ve)
    ve.tensor_tensor(av.rearrange("p (nb w) -> p nb w", w=16),
                     AX_.rearrange("p (nb w) -> p nb w", w=16), r_bc,
                     op=ALU.mult)
    drs(ve)
    ve.tensor_scalar(accv, av, 0.25, None, op0=ALU.is_gt)
    for tau in (0.75, 1.25, 1.75):
        drs(ve)
        ve.scalar_tensor_tensor(accv, av, tau, accv, op0=ALU.is_gt, op1=ALU.add)
    ve.tensor_scalar(bv, av, 2.5, None, op0=ALU.is_gt)
    drs(ve)
    ve.scalar_tensor_tensor(bv, av, 3.5, bv, op0=ALU.is_gt, op1=ALU.add)
    drs(ve)
    ve.scalar_tensor_tensor(accv, bv, 2.0, accv, op0=ALU.mult, op1=ALU.add)
    drs(ve)
    ve.tensor_scalar(bv, av, 5.0, None, op0=ALU.is_gt)
    drs(ve)
    ve.scalar_tensor_tensor(accv, bv, 4.0, accv, op0=ALU.mult, op1=ALU.add)
    drs(ve)
    ve.tensor_tensor(accv.rearrange("p (nb w) -> p nb w", w=16),
                     accv.rearrange("p (nb w) -> p nb w", w=16), sh_bc,
                     op=ALU.mult)
    drs(ve)
    return ve.tensor_tensor(Q, accv, SG, op=ALU.mult)


def emit_qd(ve, X, Q, qm, qs, qr, qsh, ta, tacc, tb, nb):
    emit_qd_head(ve, [(X, 0, nb)], qm, qs, qr, qsh)
    return emit_qd_tail(ve, X, Q, qr[:, 0:nb], qsh[:, 0:nb], ta, tacc, tb, nb)


def build_fused():
    nc = bass.Bass("TRN2", target_bir_lowering=False)
    xa = nc.dram_tensor("xa", [B * TE, C], F32, kind="ExternalInput").ap()
    xb = nc.dram_tensor("xb", [B, C], F32, kind="ExternalInput").ap()
    bmask_d = nc.dram_tensor("bmask", [128, 1], F32, kind="ExternalInput").ap()
    wkT = nc.dram_tensor("wkT", [C, C], F32, kind="ExternalInput").ap()
    wvT = nc.dram_tensor("wvT", [C, C], F32, kind="ExternalInput").ap()
    wrT = nc.dram_tensor("wrT", [C, C], F32, kind="ExternalInput").ap()
    tmH = nc.dram_tensor("tmH", [128, 3 * CCH], F32, kind="ExternalInput").ap()
    tdH = nc.dram_tensor("tdH", [128, CCH], F32, kind="ExternalInput").ap()
    tfH = nc.dram_tensor("tfH", [128, CCH], F32, kind="ExternalInput").ap()
    cvF_d = nc.dram_tensor("cvF", [128, 4 * CCH], F32, kind="ExternalInput").ap()
    wmod_d = nc.dram_tensor("wmod", [C, 6 * C], F32, kind="ExternalInput").ap()
    woT = nc.dram_tensor("woT", [C, C], F32, kind="ExternalInput").ap()
    wg_d = nc.dram_tensor("wg", [DI, C], F32, kind="ExternalInput").ap()
    wu_d = nc.dram_tensor("wu", [DI, C], F32, kind="ExternalInput").ap()
    wd_d = nc.dram_tensor("wd", [C, DI], F32, kind="ExternalInput").ap()
    outo = nc.dram_tensor("outo", [ROWS, C], F32, kind="ExternalOutput").ap()
    kvscr = nc.dram_tensor("kvscr", [2, CCH, 128, B * TE], F32, kind="Internal").ap()
    srscr = nc.dram_tensor("srscr", [C, ROWS], F32, kind="Internal").ap()
    yscr = nc.dram_tensor("yscr", [CCH, 128, ROWS], F32, kind="Internal").ap()
    x2scr = nc.dram_tensor("x2scr", [ROWS, C], F32, kind="Internal").ap()
    hqscr = nc.dram_tensor("hqscr", [32, 128, ICH * 128], BF16, kind="Internal").ap()

    idt = nc.alloc_sbuf_tensor("idt", [128, 128], F32)
    idt4 = nc.alloc_sbuf_tensor("idt4", [B, B], F32)
    idtb = nc.alloc_sbuf_tensor("idtb", [128, 128], BF16)
    epsv = nc.alloc_sbuf_tensor("epsv", [128, 1], F32)
    ones1 = nc.alloc_sbuf_tensor("ones1", [1, 128], F32)
    bmask = nc.alloc_sbuf_tensor("bmaskT", [128, 1], F32)
    scf = nc.alloc_sbuf_tensor("scf", [128, 4 * CCH], F32)
    scf_bf = nc.alloc_sbuf_tensor("scf_bf", [128, 4 * CCH], BF16)
    modF = nc.alloc_sbuf_tensor("modF", [128, 6 * CCH * 4], F32)
    ca1F = nc.alloc_sbuf_tensor("ca1F", [128, CCH * 4], F32)
    cb1F = nc.alloc_sbuf_tensor("cb1F", [128, CCH * 4], F32)
    gaF = nc.alloc_sbuf_tensor("gaF", [128, CCH * 4], F32)
    gbF = nc.alloc_sbuf_tensor("gbF", [128, CCH * 4], F32)
    sgt = nc.alloc_sbuf_tensor("sgt", [128, CCH * 4], F32)
    tmF = nc.alloc_sbuf_tensor("tmF", [128, 3 * CCH], F32)
    tm1F = nc.alloc_sbuf_tensor("tm1F", [128, 3 * CCH], F32)
    wF = nc.alloc_sbuf_tensor("wF", [128, CCH], F32)
    uF = nc.alloc_sbuf_tensor("uF", [128, CCH], F32)
    vrow = nc.alloc_sbuf_tensor("vrow", [1, C], F32)
    st6 = nc.alloc_sbuf_tensor("st6", [128, 12], F32)
    mv = nc.alloc_sbuf_tensor("mv", [128, 2], F32)
    sd = nc.alloc_sbuf_tensor("sd", [128, 1], F32)
    sinv = nc.alloc_sbuf_tensor("sinv", [128, 1], F32)
    nbv = nc.alloc_sbuf_tensor("nbv", [128, 1], F32)
    qd_m = nc.alloc_sbuf_tensor("qd_m", [128, 240], F32)
    qd_s = nc.alloc_sbuf_tensor("qd_s", [128, 240], F32)
    qd_r = nc.alloc_sbuf_tensor("qd_r", [128, 240], F32)
    qd_sh = nc.alloc_sbuf_tensor("qd_sh", [128, 240], F32)

    ps = [nc.alloc_psum_tensor(f"ps{i}", [128, 512], F32) for i in range(4)]
    psb = [nc.alloc_psum_tensor(f"psb{i}", [128, 512], BF16) for i in range(2)]
    pmod = nc.alloc_psum_tensor("pmod", [128, 192], F32)

    dma = Ctr(nc.alloc_semaphore("dmas"))
    dmo = Ctr(nc.alloc_semaphore("dmao"))

    def ln_rowstats(xt, p=128):
        with nc.Block() as blk:
            @blk.vector
            def _(ve):
                ve.bn_stats(st6[0:p, 0:6], xt[:, 0:512])
                ve.bn_stats(st6[0:p, 6:12], xt[:, 512:1024])
                dr(ve)
                ve.bn_aggr(mv[0:p, :], st6[0:p, :])
        with nc.Block() as blk:
            @blk.scalar
            def _(sc):
                sc.activation(sd[0:p, :], mv[0:p, 1:2], AF.Sqrt, bias=epsv[0:p, 0:1])
        with nc.Block() as blk:
            @blk.vector
            def _(ve):
                ve.reciprocal(sinv[0:p, :], sd[0:p, :])
                dr(ve)
                ve.scalar_tensor_tensor(nbv[0:p, :], mv[0:p, 0:1], -1.0, sinv[0:p, :],
                                        op0=ALU.mult, op1=ALU.mult)

    with nc.Block() as blk:
        @blk.gpsimd
        def _(g):
            make_identity(nc, idt[:])
            make_identity(nc, idt4[:])
            make_identity(nc, idtb[:])
            g.memset(epsv[:], EPS)
            g.memset(ones1[:], 1.0)

        @blk.sync
        def _(sync):
            dma.start(sync, tmF[:], tmH)
            dma.start(sync, wF[:], tdH)
            dma.start(sync, uF[:], tfH)
            dma.start(sync, bmask[:], bmask_d)
            dma.start(sync, scf[:], cvF_d)
            dma.wait(sync)

    with nc.Block() as blk:
        @blk.vector
        def _(ve):
            ve.tensor_scalar(tm1F[:], tmF[:], -1.0, 1.0, op0=ALU.mult, op1=ALU.add)

        @blk.scalar
        def _(sc):
            sc.activation(wF[:], wF[:], AF.Exp)
            dr(sc)
            sc.activation(scf[:], scf[:], AF.Silu)

    with nc.Block() as blk:
        @blk.vector
        def _(ve):
            ve.tensor_scalar(wF[:], wF[:], -1.0, None, op0=ALU.mult)
            ve.tensor_copy(scf_bf[:], scf[:])

    with nc.sbuf_tensor("wmt", [128, 6 * C], F32) as wmt_h, \
         nc.sbuf_tensor("wmb", [128, 6 * C], BF16) as wmb_h:
        wmt, wmb = wmt_h.ap(), wmb_h.ap()
        for ci in range(CCH):
            with nc.Block() as blk:
                @blk.sync
                def _(sync, ci=ci):
                    dma.start(sync, wmt, wmod_d[ci * 128:(ci + 1) * 128, :])
                    dma.wait(sync)
            with nc.Block() as blk:
                @blk.vector
                def _(ve):
                    ve.tensor_copy(wmb, wmt)
            with nc.Block() as blk:
                @blk.tensor
                def _(te, ci=ci):
                    for oc in range(48):
                        te.matmul(pmod[:, oc * 4:(oc + 1) * 4],
                                  wmb[:, oc * 128:(oc + 1) * 128],
                                  scf_bf[:, ci * 4:(ci + 1) * 4],
                                  start=True, stop=True)
            with nc.Block() as blk:
                @blk.vector
                def _(ve, ci=ci):
                    if ci == 0:
                        ve.tensor_copy(modF[:], pmod[:])
                    else:
                        ve.tensor_tensor(modF[:], modF[:], pmod[:], op=ALU.add)

    with nc.Block() as blk:
        @blk.vector
        def _(ve):
            ve.tensor_scalar(ca1F[:], modF[:, 32:64], 1.0, None, op0=ALU.add)
            ve.tensor_scalar(cb1F[:], modF[:, 128:160], 1.0, None, op0=ALU.add)
            for (s0, s1), dstF in (((64, 96), gaF), ((160, 192), gbF)):
                dr(ve)
                ve.tensor_scalar(sgt[:], modF[:, s0:s1], 0.0, None, op0=ALU.is_ge)
                dr(ve)
                ve.tensor_scalar(sgt[:], sgt[:], 2.0, -1.0, op0=ALU.mult, op1=ALU.add)
                ve.tensor_scalar(dstF[:], modF[:, s0:s1], -1.0, None, op0=ALU.mult)
                dr(ve)
                ve.tensor_tensor(dstF[:], dstF[:], modF[:, s0:s1], op=ALU.max)
                dr(ve)
                ve.tensor_scalar(dstF[:], dstF[:], 0.25, None, op0=ALU.max)
                dr(ve)
                ve.tensor_tensor(dstF[:], dstF[:], sgt[:], op=ALU.mult)

    def pmm(ng):
        return ps[ng % 2][:]

    with ExitStack() as stk:
        def sb(name, shape, dt):
            return stk.enter_context(nc.sbuf_tensor(name, shape, dt)).ap()
        wkb = sb("wkb", [128, CCH * C], BF16)
        wvb = sb("wvb", [128, CCH * C], BF16)
        wrb = sb("wrb", [128, CCH * C], BF16)
        h1F = sb("h1F", [128, CCH * (TE + 1)], F32)
        xkb = sb("xkb", [128, CCH * TE], BF16)
        xvb = sb("xvb", [128, CCH * TE], BF16)
        xrb = sb("xrb", [128, CCH * TK], BF16)
        xt2 = [sb(f"xt{i}", [128, C], F32) for i in range(3)]
        t12 = xt2
        st62 = [sb(f"st6_{i}", [128, 12], F32) for i in range(2)]
        mv2 = [sb(f"mv_{i}", [128, 2], F32) for i in range(2)]
        sd2 = [sb(f"sd_{i}", [128, 1], F32) for i in range(2)]
        si2 = [sb(f"si_{i}", [128, 1], F32) for i in range(2)]
        nb2 = [sb(f"nb_{i}", [128, 1], F32) for i in range(2)]
        pre2 = [sb(f"pre_{i}", [128, TE], F32) for i in range(2)]
        kv2 = [sb(f"kv_{i}", [128, 512], F32) for i in range(2)]
        xbt = sb("xbt", [B, C], F32)
        t1b = xbt
        hbF = sb("hbF", [128, CCH * 4], F32)

        if True:
            wpair = [xt2[0], xt2[1]]
            sFL = [nc.alloc_semaphore(f"p1FL{i}") for i in range(2)]
            sFC = nc.alloc_semaphore("p1FC")
            fitems = [(wdram, wbuf, ci)
                      for wdram, wbuf in ((wkT, wkb), (wvT, wvb), (wrT, wrb))
                      for ci in range(CCH)]
            with nc.Block() as blk:
                @blk.sync
                def _(sync):
                    for i, (wdram, wbuf, ci) in enumerate(fitems):
                        if i % 2 == 1:
                            continue
                        if i >= 2:
                            sync.wait_ge(sFC, i - 1)
                        sync.dma_start(wpair[0],
                                       wdram[ci * 128:(ci + 1) * 128, :]
                                       ).then_inc(sFL[0], 16)

                @blk.gpsimd
                def _(g_):
                    for i, (wdram, wbuf, ci) in enumerate(fitems):
                        if i % 2 == 0:
                            continue
                        if i >= 2:
                            g_.wait_ge(sFC, i - 1)
                        g_.dma_start(wpair[1],
                                     wdram[ci * 128:(ci + 1) * 128, :]
                                     ).then_inc(sFL[1], 16)
                        g_.wait_ge(sFL[1], 16 * (i // 2 + 1))

                @blk.vector
                def _(ve):
                    for i, (wdram, wbuf, ci) in enumerate(fitems):
                        ve.wait_ge(sFL[i % 2], 16 * (i // 2 + 1))
                        ve.tensor_copy(wbuf[:, ci * C:(ci + 1) * C],
                                       wpair[i % 2]).then_inc(sFC, 1)

        with nc.Block() as blk:
            @blk.sync
            def _(sync):
                dma.start(sync, xbt, xb)
                dma.wait(sync)
        with nc.Block() as blk:
            @blk.vector
            def _(ve):
                ve.bn_stats(st62[0][0:B, 0:6], xbt[:, 0:512])
                ve.bn_stats(st62[0][0:B, 6:12], xbt[:, 512:1024])
                dr(ve)
                ve.bn_aggr(mv2[0][0:B, :], st62[0][0:B, :])
        with nc.Block() as blk:
            @blk.scalar
            def _(sc):
                sc.activation(sd2[0][0:B, :], mv2[0][0:B, 1:2], AF.Sqrt,
                              bias=epsv[0:B, 0:1])
        with nc.Block() as blk:
            @blk.vector
            def _(ve):
                ve.reciprocal(si2[0][0:B, :], sd2[0][0:B, :])
                dr(ve)
                ve.scalar_tensor_tensor(nb2[0][0:B, :], mv2[0][0:B, 0:1], -1.0,
                                        si2[0][0:B, :], op0=ALU.mult, op1=ALU.mult)
        with nc.Block() as blk:
            @blk.scalar
            def _(sc):
                sc.activation(t1b, xbt, AF.Identity,
                              bias=nb2[0][0:B, 0:1], scale=si2[0][0:B, 0:1])
        with nc.Block() as blk:
            @blk.tensor
            def _(te):
                for ci in range(CCH):
                    te.transpose(ps[0][:, ci * 4:(ci + 1) * 4],
                                 t1b[:, ci * 128:(ci + 1) * 128], idt4[:])
        with nc.Block() as blk:
            @blk.vector
            def _(ve):
                ve.tensor_copy(hbF[:], ps[0][:, 0:32])
        with nc.Block() as blk:
            @blk.vector
            def _(ve):
                for ci in range(CCH):
                    for b in range(B):
                        col = ci * 4 + b
                        ve.scalar_tensor_tensor(
                            hbF[:, col:col + 1], hbF[:, col:col + 1],
                            ca1F[:, col:col + 1], modF[:, col:col + 1],
                            op0=ALU.mult, op1=ALU.add)
                dr(ve)
                ve.tensor_scalar(hbF[:], hbF[:], bmask[:, 0:1], None, op0=ALU.mult)

        sX = nc.alloc_semaphore("p1X")
        sBN = nc.alloc_semaphore("p1BN")
        sSQ = nc.alloc_semaphore("p1SQ")
        sNB = nc.alloc_semaphore("p1NB")
        sT1 = nc.alloc_semaphore("p1T1")
        sTR = nc.alloc_semaphore("p1TR")
        sEV = nc.alloc_semaphore("p1EV")
        sMD = nc.alloc_semaphore("p1MD")
        sMK = nc.alloc_semaphore("p1MK")
        sPR = nc.alloc_semaphore("p1PR")
        sMX = nc.alloc_semaphore("p1MX")
        sMM = nc.alloc_semaphore("p1MM")
        sEC = nc.alloc_semaphore("p1EC")
        kos = [nc.alloc_semaphore(f"p1KO{i}") for i in range(2)]
        NR = TE // 128
        NMIX = 3 * CCH
        NG = 2 * 3 * CCH + 2 * CCH

        groups = []
        for pi, (wbuf, xmb, tw) in enumerate(((wkb, xkb, TE), (wvb, xvb, TE),
                                              (wrb, xrb, TK))):
            for d in range(CCH):
                for th in range(tw // 512):
                    groups.append((pi, wbuf, xmb, tw, d, th))
        assert len(groups) == NG

        with nc.Block() as blk:
            @blk.gpsimd
            def _(g_):
                for b in range(B):
                    for r in range(NR):
                        n = b * NR + r
                        if n >= 3:
                            g_.wait_ge(sTR, n - 2)
                        g_.dma_start(xt2[n % 3][:],
                                     xa[(b * TE + r * 128):(b * TE + r * 128 + 128), :]
                                     ).then_inc(sX, 16)
                        g_.wait_ge(sX, 16 * (n + 1))

            @blk.sync
            def _(sync):
                for b in range(B):
                    for g, (pi, wbuf, xmb, tw, d, th) in enumerate(groups):
                        ng = b * NG + g
                        sync.wait_ge(sEC, ng + 1)
                        if pi < 2:
                            off2 = b * TE + th * 512
                            sync.dma_start(kvscr[pi, d, :, off2:off2 + 512],
                                           kv2[ng % 2][:]).then_inc(kos[ng % 2], 16)
                        else:
                            off2 = b * TK + th * 512
                            sync.dma_start(srscr[d * 128:(d + 1) * 128,
                                                 off2:off2 + 512],
                                           kv2[ng % 2][:]).then_inc(kos[ng % 2], 16)
                sync.wait_ge(kos[0], 16 * (B * NG // 2))
                sync.wait_ge(kos[1], 16 * (B * NG // 2))

            @blk.vector
            def _(ve):
                def bn(b, r):
                    n = b * NR + r
                    p = n % 2
                    ve.wait_ge(sX, 16 * (n + 1))
                    ve.bn_stats(st62[p][:, 0:6], xt2[n % 3][:, 0:512])
                    ve.bn_stats(st62[p][:, 6:12], xt2[n % 3][:, 512:1024])
                    dr(ve)
                    ve.bn_aggr(mv2[p][:], st62[p][:]).then_inc(sBN, 1)

                def recip(b, r):
                    n = b * NR + r
                    p = n % 2
                    ve.wait_ge(sSQ, n + 1)
                    if n >= 2:
                        ve.wait_ge(sT1, n - 1)
                    ve.reciprocal(si2[p][:], sd2[p][:])
                    dr(ve)
                    ve.scalar_tensor_tensor(nb2[p][:], mv2[p][:, 0:1], -1.0,
                                            si2[p][:], op0=ALU.mult, op1=ALU.mult
                                            ).then_inc(sNB, 1)

                def evac(b, r):
                    n = b * NR + r
                    ve.wait_ge(sTR, n + 1)
                    h3 = h1F.rearrange("p (ci x) -> p ci x", ci=CCH)
                    for g in range(2):
                        pt = ps[2] if g == 0 else ps[3]
                        dst = h3[:, g * 4:(g + 1) * 4, 1 + r * 128:1 + (r + 1) * 128]
                        inst = ve.tensor_copy(dst,
                                              pt[:].rearrange("p (ci x) -> p ci x", ci=4))
                    inst.then_inc(sEV, 1)

                for b in range(B):
                    if b > 0:
                        ve.wait_ge(sPR, NMIX * b)
                    bn(b, 0)
                    bn(b, 1)
                    for r in range(NR):
                        recip(b, r)
                        if r + 2 < NR:
                            bn(b, r + 2)
                        evac(b, r)
                    for ci in range(CCH):
                        ve.tensor_copy(h1F[:, ci * (TE + 1):ci * (TE + 1) + 1],
                                       hbF[:, ci * 4 + b:ci * 4 + b + 1])
                    ve.wait_ge(sMD, b + 1)
                    for ci in range(CCH):
                        inst = ve.tensor_scalar(
                            h1F[:, ci * (TE + 1) + W:ci * (TE + 1) + W + 1],
                            h1F[:, ci * (TE + 1) + W:ci * (TE + 1) + W + 1],
                            bmask[:, 0:1], None, op0=ALU.mult)
                    inst.then_inc(sMK, 1)
                    if b > 0:
                        ve.wait_ge(sMM, NG * b)
                    for mi, xmb, tw in ((0, xkb, TE), (1, xvb, TE), (2, xrb, TK)):
                        off = 0 if tw == TE else W
                        for ci in range(CCH):
                            i = b * NMIX + mi * CCH + ci
                            ve.wait_ge(sPR, i + 1)
                            ve.scalar_tensor_tensor(
                                xmb[:, ci * tw:ci * tw + tw],
                                h1F[:, ci * (TE + 1) + off + 1:ci * (TE + 1) + off + tw + 1],
                                tmF[:, mi * CCH + ci:mi * CCH + ci + 1],
                                pre2[i % 2][:, 0:tw], op0=ALU.mult, op1=ALU.add
                                ).then_inc(sMX, 1)

            @blk.scalar
            def _(sc):
                def sqt(b2, r):
                    n = b2 * NR + r
                    p = n % 2
                    sc.wait_ge(sBN, n + 1)
                    sc.activation(sd2[p][:], mv2[p][:, 1:2], AF.Sqrt,
                                  bias=epsv[:, 0:1]).then_inc(sSQ, 1)
                    sc.wait_ge(sNB, n + 1)
                    if n >= 3:
                        sc.wait_ge(sTR, n - 2)
                    sc.activation(t12[n % 3][:], xt2[n % 3][:], AF.Identity,
                                  bias=nb2[p][:, 0:1], scale=si2[p][:, 0:1]
                                  ).then_inc(sT1, 1)

                for b in range(B):
                    if b == 0:
                        for r in range(NR):
                            sqt(0, r)
                    sc.wait_ge(sEV, NR * (b + 1))
                    for ci in range(CCH):
                        col = ci * 4 + b
                        inst = sc.activation(
                            h1F[:, ci * (TE + 1) + 1:(ci + 1) * (TE + 1)],
                            h1F[:, ci * (TE + 1) + 1:(ci + 1) * (TE + 1)],
                            AF.Identity, bias=modF[:, col:col + 1],
                            scale=ca1F[:, col:col + 1])
                    inst.then_inc(sMD, 1)
                    sc.wait_ge(sMK, b + 1)
                    for mi, tw in ((0, TE), (1, TE), (2, TK)):
                        off = 0 if tw == TE else W
                        for ci in range(CCH):
                            i = b * NMIX + mi * CCH + ci
                            if i >= 2:
                                sc.wait_ge(sMX, i - 1)
                            sc.activation(
                                pre2[i % 2][:, 0:tw],
                                h1F[:, ci * (TE + 1) + off:ci * (TE + 1) + off + tw],
                                AF.Identity,
                                scale=tm1F[:, mi * CCH + ci:mi * CCH + ci + 1]
                                ).then_inc(sPR, 1)
                    for g, (pi, wbuf, xmb, tw, d, th) in enumerate(groups):
                        ng = b * NG + g
                        sc.wait_ge(sMM, ng + 1)
                        if ng >= 2:
                            sc.wait_ge(kos[ng % 2], 16 * (ng // 2))
                        if pi < 2:
                            sc.activation(kv2[ng % 2][:], pmm(ng), AF.Identity
                                          ).then_inc(sEC, 1)
                        else:
                            sc.activation(kv2[ng % 2][:], pmm(ng), AF.Sigmoid
                                          ).then_inc(sEC, 1)
                        if b + 1 < B and g >= 1 and (g - 1) % 5 == 0 \
                                and (g - 1) // 5 < NR:
                            sqt(b + 1, (g - 1) // 5)

            @blk.tensor
            def _(te):
                def trow(b2, r):
                    n = b2 * NR + r
                    te.wait_ge(sT1, n + 1)
                    if n >= 1:
                        te.wait_ge(sEV, n)
                    for ci in range(CCH):
                        pt = ps[2] if ci < 4 else ps[3]
                        inst = te.transpose(
                            pt[:, (ci % 4) * 128:(ci % 4 + 1) * 128],
                            t12[n % 3][:, ci * 128:(ci + 1) * 128], idt[:])
                    inst.then_inc(sTR, 1)

                for r in range(NR):
                    trow(0, r)
                for b in range(B):
                    for g, (pi, wbuf, xmb, tw, d, th) in enumerate(groups):
                        ng = b * NG + g
                        te.wait_ge(sMX, b * NMIX + 8 * (pi + 1))
                        if ng >= 2:
                            te.wait_ge(sEC, ng - 1)
                        pm = pmm(ng)
                        for ci in range(CCH):
                            inst = te.matmul(
                                pm,
                                wbuf[:, ci * C + d * 128:ci * C + (d + 1) * 128],
                                xmb[:, ci * tw + th * 512:ci * tw + th * 512 + 512],
                                start=(ci == 0), stop=(ci == CCH - 1))
                        inst.then_inc(sMM, 1)
                        if b + 1 < B and g >= 2 and (g - 2) % 5 == 0 \
                                and (g - 2) // 5 < NR:
                            trow(b + 1, (g - 2) // 5)
    with ExitStack() as stk:
        def sb(name, shape, dt):
            return stk.enter_context(nc.sbuf_tensor(name, shape, dt)).ap()
        kt = [sb(f"kt{i}", [128, TE], F32) for i in range(2)]
        vt = [sb(f"vt{i}", [128, TE], F32) for i in range(2)]
        ppb = [sb(f"ppb{i}", [128, TE + 1], F32) for i in range(2)]
        aab = [sb(f"aab{i}", [128, TE + 1], F32) for i in range(2)]
        bbb = [sb(f"bbb{i}", [128, TE + 1], F32) for i in range(2)]
        e1b = [sb(f"e1b{i}", [128, TE], F32) for i in range(2)]
        e2b = [sb(f"e2b{i}", [128, TE], F32) for i in range(2)]
        d1 = [sb(f"d1_{i}", [128, TE], F32) for i in range(2)]
        gar = [sb(f"gar{i}", [128, TK], F32) for i in range(2)]
        gv = [sb(f"gv{i}", [128, TK], F32) for i in range(2)]
        num = [sb(f"num{i}", [128, TK], F32) for i in range(2)]
        den = [sb(f"den{i}", [128, TK], F32) for i in range(2)]
        yv = [sb(f"yv{i}", [128, TK], F32) for i in range(2)]

        lds = [nc.alloc_semaphore(f"p2LD{i}") for i in range(2)]
        sMS = nc.alloc_semaphore("p2MS")
        sVA = nc.alloc_semaphore("p2VA")
        sSE = nc.alloc_semaphore("p2SE")
        sVN = nc.alloc_semaphore("p2VN")
        sSD = nc.alloc_semaphore("p2SD")
        sVY = nc.alloc_semaphore("p2VY")
        sYO = nc.alloc_semaphore("p2YO")
        NIT = B * CCH

        def it_bc(i):
            return i // CCH, i % CCH

        with nc.Block() as blk:
            @blk.sync
            def _(sync):
                for i in range(NIT):
                    b, ci = it_bc(i)
                    p = i % 2
                    if i >= 2:
                        sync.wait_ge(sVN, i - 1)
                        sync.wait_ge(lds[p], 32 * (i // 2))
                    off = b * TE
                    sync.dma_start(kt[p][:], kvscr[0, ci, :, off:off + TE]
                                   ).then_inc(lds[p], 16)
                    sync.dma_start(vt[p][:], kvscr[1, ci, :, off:off + TE]
                                   ).then_inc(lds[p], 16)
                    if i >= 1:
                        bo, cio = it_bc(i - 1)
                        sync.wait_ge(sVY, i)
                        if i >= 2:
                            sync.wait_ge(sYO, 16 * (i - 1))
                        sync.dma_start(yscr[cio, :, bo * TK:(bo + 1) * TK],
                                       yv[(i - 1) % 2][:]).then_inc(sYO, 16)
                bo, cio = it_bc(NIT - 1)
                sync.wait_ge(sVY, NIT)
                sync.wait_ge(sYO, 16 * (NIT - 1))
                sync.dma_start(yscr[cio, :, bo * TK:(bo + 1) * TK],
                               yv[(NIT - 1) % 2][:]).then_inc(sYO, 16)
                sync.wait_ge(sYO, 16 * NIT)

            @blk.gpsimd
            def _(g):
                for i in range(NIT):
                    p = i % 2
                    if i >= 2:
                        g.wait_ge(sVN, i - 1)
                    g.memset(ppb[p][:, 0:1], NEG_BIG)
                    g.memset(aab[p][:, 0:1], 0.0)
                    g.memset(bbb[p][:, 0:1], 0.0)
                    g.drain()
                    g.sem_inc(sMS, 1)

            @blk.vector
            def _(ve):
                def S1(i):
                    b, ci = it_bc(i)
                    p = i % 2
                    ve.wait_ge(lds[p], 32 * (i // 2 + 1))
                    ve.wait_ge(sMS, i + 1)
                    ve.tensor_tensor_scan(
                        ppb[p][:, 1:TE + 1],
                        wF[:, ci:ci + 1].broadcast_to((128, TE)), kt[p][:],
                        ppb[p][:, 0:1], op0=ALU.add, op1=ALU.max)
                    drh(ve)
                    ve.tensor_tensor(e2b[p][:], kt[p][:], ppb[p][:, 1:TE + 1],
                                     op=ALU.subtract)
                    ve.tensor_tensor(e1b[p][:], ppb[p][:, 0:TE], ppb[p][:, 1:TE + 1],
                                     op=ALU.subtract)
                    ve.scalar_tensor_tensor(gar[p][:], kt[p][:, W:TE],
                                            uF[:, ci:ci + 1], ppb[p][:, W:TE],
                                            op0=ALU.add, op1=ALU.subtract
                                            ).then_inc(sVA, 1)

                def S2(i):
                    p = i % 2
                    ve.wait_ge(sSE, i + 1)
                    ve.tensor_tensor(d1[p][:], e2b[p][:], vt[p][:], op=ALU.mult)
                    ve.tensor_tensor(gv[p][:], gar[p][:], vt[p][:, W:TE], op=ALU.mult)
                    drs(ve)
                    ve.tensor_scalar(d1[p][:, 0:W], d1[p][:, 0:W], bmask[:, 0:1],
                                     None, op0=ALU.mult)
                    ve.tensor_scalar(e2b[p][:, 0:W], e2b[p][:, 0:W], bmask[:, 0:1],
                                     None, op0=ALU.mult)
                    drh(ve)
                    ve.tensor_tensor_scan(aab[p][:, 1:TE + 1], e1b[p][:], d1[p][:],
                                          aab[p][:, 0:1], op0=ALU.mult, op1=ALU.add)
                    ve.tensor_tensor_scan(bbb[p][:, 1:TE + 1], e1b[p][:], e2b[p][:],
                                          bbb[p][:, 0:1], op0=ALU.mult, op1=ALU.add)
                    drh(ve)
                    ve.tensor_tensor(num[p][:], aab[p][:, W:TE], gv[p][:], op=ALU.add)
                    ve.tensor_tensor(den[p][:], bbb[p][:, W:TE], gar[p][:], op=ALU.add
                                     ).then_inc(sVN, 1)

                def S3(i):
                    p = i % 2
                    ve.wait_ge(sSD, i + 1)
                    if i >= 2:
                        ve.wait_ge(sYO, 16 * (i - 1))
                    ve.tensor_tensor(yv[p][:], num[p][:], den[p][:], op=ALU.mult
                                     ).then_inc(sVY, 1)

                S1(0)
                for i in range(NIT):
                    S2(i)
                    if i + 1 < NIT:
                        S1(i + 1)
                    S3(i)

            @blk.scalar
            def _(sc):
                for i in range(NIT):
                    b, ci = it_bc(i)
                    p = i % 2
                    sc.wait_ge(sVA, i + 1)
                    sc.activation(e2b[p][:], e2b[p][:], AF.Exp)
                    sc.activation(e1b[p][:], e1b[p][:], AF.Exp, bias=wF[:, ci:ci + 1])
                    sc.activation(gar[p][:], gar[p][:], AF.Exp).then_inc(sSE, 1)
                    sc.wait_ge(sVN, i + 1)
                    sc.activation(den[p][:], den[p][:], AF.Ln)
                    dr(sc)
                    sc.activation(den[p][:], den[p][:], AF.Exp, scale=-1.0
                                  ).then_inc(sSD, 1)

    with ExitStack() as stk0:
        def sbw(name, shape, dt):
            return stk0.enter_context(nc.sbuf_tensor(name, shape, dt)).ap()
        wgqT = sbw("wgqT", [128, CCH * DI], BF16)
        wuqT = sbw("wuqT", [128, CCH * DI], BF16)
        ps4 = nc.alloc_psum_tensor("ps4", [128, 512], F32)

        with ExitStack() as stk1:
            def sb1(name, shape, dt):
                return stk1.enter_context(nc.sbuf_tensor(name, shape, dt)).ap()
            wscr1 = [sb1(f"wscr1_{i}", [128, C], F32) for i in range(4)]
            axs1 = [sb1(f"axs1_{i}", [128, C], F32) for i in range(4)]
            sgs1 = [sb1(f"sgs1_{i}", [128, C], F32) for i in range(4)]
            ta1 = sb1("qta1", [128, C], F32)
            tacc1 = sb1("qtacc1", [128, C], F32)
            tb1 = sb1("qtb1", [128, C], F32)
            qtile1 = [sb1(f"qtile1_{i}", [128, C], BF16) for i in range(4)]
            qdl_m = sb1("qdl_m", [128, 256], F32)
            qdl_s = sb1("qdl_s", [128, 256], F32)
            qdl_r = sb1("qdl_r", [128, 256], F32)
            qdl_sh = sb1("qdl_sh", [128, 256], F32)
            sWL = [nc.alloc_semaphore(f"p3WL{i}") for i in range(2)]
            sAS = nc.alloc_semaphore("p3AS")
            sQ1 = nc.alloc_semaphore("p3Q1")
            sT3 = nc.alloc_semaphore("p3T3")
            sE3 = nc.alloc_semaphore("p3E3")
            wjobs = [(wg_d, wgqT, wch) for wch in range(ICH)] + \
                    [(wu_d, wuqT, wch) for wch in range(ICH)]
            NWJ = len(wjobs)

            with nc.Block() as blk:
                @blk.gpsimd
                def _(g_):
                    for i, (wdram, wdst, wch) in enumerate(wjobs):
                        if i >= 4:
                            g_.wait_ge(sQ1, i - 3)
                        g_.dma_start(wscr1[i % 4], wdram[wch * 128:(wch + 1) * 128, :]
                                     ).then_inc(sWL[i % 2], 16)
                        g_.wait_ge(sWL[i % 2], 16 * (i // 2 + 1))

                @blk.scalar
                def _(sc):
                    for i, (wdram, wdst, wch) in enumerate(wjobs):
                        sc.wait_ge(sWL[i % 2], 16 * (i // 2 + 1))
                        if i >= 4:
                            sc.wait_ge(sQ1, i - 3)
                        sc.activation(axs1[i % 4][:], wscr1[i % 4][:], AF.Abs)
                        sc.activation(sgs1[i % 4][:], wscr1[i % 4][:], AF.Sign
                                      ).then_inc(sAS, 1)
                        if i >= 1:
                            _, wdst_p, wch_p = wjobs[i - 1]
                            sc.wait_ge(sT3, i)
                            d3 = wdst_p.rearrange("p (ci x) -> p ci x", ci=CCH)
                            for g in range(2):
                                inst = sc.activation(
                                    d3[:, g * 4:(g + 1) * 4,
                                       wch_p * 128:(wch_p + 1) * 128],
                                    psb[g][:].rearrange("p (ci x) -> p ci x",
                                                        ci=4), AF.Copy)
                            inst.then_inc(sE3, 1)
                    _, wdst_p, wch_p = wjobs[NWJ - 1]
                    sc.wait_ge(sT3, NWJ)
                    d3 = wdst_p.rearrange("p (ci x) -> p ci x", ci=CCH)
                    for g in range(2):
                        inst = sc.activation(
                            d3[:, g * 4:(g + 1) * 4,
                               wch_p * 128:(wch_p + 1) * 128],
                            psb[g][:].rearrange("p (ci x) -> p ci x", ci=4),
                            AF.Copy)
                    inst.then_inc(sE3, 1)

                @blk.vector
                def _(ve):
                    for g4 in range(NWJ // 4):
                        j0 = 4 * g4
                        for h in range(2):
                            ve.wait_ge(sWL[h], 16 * (j0 // 2 + 2))
                        if g4 >= 1:
                            ve.wait_ge(sT3, j0)
                        emit_qd_head(ve, [(wscr1[k], 64 * k, 64)
                                          for k in range(4)],
                                     qdl_m, qdl_s, qdl_r, qdl_sh)
                        for k in range(4):
                            j = j0 + k
                            ve.wait_ge(sAS, j + 1)
                            inst = emit_qd_tail_s(
                                ve, axs1[k], sgs1[k], qtile1[k],
                                qdl_r[:, 64 * k:64 * k + 64],
                                qdl_sh[:, 64 * k:64 * k + 64],
                                ta1, tacc1, tb1, C // 16)
                            inst.then_inc(sQ1, 1)

                @blk.tensor
                def _(te):
                    for i in range(NWJ):
                        te.wait_ge(sQ1, i + 1)
                        if i >= 1:
                            te.wait_ge(sE3, i)
                        for g in range(2):
                            for k in range(4):
                                ci = g * 4 + k
                                inst = te.transpose(
                                    psb[g][:, k * 128:(k + 1) * 128],
                                    qtile1[i % 4][:, ci * 128:(ci + 1) * 128],
                                    idtb[:])
                        inst.then_inc(sT3, 1)

        with ExitStack() as stk2:
            def sb2(name, shape, dt):
                return stk2.enter_context(nc.sbuf_tensor(name, shape, dt)).ap()
            wob = sb2("wob", [128, CCH * C], BF16)
            ga_bc = sb2("ga_bc", [128, C], F32)
            cb1_bc = sb2("cb1_bc", [128, C], F32)
            sb_bc = sb2("sb_bc", [128, C], F32)
            yt0 = sb2("yt0", [128, CCH * 128], F32)
            srx0 = sb2("srx0", [128, CCH * 128], F32)
            yt = [yt0, yt0]
            srx = [srx0, srx0]
            xrow = [sb2(f"xrow{i}", [128, C], F32) for i in range(2)]
            rwkv = sb2("rwkv", [128, CCH * 128], BF16)
            x2t = sb2("x2t", [128, C], F32)
            h2t = sb2("h2t", [128, C], F32)
            hqt = sb2("hqt", [128, C], BF16)
            hqF = sb2("hqF", [128, CCH * 128], BF16)
            atmp = sb2("atmp", [128, 512], F32)
            sg3 = [sb2(f"sg512_{i}", [128, 512], F32) for i in range(3)]
            hh = sb2("hh", [128, DI], F32)
            hhqt = sb2("hhqt", [128, DI], BF16)
            hhqF0 = sb2("hhqF0", [128, ICH * 128], BF16)
            hhqF = [hhqF0, hhqF0]
            HV = 1536
            ta2 = sb2("qta2", [128, HV], F32)
            tacc2 = sb2("qtacc2", [128, HV], F32)
            tb2 = sb2("qtb2", [128, HV], F32)
            GQ = (DI - HV) // 2

            wscr2 = hh[:, 0:C]
            for ci in range(CCH):
                with nc.Block() as blk:
                    @blk.sync
                    def _(sync, ci=ci):
                        dma.start(sync, wscr2, woT[ci * 128:(ci + 1) * 128, :])
                        dma.wait(sync)
                with nc.Block() as blk:
                    @blk.vector
                    def _(ve, ci=ci):
                        ve.tensor_copy(wob[:, ci * C:(ci + 1) * C], wscr2)

            sLD3 = [nc.alloc_semaphore(f"p3LD{i}") for i in range(2)]
            sAF3 = nc.alloc_semaphore("p3AF3")
            sABh = nc.alloc_semaphore("p3ABh")
            sABq = nc.alloc_semaphore("p3ABq")
            sVR = nc.alloc_semaphore("p3VR")
            sRP = nc.alloc_semaphore("p3RP")
            sRB = nc.alloc_semaphore("p3RB")
            sRW = nc.alloc_semaphore("p3RW")
            sWO = nc.alloc_semaphore("p3WO")
            sX2 = nc.alloc_semaphore("p3X2")
            sXO = nc.alloc_semaphore("p3XO")
            sBN2 = nc.alloc_semaphore("p3BN")
            sSQ2 = nc.alloc_semaphore("p3SQ")
            sNB2 = nc.alloc_semaphore("p3NB")
            sAF2 = nc.alloc_semaphore("p3AF")
            sQH = nc.alloc_semaphore("p3QH")
            sTH = nc.alloc_semaphore("p3TH")
            sHF = nc.alloc_semaphore("p3HF")
            sGU = nc.alloc_semaphore("p3GU")
            sSL = nc.alloc_semaphore("p3SL")
            sHH = nc.alloc_semaphore("p3HH")
            sQV = nc.alloc_semaphore("p3QV")
            sTH2 = nc.alloc_semaphore("p3TH2")
            sHQ = nc.alloc_semaphore("p3HQ")
            sHO = nc.alloc_semaphore("p3HO")

            with nc.Block() as blk:
                @blk.gpsimd
                def _(g_):
                    for t in range(32):
                        b = t // 8
                        p = t % 2
                        row0 = t * 128
                        xrow0 = b * TE + W + (t % 8) * 128
                        if t % 8 == 0:
                            for vi, (srcF, soff) in enumerate(
                                    ((gaF, 0), (cb1F, 0), (modF, 96))):
                                if b > 0 or vi > 0:
                                    g_.wait_ge(sRP, 6 * b + 2 * vi)
                                for ci in range(CCH):
                                    col = soff + ci * 4 + b
                                    g_.dma_start(
                                        vrow[0:1, ci * 128:(ci + 1) * 128],
                                        srcF[:, col:col + 1]).then_inc(sVR, 16)
                                g_.wait_ge(sVR, 16 * CCH * (3 * b + vi + 1))
                        if t >= 1:
                            g_.wait_ge(sRW, t)
                        if t >= 2:
                            g_.wait_ge(sX2, 2 * (t - 1))
                        for ci in range(CCH):
                            g_.dma_start(yt[p][:, ci * 128:(ci + 1) * 128],
                                         yscr[ci, :, row0:row0 + 128]
                                         ).then_inc(sLD3[p], 16)
                            g_.dma_start(srx[p][:, ci * 128:(ci + 1) * 128],
                                         srscr[ci * 128:(ci + 1) * 128,
                                               row0:row0 + 128]
                                         ).then_inc(sLD3[p], 16)
                        g_.dma_start(xrow[p][:], xa[xrow0:xrow0 + 128, :]
                                     ).then_inc(sLD3[p], 16)
                        g_.wait_ge(sLD3[p], 17 * 16 * (t // 2 + 1))

                @blk.tensor
                def _(te):
                    for t in range(32):
                        b = t // 8
                        if t % 8 == 0:
                            if t >= 8:
                                te.wait_ge(sHH, 6 * t)
                            for vi in range(3):
                                te.wait_ge(sVR, 16 * CCH * (3 * b + vi + 1))
                                for half in range(2):
                                    k2 = 2 * (3 * b + vi) + half
                                    if k2 >= 1:
                                        te.wait_ge(sRB, k2)
                                    te.matmul(ps4[:], ones1[0:1, :],
                                              vrow[0:1, half * 512:(half + 1) * 512],
                                              start=True, stop=True
                                              ).then_inc(sRP, 1)
                        te.wait_ge(sRW, t + 1)
                        if t >= 1:
                            te.wait_ge(sX2, 2 * t)
                            te.wait_ge(sSL, 6 * t - 1)
                        for ch in range(2):
                            for ci in range(CCH):
                                inst = te.matmul(
                                    ps[ch][:],
                                    rwkv[:, ci * 128:(ci + 1) * 128],
                                    wob[:, ci * C + ch * 512:ci * C + ch * 512 + 512],
                                    start=(ci == 0), stop=(ci == CCH - 1))
                            inst.then_inc(sWO, 1)
                        te.wait_ge(sQH, t + 1)
                        if t >= 2:
                            te.wait_ge(sHQ, 3 * (t - 1))
                        for g in range(2):
                            for k in range(4):
                                ci = g * 4 + k
                                inst = te.transpose(
                                    psb[g][:, k * 128:(k + 1) * 128],
                                    hqt[:, ci * 128:(ci + 1) * 128], idtb[:])
                        inst.then_inc(sTH, 1)
                        te.wait_ge(sHF, t + 1)
                        te.wait_ge(sX2, 2 * t + 2)
                        for ic in range(6):
                            n = 512 if ic < 5 else 256
                            pg = ps[(ic % 2) * 2]
                            pu = (ps[1], ps[3], ps4)[ic % 3]
                            if 6 * t + ic - 1 >= 1:
                                te.wait_ge(sSL, 6 * t + ic - 1)
                            if 6 * t + ic - 2 >= 1:
                                te.wait_ge(sHH, 6 * t + ic - 2)
                            if t % 8 == 0 and t >= 0 and ic == 2:
                                te.wait_ge(sRB, 6 * (t // 8 + 1))
                            for ci in range(CCH):
                                lhs = hqF[:, ci * 128:(ci + 1) * 128]
                                te.matmul(pg[:, 0:n], lhs,
                                          wgqT[:, ci * DI + ic * 512:ci * DI + ic * 512 + n],
                                          start=(ci == 0), stop=(ci == CCH - 1))
                                inst = te.matmul(
                                    pu[:, 0:n], lhs,
                                    wuqT[:, ci * DI + ic * 512:ci * DI + ic * 512 + n],
                                    start=(ci == 0), stop=(ci == CCH - 1))
                            inst.then_inc(sGU, 1)
                        if t >= 1:
                            te.wait_ge(sQV, t)
                            te.wait_ge(sHF, t + 1)
                            for gi, g0 in enumerate(range(0, ICH, 8)):
                                gn = min(8, ICH - g0)
                                if gi >= 1:
                                    te.wait_ge(sHQ, 3 * (t - 1) + gi)
                                for k in range(gn):
                                    inst = te.transpose(
                                        psb[k // 4][:, (k % 4) * 128:
                                                    (k % 4 + 1) * 128],
                                        hhqt[:, (g0 + k) * 128:(g0 + k + 1) * 128],
                                        idtb[:])
                                inst.then_inc(sTH2, 1)
                    te.wait_ge(sQV, 32)
                    te.wait_ge(sHQ, 3 * 31)
                    for gi, g0 in enumerate(range(0, ICH, 8)):
                        gn = min(8, ICH - g0)
                        if gi >= 1:
                            te.wait_ge(sHQ, 3 * 31 + gi)
                        for k in range(gn):
                            inst = te.transpose(
                                psb[k // 4][:, (k % 4) * 128:(k % 4 + 1) * 128],
                                hhqt[:, (g0 + k) * 128:(g0 + k + 1) * 128],
                                idtb[:])
                        inst.then_inc(sTH2, 1)

                @blk.vector
                def _(ve):
                    for t in range(32):
                        b = t // 8
                        p = t % 2
                        if t % 8 == 0:
                            for vi, bc in enumerate((ga_bc, cb1_bc, sb_bc)):
                                for half in range(2):
                                    ve.wait_ge(sRP, 2 * (3 * b + vi) + half + 1)
                                    ve.tensor_copy(bc[:, half * 512:(half + 1) * 512],
                                                   ps4[:]).then_inc(sRB, 1)
                        ve.wait_ge(sLD3[p], 17 * 16 * (t // 2 + 1))
                        if t >= 1:
                            ve.wait_ge(sWO, 2 * t)
                        ve.tensor_tensor(rwkv, srx[p], yt[p], op=ALU.mult
                                         ).then_inc(sRW, 1)
                        for ch in range(2):
                            ve.wait_ge(sWO, 2 * t + ch + 1)
                            if t >= 1 and ch == 0:
                                ve.wait_ge(sXO, 16 * t)
                            drs(ve)
                            ve.tensor_tensor(atmp[:], ps[ch][:],
                                             ga_bc[:, ch * 512:(ch + 1) * 512],
                                             op=ALU.mult)
                            drs(ve)
                            ve.tensor_tensor(x2t[:, ch * 512:(ch + 1) * 512],
                                             atmp[:],
                                             xrow[p][:, ch * 512:(ch + 1) * 512],
                                             op=ALU.add).then_inc(sX2, 1)
                        drs(ve)
                        ve.bn_stats(st6[:, 0:6], x2t[:, 0:512])
                        ve.bn_stats(st6[:, 6:12], x2t[:, 512:1024])
                        dr(ve)
                        ve.bn_aggr(mv[:], st6[:]).then_inc(sBN2, 1)
                        ve.wait_ge(sSQ2, t + 1)
                        ve.reciprocal(sinv[:], sd[:])
                        dr(ve)
                        ve.scalar_tensor_tensor(nbv[:], mv[:, 0:1], -1.0, sinv[:],
                                                op0=ALU.mult, op1=ALU.mult
                                                ).then_inc(sNB2, 1)
                        ve.wait_ge(sAF2, t + 1)
                        ve.tensor_tensor(h2t, h2t, cb1_bc[:], op=ALU.mult)
                        drs(ve)
                        ve.tensor_tensor(h2t, h2t, sb_bc[:], op=ALU.add
                                         ).then_inc(sAF3, 1)
                        ve.wait_ge(sABq, t + 1)
                        if t >= 1:
                            ve.wait_ge(sABh, t)
                        srcs = [(h2t, 0, 64)]
                        if t >= 1:
                            srcs += [(hh[:, 0:HV], 64, HV // 16),
                                     (hh[:, HV:DI], 64 + HV // 16,
                                      (DI - HV) // 16)]
                        emit_qd_head(ve, srcs, qd_m, qd_s, qd_r, qd_sh)
                        inst = emit_qd_tail_s(ve, h2t, hqt, hqt, qd_r[:, 0:64],
                                              qd_sh[:, 0:64], ta2, tacc2, tb2,
                                              C // 16)
                        inst.then_inc(sQH, 1)
                        def mul(ic):
                            n = 512 if ic < 5 else 256
                            pu = (ps[1], ps[3], ps4)[ic % 3]
                            ve.wait_ge(sSL, 6 * t + ic + 1)
                            ve.tensor_tensor(hh[:, ic * 512:ic * 512 + n],
                                             sg3[ic % 3][:, 0:n], pu[:, 0:n],
                                             op=ALU.mult).then_inc(sHH, 1)
                        if t >= 1:
                            emit_qd_tail_s(ve, hh[:, 0:HV], hhqt[:, 0:HV],
                                           hhqt[:, 0:HV],
                                           qd_r[:, 64:64 + HV // 16],
                                           qd_sh[:, 64:64 + HV // 16],
                                           ta2, tacc2, tb2, HV // 16)
                        drs(ve)
                        mul(0)
                        mul(1)
                        mul(2)
                        if t >= 1:
                            inst = emit_qd_tail_s(ve, hh[:, HV:DI],
                                                  hhqt[:, HV:DI], hhqt[:, HV:DI],
                                                  qd_r[:, 64 + HV // 16:240],
                                                  qd_sh[:, 64 + HV // 16:240],
                                                  ta2, tacc2, tb2,
                                                  (DI - HV) // 16)
                            inst.then_inc(sQV, 1)
                        drs(ve)
                        for ic in range(3, 6):
                            mul(ic)
                    ve.wait_ge(sABh, 32)
                    emit_qd_head(ve, [(hh[:, 0:HV], 64, HV // 16),
                                      (hh[:, HV:DI], 64 + HV // 16,
                                       (DI - HV) // 16)],
                                 qd_m, qd_s, qd_r, qd_sh)
                    emit_qd_tail_s(ve, hh[:, 0:HV], hhqt[:, 0:HV],
                                   hhqt[:, 0:HV],
                                   qd_r[:, 64:64 + HV // 16],
                                   qd_sh[:, 64:64 + HV // 16],
                                   ta2, tacc2, tb2, HV // 16)
                    drs(ve)
                    inst = emit_qd_tail_s(ve, hh[:, HV:DI], hhqt[:, HV:DI],
                                          hhqt[:, HV:DI],
                                          qd_r[:, 64 + HV // 16:240],
                                          qd_sh[:, 64 + HV // 16:240],
                                          ta2, tacc2, tb2, (DI - HV) // 16)
                    inst.then_inc(sQV, 1)

                @blk.scalar
                def _(sc):
                    for t in range(32):
                        sc.wait_ge(sBN2, t + 1)
                        sc.activation(sd[:], mv[:, 1:2], AF.Sqrt,
                                      bias=epsv[:, 0:1]).then_inc(sSQ2, 1)
                        sc.wait_ge(sNB2, t + 1)
                        if t >= 1:
                            sc.wait_ge(sQH, t)
                        sc.activation(h2t, x2t, AF.Identity,
                                      bias=nbv[:, 0:1], scale=sinv[:, 0:1]
                                      ).then_inc(sAF2, 1)
                        if t >= 1:
                            sc.wait_ge(sHH, 6 * t)
                            sc.wait_ge(sTH2, 3 * (t - 1))
                            sc.activation(hhqt[:], hh[:], AF.Sign)
                            drs(sc)
                            sc.activation(hh[:], hh[:], AF.Abs).then_inc(sABh, 1)
                        sc.wait_ge(sAF3, t + 1)
                        sc.wait_ge(sTH, t)
                        sc.activation(hqt[:], h2t[:], AF.Sign)
                        drs(sc)
                        sc.activation(h2t[:], h2t[:], AF.Abs).then_inc(sABq, 1)
                        sc.wait_ge(sTH, t + 1)
                        if t >= 1:
                            sc.wait_ge(sGU, 6 * t)
                        sc.activation(hqF[:, 0:512], psb[0][:], AF.Copy)
                        inst = sc.activation(hqF[:, 512:1024], psb[1][:], AF.Copy)
                        inst.then_inc(sHF, 1)
                        for ic in range(6):
                            n = 512 if ic < 5 else 256
                            pg = ps[(ic % 2) * 2]
                            sc.wait_ge(sGU, 6 * t + ic + 1)
                            if 6 * t + ic - 2 >= 1:
                                sc.wait_ge(sHH, 6 * t + ic - 2)
                            sc.activation(sg3[ic % 3][:, 0:n], pg[:, 0:n],
                                          AF.Silu).then_inc(sSL, 1)
                        if t >= 1:
                            for gi, g0 in enumerate(range(0, ICH, 8)):
                                gn = min(8, ICH - g0)
                                sc.wait_ge(sTH2, 3 * (t - 1) + gi + 1)
                                if t >= 2 and gi == 0:
                                    sc.wait_ge(sHO, 16 * (t - 1))
                                for k in range(gn):
                                    inst = sc.activation(
                                        hhqF[(t - 1) % 2][:, (g0 + k) * 128:
                                                          (g0 + k + 1) * 128],
                                        psb[k // 4][:, (k % 4) * 128:
                                                    (k % 4 + 1) * 128], AF.Copy)
                                inst.then_inc(sHQ, 1)
                    sc.wait_ge(sHH, 6 * 32)
                    sc.wait_ge(sTH2, 3 * 31)
                    sc.activation(hhqt[:], hh[:], AF.Sign)
                    drs(sc)
                    sc.activation(hh[:], hh[:], AF.Abs).then_inc(sABh, 1)
                    for gi, g0 in enumerate(range(0, ICH, 8)):
                        gn = min(8, ICH - g0)
                        sc.wait_ge(sTH2, 3 * 31 + gi + 1)
                        if gi == 0:
                            sc.wait_ge(sHO, 16 * 31)
                        for k in range(gn):
                            inst = sc.activation(
                                hhqF[31 % 2][:, (g0 + k) * 128:(g0 + k + 1) * 128],
                                psb[k // 4][:, (k % 4) * 128:(k % 4 + 1) * 128],
                                AF.Copy)
                        inst.then_inc(sHQ, 1)

                @blk.sync
                def _(sync):
                    for t in range(32):
                        p = t % 2
                        row0 = t * 128
                        sync.wait_ge(sX2, 2 * t + 2)
                        if t >= 1:
                            sync.wait_ge(sXO, 16 * t)
                        sync.dma_start(x2scr[row0:row0 + 128, :], x2t
                                       ).then_inc(sXO, 16)
                        sync.wait_ge(sHQ, 3 * (t + 1))
                        if t >= 1:
                            sync.wait_ge(sHO, 16 * t)
                        sync.dma_start(hqscr[t, :, :], hhqF[p]).then_inc(sHO, 16)
                    sync.wait_ge(sXO, 16 * 32)
                    sync.wait_ge(sHO, 16 * 32)

    with ExitStack() as stk3:
        def sb3(name, shape, dt):
            return stk3.enter_context(nc.sbuf_tensor(name, shape, dt)).ap()
        wdqT = sb3("wdqT", [128, ICH * C], BF16)
        ps4b = ps4
        with ExitStack() as stk4:
            def sb4(name, shape, dt):
                return stk4.enter_context(nc.sbuf_tensor(name, shape, dt)).ap()
            wscr3 = [sb4(f"wscr3_{i}", [128, DI], F32) for i in range(2)]
            axs3 = [sb4(f"axs3_{i}", [128, DI], F32) for i in range(2)]
            sgs3 = [sb4(f"sgs3_{i}", [128, DI], F32) for i in range(2)]
            ta3 = sb4("qta3", [128, DI], F32)
            tacc3 = sb4("qtacc3", [128, DI], F32)
            tb3 = sb4("qtb3", [128, DI], F32)
            qtile3 = [sb4(f"qtile3_{i}", [128, DI], BF16) for i in range(2)]
            qdw_m = sb4("qdw_m", [128, 2 * (DI // 16)], F32)
            qdw_s = sb4("qdw_s", [128, 2 * (DI // 16)], F32)
            qdw_r = sb4("qdw_r", [128, 2 * (DI // 16)], F32)
            qdw_sh = sb4("qdw_sh", [128, 2 * (DI // 16)], F32)
            sWL4 = [nc.alloc_semaphore(f"p4WL{i}") for i in range(2)]
            sAS4 = nc.alloc_semaphore("p4AS")
            sQ4 = nc.alloc_semaphore("p4Q4")
            sT4 = nc.alloc_semaphore("p4T4")
            sE4 = nc.alloc_semaphore("p4E4")

            with nc.Block() as blk:
                @blk.gpsimd
                def _(g_):
                    for cc in range(CCH):
                        p = cc % 2
                        if cc >= 2:
                            g_.wait_ge(sQ4, cc - 1)
                        g_.dma_start(wscr3[p], wd_d[cc * 128:(cc + 1) * 128, :]
                                     ).then_inc(sWL4[p], 16)
                        g_.wait_ge(sWL4[p], 16 * (cc // 2 + 1))

                @blk.scalar
                def _(sc):
                    for cc in range(CCH):
                        p = cc % 2
                        sc.wait_ge(sWL4[p], 16 * (cc // 2 + 1))
                        if cc >= 2:
                            sc.wait_ge(sQ4, cc - 1)
                        sc.activation(axs3[p][:], wscr3[p][:], AF.Abs)
                        sc.activation(sgs3[p][:], wscr3[p][:], AF.Sign
                                      ).then_inc(sAS4, 1)
                        for gi, g0 in enumerate(range(0, ICH, 8)):
                            gn = min(8, ICH - g0)
                            sc.wait_ge(sT4, 3 * cc + gi + 1)
                            for k in range(gn):
                                inst = sc.activation(
                                    wdqT[:, (g0 + k) * C + cc * 128:
                                         (g0 + k) * C + (cc + 1) * 128],
                                    psb[k // 4][:, (k % 4) * 128:
                                                (k % 4 + 1) * 128], AF.Copy)
                            inst.then_inc(sE4, 1)

                @blk.vector
                def _(ve):
                    NB3 = DI // 16
                    for c2 in range(CCH // 2):
                        cc0 = 2 * c2
                        ve.wait_ge(sWL4[0], 16 * (c2 + 1))
                        ve.wait_ge(sWL4[1], 16 * (c2 + 1))
                        if c2 >= 1:
                            ve.wait_ge(sT4, 3 * (cc0 - 1))
                        emit_qd_head(ve, [(wscr3[0], 0, NB3),
                                          (wscr3[1], NB3, NB3)],
                                     qdw_m, qdw_s, qdw_r, qdw_sh)
                        for k in range(2):
                            cc = cc0 + k
                            ve.wait_ge(sAS4, cc + 1)
                            if k == 1 and c2 >= 1:
                                ve.wait_ge(sT4, 3 * cc0)
                            inst = emit_qd_tail_s(
                                ve, axs3[k], sgs3[k], qtile3[k],
                                qdw_r[:, k * NB3:(k + 1) * NB3],
                                qdw_sh[:, k * NB3:(k + 1) * NB3],
                                ta3, tacc3, tb3, NB3)
                            inst.then_inc(sQ4, 1)

                @blk.tensor
                def _(te):
                    for cc in range(CCH):
                        p = cc % 2
                        te.wait_ge(sQ4, cc + 1)
                        for gi, g0 in enumerate(range(0, ICH, 8)):
                            gn = min(8, ICH - g0)
                            if 3 * cc + gi >= 1:
                                te.wait_ge(sE4, 3 * cc + gi)
                            for k in range(gn):
                                inst = te.transpose(
                                    psb[k // 4][:, (k % 4) * 128:(k % 4 + 1) * 128],
                                    qtile3[p][:, (g0 + k) * 128:(g0 + k + 1) * 128],
                                    idtb[:])
                            inst.then_inc(sT4, 1)

        with ExitStack() as stk5:
            def sb5(name, shape, dt):
                return stk5.enter_context(nc.sbuf_tensor(name, shape, dt)).ap()
            hhqF2 = [sb5(f"hhqF2_{i}", [128, ICH * 128], BF16) for i in range(2)]
            x2r = [sb5(f"x2r{i}", [128, C], F32) for i in range(2)]
            gb_bc = sb5("gb_bc", [128, C], F32)
            ot = [sb5(f"ot{i}", [128, C], F32) for i in range(2)]
            dtmp = sb5("dtmp", [128, 512], F32)
            sLD5 = [nc.alloc_semaphore(f"p5LD{i}") for i in range(2)]
            sVR5 = nc.alloc_semaphore("p5VR")
            sRP5 = nc.alloc_semaphore("p5RP")
            sRB5 = nc.alloc_semaphore("p5RB")
            sDM = nc.alloc_semaphore("p5DM")
            sDE = nc.alloc_semaphore("p5DE")
            soo = [nc.alloc_semaphore(f"p5OO{i}") for i in range(2)]

            with nc.Block() as blk:
                @blk.gpsimd
                def _(g_):
                    for t in range(32):
                        b = t // 8
                        p = t % 2
                        row0 = t * 128
                        if t % 8 == 0:
                            if b > 0:
                                g_.wait_ge(sRP5, 2 * b)
                            for ci in range(CCH):
                                col = ci * 4 + b
                                g_.dma_start(vrow[0:1, ci * 128:(ci + 1) * 128],
                                             gbF[:, col:col + 1]).then_inc(sVR5, 16)
                            g_.wait_ge(sVR5, 16 * CCH * (b + 1))
                        if t >= 2:
                            g_.wait_ge(sDE, 2 * (t - 1))
                        g_.dma_start(x2r[p][:], x2scr[row0:row0 + 128, :]
                                     ).then_inc(sLD5[p], 16)
                        g_.dma_start(hhqF2[p][:], hqscr[t, :, :]
                                     ).then_inc(sLD5[p], 16)
                        g_.wait_ge(sLD5[p], 32 * (t // 2 + 1))

                @blk.tensor
                def _(te):
                    for t in range(32):
                        b = t // 8
                        p = t % 2
                        if t % 8 == 0:
                            te.wait_ge(sVR5, 16 * CCH * (b + 1))
                            for half in range(2):
                                k2 = 2 * b + half
                                if k2 >= 1:
                                    te.wait_ge(sRB5, k2)
                                te.matmul(ps4b[:], ones1[0:1, :],
                                          vrow[0:1, half * 512:(half + 1) * 512],
                                          start=True, stop=True).then_inc(sRP5, 1)
                        te.wait_ge(sLD5[p], 32 * (t // 2 + 1))
                        for ch in range(2):
                            if 2 * t + ch >= 2:
                                te.wait_ge(sDE, 2 * t + ch - 1)
                            for ici in range(ICH):
                                inst = te.matmul(
                                    ps[ch % 2][:],
                                    hhqF2[p][:, ici * 128:(ici + 1) * 128],
                                    wdqT[:, ici * C + ch * 512:ici * C + ch * 512 + 512],
                                    start=(ici == 0), stop=(ici == ICH - 1))
                            inst.then_inc(sDM, 1)

                @blk.vector
                def _(ve):
                    for t in range(32):
                        b = t // 8
                        p = t % 2
                        if t % 8 == 0:
                            for half in range(2):
                                ve.wait_ge(sRP5, 2 * b + half + 1)
                                ve.tensor_copy(gb_bc[:, half * 512:(half + 1) * 512],
                                               ps4b[:]).then_inc(sRB5, 1)
                        for ch in range(2):
                            ve.wait_ge(sDM, 2 * t + ch + 1)
                            if t >= 2 and ch == 0:
                                ve.wait_ge(soo[t % 2], 16 * (t // 2))
                            drs(ve)
                            ve.tensor_tensor(dtmp[:], ps[ch % 2][:],
                                             gb_bc[:, ch * 512:(ch + 1) * 512],
                                             op=ALU.mult)
                            drs(ve)
                            ve.tensor_tensor(ot[p][:, ch * 512:(ch + 1) * 512],
                                             dtmp[:],
                                             x2r[p][:, ch * 512:(ch + 1) * 512],
                                             op=ALU.add).then_inc(sDE, 1)

                @blk.sync
                def _(sync):
                    for t in range(32):
                        p = t % 2
                        row0 = t * 128
                        sync.wait_ge(sDE, 2 * t + 2)
                        if t >= 2:
                            sync.wait_ge(soo[p], 16 * (t // 2))
                        sync.dma_start(outo[row0:row0 + 128, :], ot[p]
                                       ).then_inc(soo[p], 16)
                    sync.wait_ge(soo[0], 16 * 16)
                    sync.wait_ge(soo[1], 16 * 16)
    return nc



def _fm32(vec):
    return np.ascontiguousarray(vec.reshape(CCH, 128).T)


def kernel(L0_attn_x, L0_attn_time_decay, L0_attn_time_first, L0_attn_time_mix_k,
           L0_attn_time_mix_v, L0_attn_time_mix_r, L0_attn_Wk, L0_attn_Wv,
           L0_attn_Wr, L0_attn_Wo, L0_ffn_w_gate, L0_ffn_w_up, L0_ffn_w_down,
           L0_c, L0_wmod):
    x = np.ascontiguousarray(np.asarray(L0_attn_x, np.float32))
    td = np.asarray(L0_attn_time_decay, np.float32).reshape(C)
    tf = np.asarray(L0_attn_time_first, np.float32).reshape(C)
    tmk = np.asarray(L0_attn_time_mix_k, np.float32).reshape(C)
    tmv = np.asarray(L0_attn_time_mix_v, np.float32).reshape(C)
    tmr = np.asarray(L0_attn_time_mix_r, np.float32).reshape(C)
    Wk = np.asarray(L0_attn_Wk, np.float32)
    Wv = np.asarray(L0_attn_Wv, np.float32)
    Wr = np.asarray(L0_attn_Wr, np.float32)
    Wo = np.asarray(L0_attn_Wo, np.float32)
    wg = np.ascontiguousarray(np.asarray(L0_ffn_w_gate, np.float32))
    wu = np.ascontiguousarray(np.asarray(L0_ffn_w_up, np.float32))
    wd = np.ascontiguousarray(np.asarray(L0_ffn_w_down, np.float32))
    cv = np.asarray(L0_c, np.float32)
    wmod = np.ascontiguousarray(np.asarray(L0_wmod, np.float32))

    wkT = np.ascontiguousarray(Wk.T)
    wvT = np.ascontiguousarray(Wv.T)
    wrT = np.ascontiguousarray(Wr.T)
    woT = np.ascontiguousarray(Wo.T)
    tmH = np.ascontiguousarray(
        np.stack([_fm32(tmk), _fm32(tmv), _fm32(tmr)], axis=1).reshape(128, 3 * CCH))
    tdH = _fm32(td)
    tfH = _fm32(tf)
    cvF = np.ascontiguousarray(
        cv.reshape(B, CCH, 128).transpose(2, 1, 0).reshape(128, 4 * CCH))

    nc = build_fused()
    in_maps = []
    for j in range(NCORE):
        t0 = j * TK
        xaj = np.zeros((B, TE, C), np.float32)
        lo = max(0, t0 - W)
        xaj[:, W - (t0 - lo):W, :] = x[:, lo:t0, :]
        xaj[:, W:, :] = x[:, t0:t0 + TK, :]
        xaj = np.ascontiguousarray(xaj.reshape(B * TE, C))
        if j > 0:
            xbj = np.ascontiguousarray(x[:, t0 - W - 1, :])
            bm = np.ones((128, 1), np.float32)
        else:
            xbj = np.zeros((B, C), np.float32)
            bm = np.zeros((128, 1), np.float32)
        in_maps.append({"xa": xaj, "xb": xbj, "bmask": bm, "wkT": wkT, "wvT": wvT,
                        "wrT": wrT, "tmH": tmH, "tdH": tdH, "tfH": tfH,
                        "cvF": cvF, "wmod": wmod, "woT": woT,
                        "wg": wg, "wu": wu, "wd": wd})
    res = run_bass_kernel_spmd(nc, in_maps, core_ids=list(range(NCORE)))

    r0 = np.asarray(res.results[0]["outo"])
    out = np.empty((B, T, C), r0.dtype)
    for j in range(NCORE):
        out[:, j * TK:(j + 1) * TK, :] = res.results[j]["outo"].reshape(B, TK, C)
    return out


# version 31 (final)
# speedup vs baseline: 1.0134x; 1.0002x over previous
import os
import numpy as np
from contextlib import ExitStack
import concourse.bass as bass
import concourse.mybir as mybir
from concourse.bass_utils import run_bass_kernel_spmd

F32 = mybir.dt.float32
BF16 = mybir.dt.bfloat16
AF = mybir.ActivationFunctionType
ALU = mybir.AluOpType
AX = mybir.AxisListType

NCORE = 8
B = 4
T = 8192
C = 1024
DI = 2816
TK = T // NCORE
W = 512
TE = W + TK
ROWS = B * TK
CCH = C // 128
ICH = DI // 128
NEG_BIG = -1e30
EPS = 1e-5

SIM_DRAINS = os.environ.get("BASS_SIM_FP64") == "1"


def dr(eng):
    eng.drain()


def drh(eng):
    eng.drain()


def drs(eng):
    if SIM_DRAINS:
        eng.drain()


class Ctr:
    def __init__(self, sem):
        self.sem = sem
        self.v = 0

    def start(self, eng, dst, src):
        eng.dma_start(dst, src).then_inc(self.sem, 16)
        self.v += 16

    def wait(self, eng):
        eng.wait_ge(self.sem, self.v)


def make_identity(nc, ident):
    nc.gpsimd.memset(ident, 0.0)
    nc.gpsimd.drain()
    nc.gpsimd.affine_select(
        out=ident, in_=ident, compare_op=ALU.not_equal, fill=1.0,
        base=0, pattern=[[-1, ident.shape[0]]], channel_multiplier=1)


def emit_qd_head(ve, srcs, qm, qs, qr, qsh):
    drs(ve)
    for X, c0, nb in srcs:
        ve.tensor_reduce(qm[:, c0:c0 + nb],
                         X.rearrange("p (nb w) -> p nb w", w=16),
                         axis=AX.X, op=ALU.max, apply_absolute_value=True)
    a = srcs[0][1]
    z = srcs[-1][1] + srcs[-1][2]
    dr(ve)
    ve.reciprocal(qs[:, a:z], qm[:, a:z])
    dr(ve)
    ve.tensor_scalar(qr[:, a:z], qs[:, a:z], 6.0, 1e12,
                     op0=ALU.mult, op1=ALU.min)
    ve.tensor_scalar(qsh[:, a:z], qm[:, a:z], 1.0 / 12.0, 5e-13,
                     op0=ALU.mult, op1=ALU.max)
    drs(ve)


def emit_qd_tail(ve, X, Q, qr, qsh, ta, tacc, tb, nb):
    w = nb * 16
    X3 = X.rearrange("p (nb w) -> p nb w", w=16)
    r_bc = qr.unsqueeze(2).broadcast_to((128, nb, 16))
    sh_bc = qsh.unsqueeze(2).broadcast_to((128, nb, 16))
    av, accv, bv = ta[:, :w], tacc[:, :w], tb[:, :w]
    drs(ve)
    ve.tensor_tensor(av.rearrange("p (nb w) -> p nb w", w=16), X3, r_bc, op=ALU.mult)
    drs(ve)
    ve.tensor_scalar(bv, av, -1.0, None, op0=ALU.mult)
    drs(ve)
    ve.tensor_tensor(av, av, bv, op=ALU.max)
    drs(ve)
    ve.tensor_scalar(accv, av, 0.25, None, op0=ALU.is_gt)
    for tau in (0.75, 1.25, 1.75):
        drs(ve)
        ve.scalar_tensor_tensor(accv, av, tau, accv, op0=ALU.is_gt, op1=ALU.add)
    ve.tensor_scalar(bv, av, 2.5, None, op0=ALU.is_gt)
    drs(ve)
    ve.scalar_tensor_tensor(bv, av, 3.5, bv, op0=ALU.is_gt, op1=ALU.add)
    drs(ve)
    ve.scalar_tensor_tensor(accv, bv, 2.0, accv, op0=ALU.mult, op1=ALU.add)
    drs(ve)
    ve.tensor_scalar(bv, av, 5.0, None, op0=ALU.is_gt)
    drs(ve)
    ve.scalar_tensor_tensor(accv, bv, 4.0, accv, op0=ALU.mult, op1=ALU.add)
    drs(ve)
    ve.tensor_tensor(accv.rearrange("p (nb w) -> p nb w", w=16),
                     accv.rearrange("p (nb w) -> p nb w", w=16), sh_bc, op=ALU.mult)
    ve.tensor_scalar(bv, X, 0.0, None, op0=ALU.is_ge)
    drs(ve)
    ve.tensor_scalar(bv, bv, 2.0, -1.0, op0=ALU.mult, op1=ALU.add)
    drs(ve)
    return ve.tensor_tensor(Q, accv, bv, op=ALU.mult)


def emit_qd_tail_s(ve, AX_, SG, Q, qr, qsh, ta, tacc, tb, nb):
    w = nb * 16
    r_bc = qr.unsqueeze(2).broadcast_to((128, nb, 16))
    sh_bc = qsh.unsqueeze(2).broadcast_to((128, nb, 16))
    av, accv, bv = ta[:, :w], tacc[:, :w], tb[:, :w]
    drs(ve)
    ve.tensor_tensor(av.rearrange("p (nb w) -> p nb w", w=16),
                     AX_.rearrange("p (nb w) -> p nb w", w=16), r_bc,
                     op=ALU.mult)
    drs(ve)
    ve.tensor_scalar(accv, av, 0.25, None, op0=ALU.is_gt)
    for tau in (0.75, 1.25, 1.75):
        drs(ve)
        ve.scalar_tensor_tensor(accv, av, tau, accv, op0=ALU.is_gt, op1=ALU.add)
    ve.tensor_scalar(bv, av, 2.5, None, op0=ALU.is_gt)
    drs(ve)
    ve.scalar_tensor_tensor(bv, av, 3.5, bv, op0=ALU.is_gt, op1=ALU.add)
    drs(ve)
    ve.scalar_tensor_tensor(accv, bv, 2.0, accv, op0=ALU.mult, op1=ALU.add)
    drs(ve)
    ve.tensor_scalar(bv, av, 5.0, None, op0=ALU.is_gt)
    drs(ve)
    ve.scalar_tensor_tensor(accv, bv, 4.0, accv, op0=ALU.mult, op1=ALU.add)
    drs(ve)
    ve.tensor_tensor(accv.rearrange("p (nb w) -> p nb w", w=16),
                     accv.rearrange("p (nb w) -> p nb w", w=16), sh_bc,
                     op=ALU.mult)
    drs(ve)
    return ve.tensor_tensor(Q, accv, SG, op=ALU.mult)


def emit_qd(ve, X, Q, qm, qs, qr, qsh, ta, tacc, tb, nb):
    emit_qd_head(ve, [(X, 0, nb)], qm, qs, qr, qsh)
    return emit_qd_tail(ve, X, Q, qr[:, 0:nb], qsh[:, 0:nb], ta, tacc, tb, nb)


def build_fused():
    nc = bass.Bass("TRN2", target_bir_lowering=False)
    xa = nc.dram_tensor("xa", [B * TE, C], F32, kind="ExternalInput").ap()
    xb = nc.dram_tensor("xb", [B, C], F32, kind="ExternalInput").ap()
    bmask_d = nc.dram_tensor("bmask", [128, 1], F32, kind="ExternalInput").ap()
    wkT = nc.dram_tensor("wkT", [C, C], F32, kind="ExternalInput").ap()
    wvT = nc.dram_tensor("wvT", [C, C], F32, kind="ExternalInput").ap()
    wrT = nc.dram_tensor("wrT", [C, C], F32, kind="ExternalInput").ap()
    tmH = nc.dram_tensor("tmH", [128, 3 * CCH], F32, kind="ExternalInput").ap()
    tdH = nc.dram_tensor("tdH", [128, CCH], F32, kind="ExternalInput").ap()
    tfH = nc.dram_tensor("tfH", [128, CCH], F32, kind="ExternalInput").ap()
    cvF_d = nc.dram_tensor("cvF", [128, 4 * CCH], F32, kind="ExternalInput").ap()
    wmod_d = nc.dram_tensor("wmod", [C, 6 * C], F32, kind="ExternalInput").ap()
    woT = nc.dram_tensor("woT", [C, C], F32, kind="ExternalInput").ap()
    wg_d = nc.dram_tensor("wg", [DI, C], F32, kind="ExternalInput").ap()
    wu_d = nc.dram_tensor("wu", [DI, C], F32, kind="ExternalInput").ap()
    wd_d = nc.dram_tensor("wd", [C, DI], F32, kind="ExternalInput").ap()
    outo = nc.dram_tensor("outo", [ROWS, C], F32, kind="ExternalOutput").ap()
    kvscr = nc.dram_tensor("kvscr", [2, CCH, 128, B * TE], F32, kind="Internal").ap()
    srscr = nc.dram_tensor("srscr", [C, ROWS], F32, kind="Internal").ap()
    yscr = nc.dram_tensor("yscr", [CCH, 128, ROWS], F32, kind="Internal").ap()
    x2scr = nc.dram_tensor("x2scr", [ROWS, C], F32, kind="Internal").ap()
    hqscr = nc.dram_tensor("hqscr", [32, 128, ICH * 128], BF16, kind="Internal").ap()

    idt = nc.alloc_sbuf_tensor("idt", [128, 128], F32)
    idt4 = nc.alloc_sbuf_tensor("idt4", [B, B], F32)
    idtb = nc.alloc_sbuf_tensor("idtb", [128, 128], BF16)
    epsv = nc.alloc_sbuf_tensor("epsv", [128, 1], F32)
    ones1 = nc.alloc_sbuf_tensor("ones1", [1, 128], F32)
    bmask = nc.alloc_sbuf_tensor("bmaskT", [128, 1], F32)
    scf = nc.alloc_sbuf_tensor("scf", [128, 4 * CCH], F32)
    scf_bf = nc.alloc_sbuf_tensor("scf_bf", [128, 4 * CCH], BF16)
    modF = nc.alloc_sbuf_tensor("modF", [128, 6 * CCH * 4], F32)
    ca1F = nc.alloc_sbuf_tensor("ca1F", [128, CCH * 4], F32)
    cb1F = nc.alloc_sbuf_tensor("cb1F", [128, CCH * 4], F32)
    gaF = nc.alloc_sbuf_tensor("gaF", [128, CCH * 4], F32)
    gbF = nc.alloc_sbuf_tensor("gbF", [128, CCH * 4], F32)
    sgt = nc.alloc_sbuf_tensor("sgt", [128, CCH * 4], F32)
    tmF = nc.alloc_sbuf_tensor("tmF", [128, 3 * CCH], F32)
    tm1F = nc.alloc_sbuf_tensor("tm1F", [128, 3 * CCH], F32)
    wF = nc.alloc_sbuf_tensor("wF", [128, CCH], F32)
    uF = nc.alloc_sbuf_tensor("uF", [128, CCH], F32)
    vrow = nc.alloc_sbuf_tensor("vrow", [1, C], F32)
    st6 = nc.alloc_sbuf_tensor("st6", [128, 12], F32)
    mv = nc.alloc_sbuf_tensor("mv", [128, 2], F32)
    sd = nc.alloc_sbuf_tensor("sd", [128, 1], F32)
    sinv = nc.alloc_sbuf_tensor("sinv", [128, 1], F32)
    nbv = nc.alloc_sbuf_tensor("nbv", [128, 1], F32)
    qd_m = nc.alloc_sbuf_tensor("qd_m", [128, 240], F32)
    qd_s = nc.alloc_sbuf_tensor("qd_s", [128, 240], F32)
    qd_r = nc.alloc_sbuf_tensor("qd_r", [128, 240], F32)
    qd_sh = nc.alloc_sbuf_tensor("qd_sh", [128, 240], F32)

    ps = [nc.alloc_psum_tensor(f"ps{i}", [128, 512], F32) for i in range(4)]
    psb = [nc.alloc_psum_tensor(f"psb{i}", [128, 512], BF16) for i in range(2)]
    pmod = nc.alloc_psum_tensor("pmod", [128, 192], F32)

    dma = Ctr(nc.alloc_semaphore("dmas"))
    dmo = Ctr(nc.alloc_semaphore("dmao"))

    def ln_rowstats(xt, p=128):
        with nc.Block() as blk:
            @blk.vector
            def _(ve):
                ve.bn_stats(st6[0:p, 0:6], xt[:, 0:512])
                ve.bn_stats(st6[0:p, 6:12], xt[:, 512:1024])
                dr(ve)
                ve.bn_aggr(mv[0:p, :], st6[0:p, :])
        with nc.Block() as blk:
            @blk.scalar
            def _(sc):
                sc.activation(sd[0:p, :], mv[0:p, 1:2], AF.Sqrt, bias=epsv[0:p, 0:1])
        with nc.Block() as blk:
            @blk.vector
            def _(ve):
                ve.reciprocal(sinv[0:p, :], sd[0:p, :])
                dr(ve)
                ve.scalar_tensor_tensor(nbv[0:p, :], mv[0:p, 0:1], -1.0, sinv[0:p, :],
                                        op0=ALU.mult, op1=ALU.mult)

    with nc.Block() as blk:
        @blk.gpsimd
        def _(g):
            make_identity(nc, idt[:])
            make_identity(nc, idt4[:])
            make_identity(nc, idtb[:])
            g.memset(epsv[:], EPS)
            g.memset(ones1[:], 1.0)

        @blk.sync
        def _(sync):
            dma.start(sync, tmF[:], tmH)
            dma.start(sync, wF[:], tdH)
            dma.start(sync, uF[:], tfH)
            dma.start(sync, bmask[:], bmask_d)
            dma.start(sync, scf[:], cvF_d)
            dma.wait(sync)

    with nc.Block() as blk:
        @blk.vector
        def _(ve):
            ve.tensor_scalar(tm1F[:], tmF[:], -1.0, 1.0, op0=ALU.mult, op1=ALU.add)

        @blk.scalar
        def _(sc):
            sc.activation(wF[:], wF[:], AF.Exp)
            dr(sc)
            sc.activation(scf[:], scf[:], AF.Silu)

    with nc.Block() as blk:
        @blk.vector
        def _(ve):
            ve.tensor_scalar(wF[:], wF[:], -1.0, None, op0=ALU.mult)
            ve.tensor_copy(scf_bf[:], scf[:])

    with nc.sbuf_tensor("wmt", [128, 6 * C], F32) as wmt_h, \
         nc.sbuf_tensor("wmb", [128, 6 * C], BF16) as wmb_h:
        wmt, wmb = wmt_h.ap(), wmb_h.ap()
        for ci in range(CCH):
            with nc.Block() as blk:
                @blk.sync
                def _(sync, ci=ci):
                    dma.start(sync, wmt, wmod_d[ci * 128:(ci + 1) * 128, :])
                    dma.wait(sync)
            with nc.Block() as blk:
                @blk.vector
                def _(ve):
                    ve.tensor_copy(wmb, wmt)
            with nc.Block() as blk:
                @blk.tensor
                def _(te, ci=ci):
                    for oc in range(48):
                        te.matmul(pmod[:, oc * 4:(oc + 1) * 4],
                                  wmb[:, oc * 128:(oc + 1) * 128],
                                  scf_bf[:, ci * 4:(ci + 1) * 4],
                                  start=True, stop=True)
            with nc.Block() as blk:
                @blk.vector
                def _(ve, ci=ci):
                    if ci == 0:
                        ve.tensor_copy(modF[:], pmod[:])
                    else:
                        ve.tensor_tensor(modF[:], modF[:], pmod[:], op=ALU.add)

    with nc.Block() as blk:
        @blk.vector
        def _(ve):
            ve.tensor_scalar(ca1F[:], modF[:, 32:64], 1.0, None, op0=ALU.add)
            ve.tensor_scalar(cb1F[:], modF[:, 128:160], 1.0, None, op0=ALU.add)
            for (s0, s1), dstF in (((64, 96), gaF), ((160, 192), gbF)):
                dr(ve)
                ve.tensor_scalar(sgt[:], modF[:, s0:s1], 0.0, None, op0=ALU.is_ge)
                dr(ve)
                ve.tensor_scalar(sgt[:], sgt[:], 2.0, -1.0, op0=ALU.mult, op1=ALU.add)
                ve.tensor_scalar(dstF[:], modF[:, s0:s1], -1.0, None, op0=ALU.mult)
                dr(ve)
                ve.tensor_tensor(dstF[:], dstF[:], modF[:, s0:s1], op=ALU.max)
                dr(ve)
                ve.tensor_scalar(dstF[:], dstF[:], 0.25, None, op0=ALU.max)
                dr(ve)
                ve.tensor_tensor(dstF[:], dstF[:], sgt[:], op=ALU.mult)

    def pmm(ng):
        return ps[ng % 2][:]

    with ExitStack() as stk:
        def sb(name, shape, dt):
            return stk.enter_context(nc.sbuf_tensor(name, shape, dt)).ap()
        wkb = sb("wkb", [128, CCH * C], BF16)
        wvb = sb("wvb", [128, CCH * C], BF16)
        wrb = sb("wrb", [128, CCH * C], BF16)
        h1F = sb("h1F", [128, CCH * (TE + 1)], F32)
        xkb = sb("xkb", [128, CCH * TE], BF16)
        xvb = sb("xvb", [128, CCH * TE], BF16)
        xrb = sb("xrb", [128, CCH * TK], BF16)
        xt2 = [sb(f"xt{i}", [128, C], F32) for i in range(3)]
        t12 = xt2
        st62 = [sb(f"st6_{i}", [128, 12], F32) for i in range(2)]
        mv2 = [sb(f"mv_{i}", [128, 2], F32) for i in range(2)]
        sd2 = [sb(f"sd_{i}", [128, 1], F32) for i in range(2)]
        si2 = [sb(f"si_{i}", [128, 1], F32) for i in range(2)]
        nb2 = [sb(f"nb_{i}", [128, 1], F32) for i in range(2)]
        pre2 = [sb(f"pre_{i}", [128, TE], F32) for i in range(2)]
        kv2 = [sb(f"kv_{i}", [128, 512], F32) for i in range(2)]
        xbt = sb("xbt", [B, C], F32)
        t1b = xbt
        hbF = sb("hbF", [128, CCH * 4], F32)

        if True:
            wpair = [xt2[0], xt2[1]]
            sFL = [nc.alloc_semaphore(f"p1FL{i}") for i in range(2)]
            sFC = nc.alloc_semaphore("p1FC")
            fitems = [(wdram, wbuf, ci)
                      for wdram, wbuf in ((wkT, wkb), (wvT, wvb), (wrT, wrb))
                      for ci in range(CCH)]
            with nc.Block() as blk:
                @blk.sync
                def _(sync):
                    for i, (wdram, wbuf, ci) in enumerate(fitems):
                        if i % 2 == 1:
                            continue
                        if i >= 2:
                            sync.wait_ge(sFC, i - 1)
                        sync.dma_start(wpair[0],
                                       wdram[ci * 128:(ci + 1) * 128, :]
                                       ).then_inc(sFL[0], 16)

                @blk.gpsimd
                def _(g_):
                    for i, (wdram, wbuf, ci) in enumerate(fitems):
                        if i % 2 == 0:
                            continue
                        if i >= 2:
                            g_.wait_ge(sFC, i - 1)
                        g_.dma_start(wpair[1],
                                     wdram[ci * 128:(ci + 1) * 128, :]
                                     ).then_inc(sFL[1], 16)
                        g_.wait_ge(sFL[1], 16 * (i // 2 + 1))

                @blk.vector
                def _(ve):
                    for i, (wdram, wbuf, ci) in enumerate(fitems):
                        ve.wait_ge(sFL[i % 2], 16 * (i // 2 + 1))
                        ve.tensor_copy(wbuf[:, ci * C:(ci + 1) * C],
                                       wpair[i % 2]).then_inc(sFC, 1)

        with nc.Block() as blk:
            @blk.sync
            def _(sync):
                dma.start(sync, xbt, xb)
                dma.wait(sync)
        with nc.Block() as blk:
            @blk.vector
            def _(ve):
                ve.bn_stats(st62[0][0:B, 0:6], xbt[:, 0:512])
                ve.bn_stats(st62[0][0:B, 6:12], xbt[:, 512:1024])
                dr(ve)
                ve.bn_aggr(mv2[0][0:B, :], st62[0][0:B, :])
        with nc.Block() as blk:
            @blk.scalar
            def _(sc):
                sc.activation(sd2[0][0:B, :], mv2[0][0:B, 1:2], AF.Sqrt,
                              bias=epsv[0:B, 0:1])
        with nc.Block() as blk:
            @blk.vector
            def _(ve):
                ve.reciprocal(si2[0][0:B, :], sd2[0][0:B, :])
                dr(ve)
                ve.scalar_tensor_tensor(nb2[0][0:B, :], mv2[0][0:B, 0:1], -1.0,
                                        si2[0][0:B, :], op0=ALU.mult, op1=ALU.mult)
        with nc.Block() as blk:
            @blk.scalar
            def _(sc):
                sc.activation(t1b, xbt, AF.Identity,
                              bias=nb2[0][0:B, 0:1], scale=si2[0][0:B, 0:1])
        with nc.Block() as blk:
            @blk.tensor
            def _(te):
                for ci in range(CCH):
                    te.transpose(ps[0][:, ci * 4:(ci + 1) * 4],
                                 t1b[:, ci * 128:(ci + 1) * 128], idt4[:])
        with nc.Block() as blk:
            @blk.vector
            def _(ve):
                ve.tensor_copy(hbF[:], ps[0][:, 0:32])
        with nc.Block() as blk:
            @blk.vector
            def _(ve):
                for ci in range(CCH):
                    for b in range(B):
                        col = ci * 4 + b
                        ve.scalar_tensor_tensor(
                            hbF[:, col:col + 1], hbF[:, col:col + 1],
                            ca1F[:, col:col + 1], modF[:, col:col + 1],
                            op0=ALU.mult, op1=ALU.add)
                dr(ve)
                ve.tensor_scalar(hbF[:], hbF[:], bmask[:, 0:1], None, op0=ALU.mult)

        sX = nc.alloc_semaphore("p1X")
        sBN = nc.alloc_semaphore("p1BN")
        sSQ = nc.alloc_semaphore("p1SQ")
        sNB = nc.alloc_semaphore("p1NB")
        sT1 = nc.alloc_semaphore("p1T1")
        sTR = nc.alloc_semaphore("p1TR")
        sEV = nc.alloc_semaphore("p1EV")
        sMD = nc.alloc_semaphore("p1MD")
        sMK = nc.alloc_semaphore("p1MK")
        sPR = nc.alloc_semaphore("p1PR")
        sMX = nc.alloc_semaphore("p1MX")
        sMM = nc.alloc_semaphore("p1MM")
        sEC = nc.alloc_semaphore("p1EC")
        kos = [nc.alloc_semaphore(f"p1KO{i}") for i in range(2)]
        NR = TE // 128
        NMIX = 3 * CCH
        NG = 2 * 3 * CCH + 2 * CCH

        groups = []
        for pi, (wbuf, xmb, tw) in enumerate(((wkb, xkb, TE), (wvb, xvb, TE),
                                              (wrb, xrb, TK))):
            for d in range(CCH):
                for th in range(tw // 512):
                    groups.append((pi, wbuf, xmb, tw, d, th))
        assert len(groups) == NG

        with nc.Block() as blk:
            @blk.gpsimd
            def _(g_):
                for b in range(B):
                    for r in range(NR):
                        n = b * NR + r
                        if n >= 3:
                            g_.wait_ge(sTR, n - 2)
                        g_.dma_start(xt2[n % 3][:],
                                     xa[(b * TE + r * 128):(b * TE + r * 128 + 128), :]
                                     ).then_inc(sX, 16)
                        g_.wait_ge(sX, 16 * (n + 1))

            @blk.sync
            def _(sync):
                for b in range(B):
                    for g, (pi, wbuf, xmb, tw, d, th) in enumerate(groups):
                        ng = b * NG + g
                        sync.wait_ge(sEC, ng + 1)
                        if pi < 2:
                            off2 = b * TE + th * 512
                            sync.dma_start(kvscr[pi, d, :, off2:off2 + 512],
                                           kv2[ng % 2][:]).then_inc(kos[ng % 2], 16)
                        else:
                            off2 = b * TK + th * 512
                            sync.dma_start(srscr[d * 128:(d + 1) * 128,
                                                 off2:off2 + 512],
                                           kv2[ng % 2][:]).then_inc(kos[ng % 2], 16)
                sync.wait_ge(kos[0], 16 * (B * NG // 2))
                sync.wait_ge(kos[1], 16 * (B * NG // 2))

            @blk.vector
            def _(ve):
                def bn(b, r):
                    n = b * NR + r
                    p = n % 2
                    ve.wait_ge(sX, 16 * (n + 1))
                    ve.bn_stats(st62[p][:, 0:6], xt2[n % 3][:, 0:512])
                    ve.bn_stats(st62[p][:, 6:12], xt2[n % 3][:, 512:1024])
                    dr(ve)
                    ve.bn_aggr(mv2[p][:], st62[p][:]).then_inc(sBN, 1)

                def recip(b, r):
                    n = b * NR + r
                    p = n % 2
                    ve.wait_ge(sSQ, n + 1)
                    if n >= 2:
                        ve.wait_ge(sT1, n - 1)
                    ve.reciprocal(si2[p][:], sd2[p][:])
                    dr(ve)
                    ve.scalar_tensor_tensor(nb2[p][:], mv2[p][:, 0:1], -1.0,
                                            si2[p][:], op0=ALU.mult, op1=ALU.mult
                                            ).then_inc(sNB, 1)

                def evac(b, r):
                    n = b * NR + r
                    ve.wait_ge(sTR, n + 1)
                    h3 = h1F.rearrange("p (ci x) -> p ci x", ci=CCH)
                    for g in range(2):
                        pt = ps[2] if g == 0 else ps[3]
                        dst = h3[:, g * 4:(g + 1) * 4, 1 + r * 128:1 + (r + 1) * 128]
                        inst = ve.tensor_copy(dst,
                                              pt[:].rearrange("p (ci x) -> p ci x", ci=4))
                    inst.then_inc(sEV, 1)

                for b in range(B):
                    if b > 0:
                        ve.wait_ge(sPR, NMIX * b)
                    bn(b, 0)
                    bn(b, 1)
                    for r in range(NR):
                        recip(b, r)
                        if r + 2 < NR:
                            bn(b, r + 2)
                        evac(b, r)
                    for ci in range(CCH):
                        ve.tensor_copy(h1F[:, ci * (TE + 1):ci * (TE + 1) + 1],
                                       hbF[:, ci * 4 + b:ci * 4 + b + 1])
                    ve.wait_ge(sMD, b + 1)
                    for ci in range(CCH):
                        inst = ve.tensor_scalar(
                            h1F[:, ci * (TE + 1) + W:ci * (TE + 1) + W + 1],
                            h1F[:, ci * (TE + 1) + W:ci * (TE + 1) + W + 1],
                            bmask[:, 0:1], None, op0=ALU.mult)
                    inst.then_inc(sMK, 1)
                    if b > 0:
                        ve.wait_ge(sMM, NG * b)
                    for mi, xmb, tw in ((0, xkb, TE), (1, xvb, TE), (2, xrb, TK)):
                        off = 0 if tw == TE else W
                        for ci in range(CCH):
                            i = b * NMIX + mi * CCH + ci
                            ve.wait_ge(sPR, i + 1)
                            ve.scalar_tensor_tensor(
                                xmb[:, ci * tw:ci * tw + tw],
                                h1F[:, ci * (TE + 1) + off + 1:ci * (TE + 1) + off + tw + 1],
                                tmF[:, mi * CCH + ci:mi * CCH + ci + 1],
                                pre2[i % 2][:, 0:tw], op0=ALU.mult, op1=ALU.add
                                ).then_inc(sMX, 1)

            @blk.scalar
            def _(sc):
                def sqt(b2, r):
                    n = b2 * NR + r
                    p = n % 2
                    sc.wait_ge(sBN, n + 1)
                    sc.activation(sd2[p][:], mv2[p][:, 1:2], AF.Sqrt,
                                  bias=epsv[:, 0:1]).then_inc(sSQ, 1)
                    sc.wait_ge(sNB, n + 1)
                    if n >= 3:
                        sc.wait_ge(sTR, n - 2)
                    sc.activation(t12[n % 3][:], xt2[n % 3][:], AF.Identity,
                                  bias=nb2[p][:, 0:1], scale=si2[p][:, 0:1]
                                  ).then_inc(sT1, 1)

                for b in range(B):
                    if b == 0:
                        for r in range(NR):
                            sqt(0, r)
                    sc.wait_ge(sEV, NR * (b + 1))
                    for ci in range(CCH):
                        col = ci * 4 + b
                        inst = sc.activation(
                            h1F[:, ci * (TE + 1) + 1:(ci + 1) * (TE + 1)],
                            h1F[:, ci * (TE + 1) + 1:(ci + 1) * (TE + 1)],
                            AF.Identity, bias=modF[:, col:col + 1],
                            scale=ca1F[:, col:col + 1])
                    inst.then_inc(sMD, 1)
                    sc.wait_ge(sMK, b + 1)
                    for mi, tw in ((0, TE), (1, TE), (2, TK)):
                        off = 0 if tw == TE else W
                        for ci in range(CCH):
                            i = b * NMIX + mi * CCH + ci
                            if i >= 2:
                                sc.wait_ge(sMX, i - 1)
                            sc.activation(
                                pre2[i % 2][:, 0:tw],
                                h1F[:, ci * (TE + 1) + off:ci * (TE + 1) + off + tw],
                                AF.Identity,
                                scale=tm1F[:, mi * CCH + ci:mi * CCH + ci + 1]
                                ).then_inc(sPR, 1)
                    for g, (pi, wbuf, xmb, tw, d, th) in enumerate(groups):
                        ng = b * NG + g
                        sc.wait_ge(sMM, ng + 1)
                        if ng >= 2:
                            sc.wait_ge(kos[ng % 2], 16 * (ng // 2))
                        if pi < 2:
                            sc.activation(kv2[ng % 2][:], pmm(ng), AF.Identity
                                          ).then_inc(sEC, 1)
                        else:
                            sc.activation(kv2[ng % 2][:], pmm(ng), AF.Sigmoid
                                          ).then_inc(sEC, 1)
                        if b + 1 < B and g >= 1 and (g - 1) % 5 == 0 \
                                and (g - 1) // 5 < NR:
                            sqt(b + 1, (g - 1) // 5)

            @blk.tensor
            def _(te):
                def trow(b2, r):
                    n = b2 * NR + r
                    te.wait_ge(sT1, n + 1)
                    if n >= 1:
                        te.wait_ge(sEV, n)
                    for ci in range(CCH):
                        pt = ps[2] if ci < 4 else ps[3]
                        inst = te.transpose(
                            pt[:, (ci % 4) * 128:(ci % 4 + 1) * 128],
                            t12[n % 3][:, ci * 128:(ci + 1) * 128], idt[:])
                    inst.then_inc(sTR, 1)

                for r in range(NR):
                    trow(0, r)
                for b in range(B):
                    for g, (pi, wbuf, xmb, tw, d, th) in enumerate(groups):
                        ng = b * NG + g
                        te.wait_ge(sMX, b * NMIX + 8 * (pi + 1))
                        if ng >= 2:
                            te.wait_ge(sEC, ng - 1)
                        pm = pmm(ng)
                        for ci in range(CCH):
                            inst = te.matmul(
                                pm,
                                wbuf[:, ci * C + d * 128:ci * C + (d + 1) * 128],
                                xmb[:, ci * tw + th * 512:ci * tw + th * 512 + 512],
                                start=(ci == 0), stop=(ci == CCH - 1))
                        inst.then_inc(sMM, 1)
                        if b + 1 < B and g >= 2 and (g - 2) % 5 == 0 \
                                and (g - 2) // 5 < NR:
                            trow(b + 1, (g - 2) // 5)
    with ExitStack() as stk:
        def sb(name, shape, dt):
            return stk.enter_context(nc.sbuf_tensor(name, shape, dt)).ap()
        kt = [sb(f"kt{i}", [128, TE], F32) for i in range(2)]
        vt = [sb(f"vt{i}", [128, TE], F32) for i in range(2)]
        ppb = [sb(f"ppb{i}", [128, TE + 1], F32) for i in range(2)]
        aab = [sb(f"aab{i}", [128, TE + 1], F32) for i in range(2)]
        bbb = [sb(f"bbb{i}", [128, TE + 1], F32) for i in range(2)]
        e1b = [sb(f"e1b{i}", [128, TE], F32) for i in range(2)]
        e2b = [sb(f"e2b{i}", [128, TE], F32) for i in range(2)]
        d1 = [sb(f"d1_{i}", [128, TE], F32) for i in range(2)]
        gar = [sb(f"gar{i}", [128, TK], F32) for i in range(2)]
        gv = [sb(f"gv{i}", [128, TK], F32) for i in range(2)]
        num = [sb(f"num{i}", [128, TK], F32) for i in range(2)]
        den = [sb(f"den{i}", [128, TK], F32) for i in range(2)]
        yv = [sb(f"yv{i}", [128, TK], F32) for i in range(2)]

        lds = [nc.alloc_semaphore(f"p2LD{i}") for i in range(2)]
        sMS = nc.alloc_semaphore("p2MS")
        sVA = nc.alloc_semaphore("p2VA")
        sSE = nc.alloc_semaphore("p2SE")
        sVN = nc.alloc_semaphore("p2VN")
        sSD = nc.alloc_semaphore("p2SD")
        sVY = nc.alloc_semaphore("p2VY")
        sYO = nc.alloc_semaphore("p2YO")
        NIT = B * CCH

        def it_bc(i):
            return i // CCH, i % CCH

        with nc.Block() as blk:
            @blk.sync
            def _(sync):
                for i in range(NIT):
                    b, ci = it_bc(i)
                    p = i % 2
                    if i >= 2:
                        sync.wait_ge(sVN, i - 1)
                        sync.wait_ge(lds[p], 32 * (i // 2))
                    off = b * TE
                    sync.dma_start(kt[p][:], kvscr[0, ci, :, off:off + TE]
                                   ).then_inc(lds[p], 16)
                    sync.dma_start(vt[p][:], kvscr[1, ci, :, off:off + TE]
                                   ).then_inc(lds[p], 16)
                    if i >= 1:
                        bo, cio = it_bc(i - 1)
                        sync.wait_ge(sVY, i)
                        if i >= 2:
                            sync.wait_ge(sYO, 16 * (i - 1))
                        sync.dma_start(yscr[cio, :, bo * TK:(bo + 1) * TK],
                                       yv[(i - 1) % 2][:]).then_inc(sYO, 16)
                bo, cio = it_bc(NIT - 1)
                sync.wait_ge(sVY, NIT)
                sync.wait_ge(sYO, 16 * (NIT - 1))
                sync.dma_start(yscr[cio, :, bo * TK:(bo + 1) * TK],
                               yv[(NIT - 1) % 2][:]).then_inc(sYO, 16)
                sync.wait_ge(sYO, 16 * NIT)

            @blk.gpsimd
            def _(g):
                for i in range(NIT):
                    p = i % 2
                    if i >= 2:
                        g.wait_ge(sVN, i - 1)
                    g.memset(ppb[p][:, 0:1], NEG_BIG)
                    g.memset(aab[p][:, 0:1], 0.0)
                    g.memset(bbb[p][:, 0:1], 0.0)
                    g.drain()
                    g.sem_inc(sMS, 1)

            @blk.vector
            def _(ve):
                def S1(i):
                    b, ci = it_bc(i)
                    p = i % 2
                    ve.wait_ge(lds[p], 32 * (i // 2 + 1))
                    ve.wait_ge(sMS, i + 1)
                    ve.tensor_tensor_scan(
                        ppb[p][:, 1:TE + 1],
                        wF[:, ci:ci + 1].broadcast_to((128, TE)), kt[p][:],
                        ppb[p][:, 0:1], op0=ALU.add, op1=ALU.max)
                    drh(ve)
                    ve.tensor_tensor(e2b[p][:], kt[p][:], ppb[p][:, 1:TE + 1],
                                     op=ALU.subtract)
                    ve.tensor_tensor(e1b[p][:], ppb[p][:, 0:TE], ppb[p][:, 1:TE + 1],
                                     op=ALU.subtract)
                    ve.scalar_tensor_tensor(gar[p][:], kt[p][:, W:TE],
                                            uF[:, ci:ci + 1], ppb[p][:, W:TE],
                                            op0=ALU.add, op1=ALU.subtract
                                            ).then_inc(sVA, 1)

                def S2(i):
                    p = i % 2
                    ve.wait_ge(sSE, i + 1)
                    ve.tensor_tensor(d1[p][:], e2b[p][:], vt[p][:], op=ALU.mult)
                    ve.tensor_tensor(gv[p][:], gar[p][:], vt[p][:, W:TE], op=ALU.mult)
                    drs(ve)
                    ve.tensor_scalar(d1[p][:, 0:W], d1[p][:, 0:W], bmask[:, 0:1],
                                     None, op0=ALU.mult)
                    ve.tensor_scalar(e2b[p][:, 0:W], e2b[p][:, 0:W], bmask[:, 0:1],
                                     None, op0=ALU.mult)
                    drh(ve)
                    ve.tensor_tensor_scan(aab[p][:, 1:TE + 1], e1b[p][:], d1[p][:],
                                          aab[p][:, 0:1], op0=ALU.mult, op1=ALU.add)
                    ve.tensor_tensor_scan(bbb[p][:, 1:TE + 1], e1b[p][:], e2b[p][:],
                                          bbb[p][:, 0:1], op0=ALU.mult, op1=ALU.add)
                    drh(ve)
                    ve.tensor_tensor(num[p][:], aab[p][:, W:TE], gv[p][:], op=ALU.add)
                    ve.tensor_tensor(den[p][:], bbb[p][:, W:TE], gar[p][:], op=ALU.add
                                     ).then_inc(sVN, 1)

                def S3(i):
                    p = i % 2
                    ve.wait_ge(sSD, i + 1)
                    if i >= 2:
                        ve.wait_ge(sYO, 16 * (i - 1))
                    ve.tensor_tensor(yv[p][:], num[p][:], den[p][:], op=ALU.mult
                                     ).then_inc(sVY, 1)

                S1(0)
                for i in range(NIT):
                    S2(i)
                    if i + 1 < NIT:
                        S1(i + 1)
                    S3(i)

            @blk.scalar
            def _(sc):
                for i in range(NIT):
                    b, ci = it_bc(i)
                    p = i % 2
                    sc.wait_ge(sVA, i + 1)
                    sc.activation(e2b[p][:], e2b[p][:], AF.Exp)
                    sc.activation(e1b[p][:], e1b[p][:], AF.Exp, bias=wF[:, ci:ci + 1])
                    sc.activation(gar[p][:], gar[p][:], AF.Exp).then_inc(sSE, 1)
                    sc.wait_ge(sVN, i + 1)
                    sc.activation(den[p][:], den[p][:], AF.Ln)
                    dr(sc)
                    sc.activation(den[p][:], den[p][:], AF.Exp, scale=-1.0
                                  ).then_inc(sSD, 1)

    with ExitStack() as stk0:
        def sbw(name, shape, dt):
            return stk0.enter_context(nc.sbuf_tensor(name, shape, dt)).ap()
        wgqT = sbw("wgqT", [128, CCH * DI], BF16)
        wuqT = sbw("wuqT", [128, CCH * DI], BF16)
        ps4 = nc.alloc_psum_tensor("ps4", [128, 512], F32)

        with ExitStack() as stk1:
            def sb1(name, shape, dt):
                return stk1.enter_context(nc.sbuf_tensor(name, shape, dt)).ap()
            wscr1 = [sb1(f"wscr1_{i}", [128, C], F32) for i in range(4)]
            axs1 = [sb1(f"axs1_{i}", [128, C], F32) for i in range(4)]
            sgs1 = [sb1(f"sgs1_{i}", [128, C], F32) for i in range(4)]
            ta1 = sb1("qta1", [128, C], F32)
            tacc1 = sb1("qtacc1", [128, C], F32)
            tb1 = sb1("qtb1", [128, C], F32)
            qtile1 = [sb1(f"qtile1_{i}", [128, C], BF16) for i in range(4)]
            qdl_m = sb1("qdl_m", [128, 256], F32)
            qdl_s = sb1("qdl_s", [128, 256], F32)
            qdl_r = sb1("qdl_r", [128, 256], F32)
            qdl_sh = sb1("qdl_sh", [128, 256], F32)
            sWL = [nc.alloc_semaphore(f"p3WL{i}") for i in range(2)]
            sAS = nc.alloc_semaphore("p3AS")
            sQ1 = nc.alloc_semaphore("p3Q1")
            sT3 = nc.alloc_semaphore("p3T3")
            sE3 = nc.alloc_semaphore("p3E3")
            wjobs = [(wg_d, wgqT, wch) for wch in range(ICH)] + \
                    [(wu_d, wuqT, wch) for wch in range(ICH)]
            NWJ = len(wjobs)

            with nc.Block() as blk:
                @blk.gpsimd
                def _(g_):
                    for i, (wdram, wdst, wch) in enumerate(wjobs):
                        if i >= 4:
                            g_.wait_ge(sQ1, i - 3)
                        g_.dma_start(wscr1[i % 4], wdram[wch * 128:(wch + 1) * 128, :]
                                     ).then_inc(sWL[i % 2], 16)
                        g_.wait_ge(sWL[i % 2], 16 * (i // 2 + 1))

                @blk.scalar
                def _(sc):
                    for i, (wdram, wdst, wch) in enumerate(wjobs):
                        sc.wait_ge(sWL[i % 2], 16 * (i // 2 + 1))
                        if i >= 4:
                            sc.wait_ge(sQ1, i - 3)
                        sc.activation(axs1[i % 4][:], wscr1[i % 4][:], AF.Abs)
                        sc.activation(sgs1[i % 4][:], wscr1[i % 4][:], AF.Sign
                                      ).then_inc(sAS, 1)
                        if i >= 1:
                            _, wdst_p, wch_p = wjobs[i - 1]
                            sc.wait_ge(sT3, i)
                            d3 = wdst_p.rearrange("p (ci x) -> p ci x", ci=CCH)
                            for g in range(2):
                                inst = sc.activation(
                                    d3[:, g * 4:(g + 1) * 4,
                                       wch_p * 128:(wch_p + 1) * 128],
                                    psb[g][:].rearrange("p (ci x) -> p ci x",
                                                        ci=4), AF.Copy)
                            inst.then_inc(sE3, 1)
                    _, wdst_p, wch_p = wjobs[NWJ - 1]
                    sc.wait_ge(sT3, NWJ)
                    d3 = wdst_p.rearrange("p (ci x) -> p ci x", ci=CCH)
                    for g in range(2):
                        inst = sc.activation(
                            d3[:, g * 4:(g + 1) * 4,
                               wch_p * 128:(wch_p + 1) * 128],
                            psb[g][:].rearrange("p (ci x) -> p ci x", ci=4),
                            AF.Copy)
                    inst.then_inc(sE3, 1)

                @blk.vector
                def _(ve):
                    for g4 in range(NWJ // 4):
                        j0 = 4 * g4
                        for h in range(2):
                            ve.wait_ge(sWL[h], 16 * (j0 // 2 + 2))
                        if g4 >= 1:
                            ve.wait_ge(sT3, j0)
                        emit_qd_head(ve, [(wscr1[k], 64 * k, 64)
                                          for k in range(4)],
                                     qdl_m, qdl_s, qdl_r, qdl_sh)
                        for k in range(4):
                            j = j0 + k
                            ve.wait_ge(sAS, j + 1)
                            inst = emit_qd_tail_s(
                                ve, axs1[k], sgs1[k], qtile1[k],
                                qdl_r[:, 64 * k:64 * k + 64],
                                qdl_sh[:, 64 * k:64 * k + 64],
                                ta1, tacc1, tb1, C // 16)
                            inst.then_inc(sQ1, 1)

                @blk.tensor
                def _(te):
                    for i in range(NWJ):
                        te.wait_ge(sQ1, i + 1)
                        if i >= 1:
                            te.wait_ge(sE3, i)
                        for g in range(2):
                            for k in range(4):
                                ci = g * 4 + k
                                inst = te.transpose(
                                    psb[g][:, k * 128:(k + 1) * 128],
                                    qtile1[i % 4][:, ci * 128:(ci + 1) * 128],
                                    idtb[:])
                        inst.then_inc(sT3, 1)

        with ExitStack() as stk2:
            def sb2(name, shape, dt):
                return stk2.enter_context(nc.sbuf_tensor(name, shape, dt)).ap()
            wob = sb2("wob", [128, CCH * C], BF16)
            ga_bc = sb2("ga_bc", [128, C], F32)
            cb1_bc = sb2("cb1_bc", [128, C], F32)
            sb_bc = sb2("sb_bc", [128, C], F32)
            yt0 = sb2("yt0", [128, CCH * 128], F32)
            srx0 = sb2("srx0", [128, CCH * 128], F32)
            yt = [yt0, yt0]
            srx = [srx0, srx0]
            xrow = [sb2(f"xrow{i}", [128, C], F32) for i in range(2)]
            rwkv = sb2("rwkv", [128, CCH * 128], BF16)
            x2t = sb2("x2t", [128, C], F32)
            h2t = sb2("h2t", [128, C], F32)
            hqt = sb2("hqt", [128, C], BF16)
            hqF = sb2("hqF", [128, CCH * 128], BF16)
            atmp = sb2("atmp", [128, 512], F32)
            sg3 = [sb2(f"sg512_{i}", [128, 512], F32) for i in range(3)]
            hh = sb2("hh", [128, DI], F32)
            hhqt = sb2("hhqt", [128, DI], BF16)
            hhqF0 = sb2("hhqF0", [128, ICH * 128], BF16)
            hhqF = [hhqF0, hhqF0]
            HV = 1536
            ta2 = sb2("qta2", [128, HV], F32)
            tacc2 = sb2("qtacc2", [128, HV], F32)
            tb2 = sb2("qtb2", [128, HV], F32)
            GQ = (DI - HV) // 2

            sWoL = [nc.alloc_semaphore(f"p3WoL{i}") for i in range(2)]
            sWoC = nc.alloc_semaphore("p3WoC")
            with nc.Block() as blk:
                @blk.sync
                def _(sync):
                    for ci in range(CCH):
                        if ci >= 2:
                            sync.wait_ge(sWoC, ci - 1)
                        sync.dma_start(hh[:, (ci % 2) * C:(ci % 2) * C + C],
                                       woT[ci * 128:(ci + 1) * 128, :]
                                       ).then_inc(sWoL[ci % 2], 16)

                @blk.vector
                def _(ve):
                    for ci in range(CCH):
                        ve.wait_ge(sWoL[ci % 2], 16 * (ci // 2 + 1))
                        ve.tensor_copy(wob[:, ci * C:(ci + 1) * C],
                                       hh[:, (ci % 2) * C:(ci % 2) * C + C]
                                       ).then_inc(sWoC, 1)

            sLD3 = [nc.alloc_semaphore(f"p3LD{i}") for i in range(2)]
            sAF3 = nc.alloc_semaphore("p3AF3")
            sABh = nc.alloc_semaphore("p3ABh")
            sABq = nc.alloc_semaphore("p3ABq")
            sVR = nc.alloc_semaphore("p3VR")
            sRP = nc.alloc_semaphore("p3RP")
            sRB = nc.alloc_semaphore("p3RB")
            sRW = nc.alloc_semaphore("p3RW")
            sWO = nc.alloc_semaphore("p3WO")
            sX2 = nc.alloc_semaphore("p3X2")
            sXO = nc.alloc_semaphore("p3XO")
            sBN2 = nc.alloc_semaphore("p3BN")
            sSQ2 = nc.alloc_semaphore("p3SQ")
            sNB2 = nc.alloc_semaphore("p3NB")
            sAF2 = nc.alloc_semaphore("p3AF")
            sQH = nc.alloc_semaphore("p3QH")
            sTH = nc.alloc_semaphore("p3TH")
            sHF = nc.alloc_semaphore("p3HF")
            sGU = nc.alloc_semaphore("p3GU")
            sSL = nc.alloc_semaphore("p3SL")
            sHH = nc.alloc_semaphore("p3HH")
            sQV = nc.alloc_semaphore("p3QV")
            sTH2 = nc.alloc_semaphore("p3TH2")
            sHQ = nc.alloc_semaphore("p3HQ")
            sHO = nc.alloc_semaphore("p3HO")

            with nc.Block() as blk:
                @blk.gpsimd
                def _(g_):
                    for t in range(32):
                        b = t // 8
                        p = t % 2
                        row0 = t * 128
                        xrow0 = b * TE + W + (t % 8) * 128
                        if t % 8 == 0:
                            for vi, (srcF, soff) in enumerate(
                                    ((gaF, 0), (cb1F, 0), (modF, 96))):
                                if b > 0 or vi > 0:
                                    g_.wait_ge(sRP, 6 * b + 2 * vi)
                                for ci in range(CCH):
                                    col = soff + ci * 4 + b
                                    g_.dma_start(
                                        vrow[0:1, ci * 128:(ci + 1) * 128],
                                        srcF[:, col:col + 1]).then_inc(sVR, 16)
                                g_.wait_ge(sVR, 16 * CCH * (3 * b + vi + 1))
                        if t >= 1:
                            g_.wait_ge(sRW, t)
                        if t >= 2:
                            g_.wait_ge(sX2, 2 * (t - 1))
                        for ci in range(CCH):
                            g_.dma_start(yt[p][:, ci * 128:(ci + 1) * 128],
                                         yscr[ci, :, row0:row0 + 128]
                                         ).then_inc(sLD3[p], 16)
                            g_.dma_start(srx[p][:, ci * 128:(ci + 1) * 128],
                                         srscr[ci * 128:(ci + 1) * 128,
                                               row0:row0 + 128]
                                         ).then_inc(sLD3[p], 16)
                        g_.dma_start(xrow[p][:], xa[xrow0:xrow0 + 128, :]
                                     ).then_inc(sLD3[p], 16)
                        g_.wait_ge(sLD3[p], 17 * 16 * (t // 2 + 1))

                @blk.tensor
                def _(te):
                    for t in range(32):
                        b = t // 8
                        if t % 8 == 0:
                            if t >= 8:
                                te.wait_ge(sHH, 6 * t)
                            for vi in range(3):
                                te.wait_ge(sVR, 16 * CCH * (3 * b + vi + 1))
                                for half in range(2):
                                    k2 = 2 * (3 * b + vi) + half
                                    if k2 >= 1:
                                        te.wait_ge(sRB, k2)
                                    te.matmul(ps4[:], ones1[0:1, :],
                                              vrow[0:1, half * 512:(half + 1) * 512],
                                              start=True, stop=True
                                              ).then_inc(sRP, 1)
                        te.wait_ge(sRW, t + 1)
                        if t >= 1:
                            te.wait_ge(sX2, 2 * t)
                            te.wait_ge(sSL, 6 * t - 1)
                        for ch in range(2):
                            for ci in range(CCH):
                                inst = te.matmul(
                                    ps[ch][:],
                                    rwkv[:, ci * 128:(ci + 1) * 128],
                                    wob[:, ci * C + ch * 512:ci * C + ch * 512 + 512],
                                    start=(ci == 0), stop=(ci == CCH - 1))
                            inst.then_inc(sWO, 1)
                        te.wait_ge(sQH, t + 1)
                        if t >= 2:
                            te.wait_ge(sHQ, 3 * (t - 1))
                        for g in range(2):
                            for k in range(4):
                                ci = g * 4 + k
                                inst = te.transpose(
                                    psb[g][:, k * 128:(k + 1) * 128],
                                    hqt[:, ci * 128:(ci + 1) * 128], idtb[:])
                        inst.then_inc(sTH, 1)
                        te.wait_ge(sHF, t + 1)
                        te.wait_ge(sX2, 2 * t + 2)
                        for ic in range(6):
                            n = 512 if ic < 5 else 256
                            pg = ps[(ic % 2) * 2]
                            pu = (ps[1], ps[3], ps4)[ic % 3]
                            if 6 * t + ic - 1 >= 1:
                                te.wait_ge(sSL, 6 * t + ic - 1)
                            if 6 * t + ic - 2 >= 1:
                                te.wait_ge(sHH, 6 * t + ic - 2)
                            if t % 8 == 0 and t >= 0 and ic == 2:
                                te.wait_ge(sRB, 6 * (t // 8 + 1))
                            for ci in range(CCH):
                                lhs = hqF[:, ci * 128:(ci + 1) * 128]
                                te.matmul(pg[:, 0:n], lhs,
                                          wgqT[:, ci * DI + ic * 512:ci * DI + ic * 512 + n],
                                          start=(ci == 0), stop=(ci == CCH - 1))
                                inst = te.matmul(
                                    pu[:, 0:n], lhs,
                                    wuqT[:, ci * DI + ic * 512:ci * DI + ic * 512 + n],
                                    start=(ci == 0), stop=(ci == CCH - 1))
                            inst.then_inc(sGU, 1)
                        if t >= 1:
                            te.wait_ge(sQV, t)
                            te.wait_ge(sHF, t + 1)
                            for gi, g0 in enumerate(range(0, ICH, 8)):
                                gn = min(8, ICH - g0)
                                if gi >= 1:
                                    te.wait_ge(sHQ, 3 * (t - 1) + gi)
                                for k in range(gn):
                                    inst = te.transpose(
                                        psb[k // 4][:, (k % 4) * 128:
                                                    (k % 4 + 1) * 128],
                                        hhqt[:, (g0 + k) * 128:(g0 + k + 1) * 128],
                                        idtb[:])
                                inst.then_inc(sTH2, 1)
                    te.wait_ge(sQV, 32)
                    te.wait_ge(sHQ, 3 * 31)
                    for gi, g0 in enumerate(range(0, ICH, 8)):
                        gn = min(8, ICH - g0)
                        if gi >= 1:
                            te.wait_ge(sHQ, 3 * 31 + gi)
                        for k in range(gn):
                            inst = te.transpose(
                                psb[k // 4][:, (k % 4) * 128:(k % 4 + 1) * 128],
                                hhqt[:, (g0 + k) * 128:(g0 + k + 1) * 128],
                                idtb[:])
                        inst.then_inc(sTH2, 1)

                @blk.vector
                def _(ve):
                    for t in range(32):
                        b = t // 8
                        p = t % 2
                        if t % 8 == 0:
                            for vi, bc in enumerate((ga_bc, cb1_bc, sb_bc)):
                                for half in range(2):
                                    ve.wait_ge(sRP, 2 * (3 * b + vi) + half + 1)
                                    ve.tensor_copy(bc[:, half * 512:(half + 1) * 512],
                                                   ps4[:]).then_inc(sRB, 1)
                        ve.wait_ge(sLD3[p], 17 * 16 * (t // 2 + 1))
                        if t >= 1:
                            ve.wait_ge(sWO, 2 * t)
                        ve.tensor_tensor(rwkv, srx[p], yt[p], op=ALU.mult
                                         ).then_inc(sRW, 1)
                        for ch in range(2):
                            ve.wait_ge(sWO, 2 * t + ch + 1)
                            if t >= 1 and ch == 0:
                                ve.wait_ge(sXO, 16 * t)
                            drs(ve)
                            ve.tensor_tensor(atmp[:], ps[ch][:],
                                             ga_bc[:, ch * 512:(ch + 1) * 512],
                                             op=ALU.mult)
                            drs(ve)
                            ve.tensor_tensor(x2t[:, ch * 512:(ch + 1) * 512],
                                             atmp[:],
                                             xrow[p][:, ch * 512:(ch + 1) * 512],
                                             op=ALU.add).then_inc(sX2, 1)
                        drs(ve)
                        ve.bn_stats(st6[:, 0:6], x2t[:, 0:512])
                        ve.bn_stats(st6[:, 6:12], x2t[:, 512:1024])
                        dr(ve)
                        ve.bn_aggr(mv[:], st6[:]).then_inc(sBN2, 1)
                        ve.wait_ge(sSQ2, t + 1)
                        ve.reciprocal(sinv[:], sd[:])
                        dr(ve)
                        ve.scalar_tensor_tensor(nbv[:], mv[:, 0:1], -1.0, sinv[:],
                                                op0=ALU.mult, op1=ALU.mult
                                                ).then_inc(sNB2, 1)
                        ve.wait_ge(sAF2, t + 1)
                        ve.tensor_tensor(h2t, h2t, cb1_bc[:], op=ALU.mult)
                        drs(ve)
                        ve.tensor_tensor(h2t, h2t, sb_bc[:], op=ALU.add
                                         ).then_inc(sAF3, 1)
                        ve.wait_ge(sABq, t + 1)
                        if t >= 1:
                            ve.wait_ge(sABh, t)
                        srcs = [(h2t, 0, 64)]
                        if t >= 1:
                            srcs += [(hh[:, 0:HV], 64, HV // 16),
                                     (hh[:, HV:DI], 64 + HV // 16,
                                      (DI - HV) // 16)]
                        emit_qd_head(ve, srcs, qd_m, qd_s, qd_r, qd_sh)
                        inst = emit_qd_tail_s(ve, h2t, hqt, hqt, qd_r[:, 0:64],
                                              qd_sh[:, 0:64], ta2, tacc2, tb2,
                                              C // 16)
                        inst.then_inc(sQH, 1)
                        def mul(ic):
                            n = 512 if ic < 5 else 256
                            pu = (ps[1], ps[3], ps4)[ic % 3]
                            ve.wait_ge(sSL, 6 * t + ic + 1)
                            ve.tensor_tensor(hh[:, ic * 512:ic * 512 + n],
                                             sg3[ic % 3][:, 0:n], pu[:, 0:n],
                                             op=ALU.mult).then_inc(sHH, 1)
                        if t >= 1:
                            emit_qd_tail_s(ve, hh[:, 0:HV], hhqt[:, 0:HV],
                                           hhqt[:, 0:HV],
                                           qd_r[:, 64:64 + HV // 16],
                                           qd_sh[:, 64:64 + HV // 16],
                                           ta2, tacc2, tb2, HV // 16)
                        drs(ve)
                        mul(0)
                        mul(1)
                        mul(2)
                        if t >= 1:
                            inst = emit_qd_tail_s(ve, hh[:, HV:DI],
                                                  hhqt[:, HV:DI], hhqt[:, HV:DI],
                                                  qd_r[:, 64 + HV // 16:240],
                                                  qd_sh[:, 64 + HV // 16:240],
                                                  ta2, tacc2, tb2,
                                                  (DI - HV) // 16)
                            inst.then_inc(sQV, 1)
                        drs(ve)
                        for ic in range(3, 6):
                            mul(ic)
                    ve.wait_ge(sABh, 32)
                    emit_qd_head(ve, [(hh[:, 0:HV], 64, HV // 16),
                                      (hh[:, HV:DI], 64 + HV // 16,
                                       (DI - HV) // 16)],
                                 qd_m, qd_s, qd_r, qd_sh)
                    emit_qd_tail_s(ve, hh[:, 0:HV], hhqt[:, 0:HV],
                                   hhqt[:, 0:HV],
                                   qd_r[:, 64:64 + HV // 16],
                                   qd_sh[:, 64:64 + HV // 16],
                                   ta2, tacc2, tb2, HV // 16)
                    drs(ve)
                    inst = emit_qd_tail_s(ve, hh[:, HV:DI], hhqt[:, HV:DI],
                                          hhqt[:, HV:DI],
                                          qd_r[:, 64 + HV // 16:240],
                                          qd_sh[:, 64 + HV // 16:240],
                                          ta2, tacc2, tb2, (DI - HV) // 16)
                    inst.then_inc(sQV, 1)

                @blk.scalar
                def _(sc):
                    for t in range(32):
                        sc.wait_ge(sBN2, t + 1)
                        sc.activation(sd[:], mv[:, 1:2], AF.Sqrt,
                                      bias=epsv[:, 0:1]).then_inc(sSQ2, 1)
                        sc.wait_ge(sNB2, t + 1)
                        if t >= 1:
                            sc.wait_ge(sQH, t)
                        sc.activation(h2t, x2t, AF.Identity,
                                      bias=nbv[:, 0:1], scale=sinv[:, 0:1]
                                      ).then_inc(sAF2, 1)
                        if t >= 1:
                            sc.wait_ge(sHH, 6 * t)
                            sc.wait_ge(sTH2, 3 * (t - 1))
                            sc.activation(hhqt[:], hh[:], AF.Sign)
                            drs(sc)
                            sc.activation(hh[:], hh[:], AF.Abs).then_inc(sABh, 1)
                        sc.wait_ge(sAF3, t + 1)
                        sc.wait_ge(sTH, t)
                        sc.activation(hqt[:], h2t[:], AF.Sign)
                        drs(sc)
                        sc.activation(h2t[:], h2t[:], AF.Abs).then_inc(sABq, 1)
                        sc.wait_ge(sTH, t + 1)
                        if t >= 1:
                            sc.wait_ge(sGU, 6 * t)
                        sc.activation(hqF[:, 0:512], psb[0][:], AF.Copy)
                        inst = sc.activation(hqF[:, 512:1024], psb[1][:], AF.Copy)
                        inst.then_inc(sHF, 1)
                        for ic in range(6):
                            n = 512 if ic < 5 else 256
                            pg = ps[(ic % 2) * 2]
                            sc.wait_ge(sGU, 6 * t + ic + 1)
                            if 6 * t + ic - 2 >= 1:
                                sc.wait_ge(sHH, 6 * t + ic - 2)
                            sc.activation(sg3[ic % 3][:, 0:n], pg[:, 0:n],
                                          AF.Silu).then_inc(sSL, 1)
                        if t >= 1:
                            for gi, g0 in enumerate(range(0, ICH, 8)):
                                gn = min(8, ICH - g0)
                                sc.wait_ge(sTH2, 3 * (t - 1) + gi + 1)
                                if t >= 2 and gi == 0:
                                    sc.wait_ge(sHO, 16 * (t - 1))
                                for k in range(gn):
                                    inst = sc.activation(
                                        hhqF[(t - 1) % 2][:, (g0 + k) * 128:
                                                          (g0 + k + 1) * 128],
                                        psb[k // 4][:, (k % 4) * 128:
                                                    (k % 4 + 1) * 128], AF.Copy)
                                inst.then_inc(sHQ, 1)
                    sc.wait_ge(sHH, 6 * 32)
                    sc.wait_ge(sTH2, 3 * 31)
                    sc.activation(hhqt[:], hh[:], AF.Sign)
                    drs(sc)
                    sc.activation(hh[:], hh[:], AF.Abs).then_inc(sABh, 1)
                    for gi, g0 in enumerate(range(0, ICH, 8)):
                        gn = min(8, ICH - g0)
                        sc.wait_ge(sTH2, 3 * 31 + gi + 1)
                        if gi == 0:
                            sc.wait_ge(sHO, 16 * 31)
                        for k in range(gn):
                            inst = sc.activation(
                                hhqF[31 % 2][:, (g0 + k) * 128:(g0 + k + 1) * 128],
                                psb[k // 4][:, (k % 4) * 128:(k % 4 + 1) * 128],
                                AF.Copy)
                        inst.then_inc(sHQ, 1)

                @blk.sync
                def _(sync):
                    for t in range(32):
                        p = t % 2
                        row0 = t * 128
                        sync.wait_ge(sX2, 2 * t + 2)
                        if t >= 1:
                            sync.wait_ge(sXO, 16 * t)
                        sync.dma_start(x2scr[row0:row0 + 128, :], x2t
                                       ).then_inc(sXO, 16)
                        sync.wait_ge(sHQ, 3 * (t + 1))
                        if t >= 1:
                            sync.wait_ge(sHO, 16 * t)
                        sync.dma_start(hqscr[t, :, :], hhqF[p]).then_inc(sHO, 16)
                    sync.wait_ge(sXO, 16 * 32)
                    sync.wait_ge(sHO, 16 * 32)

    with ExitStack() as stk3:
        def sb3(name, shape, dt):
            return stk3.enter_context(nc.sbuf_tensor(name, shape, dt)).ap()
        wdqT = sb3("wdqT", [128, ICH * C], BF16)
        ps4b = ps4
        with ExitStack() as stk4:
            def sb4(name, shape, dt):
                return stk4.enter_context(nc.sbuf_tensor(name, shape, dt)).ap()
            wscr3 = [sb4(f"wscr3_{i}", [128, DI], F32) for i in range(2)]
            axs3 = [sb4(f"axs3_{i}", [128, DI], F32) for i in range(2)]
            sgs3 = [sb4(f"sgs3_{i}", [128, DI], F32) for i in range(2)]
            ta3 = sb4("qta3", [128, DI], F32)
            tacc3 = sb4("qtacc3", [128, DI], F32)
            tb3 = sb4("qtb3", [128, DI], F32)
            qtile3 = [sb4(f"qtile3_{i}", [128, DI], BF16) for i in range(2)]
            qdw_m = sb4("qdw_m", [128, 2 * (DI // 16)], F32)
            qdw_s = sb4("qdw_s", [128, 2 * (DI // 16)], F32)
            qdw_r = sb4("qdw_r", [128, 2 * (DI // 16)], F32)
            qdw_sh = sb4("qdw_sh", [128, 2 * (DI // 16)], F32)
            sWL4 = [nc.alloc_semaphore(f"p4WL{i}") for i in range(2)]
            sAS4 = nc.alloc_semaphore("p4AS")
            sQ4 = nc.alloc_semaphore("p4Q4")
            sT4 = nc.alloc_semaphore("p4T4")
            sE4 = nc.alloc_semaphore("p4E4")

            with nc.Block() as blk:
                @blk.gpsimd
                def _(g_):
                    for cc in range(CCH):
                        p = cc % 2
                        if cc >= 2:
                            g_.wait_ge(sQ4, cc - 1)
                        g_.dma_start(wscr3[p], wd_d[cc * 128:(cc + 1) * 128, :]
                                     ).then_inc(sWL4[p], 16)
                        g_.wait_ge(sWL4[p], 16 * (cc // 2 + 1))

                @blk.scalar
                def _(sc):
                    for cc in range(CCH):
                        p = cc % 2
                        sc.wait_ge(sWL4[p], 16 * (cc // 2 + 1))
                        if cc >= 2:
                            sc.wait_ge(sQ4, cc - 1)
                        sc.activation(axs3[p][:], wscr3[p][:], AF.Abs)
                        sc.activation(sgs3[p][:], wscr3[p][:], AF.Sign
                                      ).then_inc(sAS4, 1)
                        for gi, g0 in enumerate(range(0, ICH, 8)):
                            gn = min(8, ICH - g0)
                            sc.wait_ge(sT4, 3 * cc + gi + 1)
                            for k in range(gn):
                                inst = sc.activation(
                                    wdqT[:, (g0 + k) * C + cc * 128:
                                         (g0 + k) * C + (cc + 1) * 128],
                                    psb[k // 4][:, (k % 4) * 128:
                                                (k % 4 + 1) * 128], AF.Copy)
                            inst.then_inc(sE4, 1)

                @blk.vector
                def _(ve):
                    NB3 = DI // 16
                    for c2 in range(CCH // 2):
                        cc0 = 2 * c2
                        ve.wait_ge(sWL4[0], 16 * (c2 + 1))
                        ve.wait_ge(sWL4[1], 16 * (c2 + 1))
                        if c2 >= 1:
                            ve.wait_ge(sT4, 3 * (cc0 - 1))
                        emit_qd_head(ve, [(wscr3[0], 0, NB3),
                                          (wscr3[1], NB3, NB3)],
                                     qdw_m, qdw_s, qdw_r, qdw_sh)
                        for k in range(2):
                            cc = cc0 + k
                            ve.wait_ge(sAS4, cc + 1)
                            if k == 1 and c2 >= 1:
                                ve.wait_ge(sT4, 3 * cc0)
                            inst = emit_qd_tail_s(
                                ve, axs3[k], sgs3[k], qtile3[k],
                                qdw_r[:, k * NB3:(k + 1) * NB3],
                                qdw_sh[:, k * NB3:(k + 1) * NB3],
                                ta3, tacc3, tb3, NB3)
                            inst.then_inc(sQ4, 1)

                @blk.tensor
                def _(te):
                    for cc in range(CCH):
                        p = cc % 2
                        te.wait_ge(sQ4, cc + 1)
                        for gi, g0 in enumerate(range(0, ICH, 8)):
                            gn = min(8, ICH - g0)
                            if 3 * cc + gi >= 1:
                                te.wait_ge(sE4, 3 * cc + gi)
                            for k in range(gn):
                                inst = te.transpose(
                                    psb[k // 4][:, (k % 4) * 128:(k % 4 + 1) * 128],
                                    qtile3[p][:, (g0 + k) * 128:(g0 + k + 1) * 128],
                                    idtb[:])
                            inst.then_inc(sT4, 1)

        with ExitStack() as stk5:
            def sb5(name, shape, dt):
                return stk5.enter_context(nc.sbuf_tensor(name, shape, dt)).ap()
            hhqF2 = [sb5(f"hhqF2_{i}", [128, ICH * 128], BF16) for i in range(2)]
            x2r = [sb5(f"x2r{i}", [128, C], F32) for i in range(2)]
            gb_bc = sb5("gb_bc", [128, C], F32)
            ot = [sb5(f"ot{i}", [128, C], F32) for i in range(2)]
            dtmp = sb5("dtmp", [128, 512], F32)
            sLD5 = [nc.alloc_semaphore(f"p5LD{i}") for i in range(2)]
            sVR5 = nc.alloc_semaphore("p5VR")
            sRP5 = nc.alloc_semaphore("p5RP")
            sRB5 = nc.alloc_semaphore("p5RB")
            sDM = nc.alloc_semaphore("p5DM")
            sDE = nc.alloc_semaphore("p5DE")
            soo = [nc.alloc_semaphore(f"p5OO{i}") for i in range(2)]

            with nc.Block() as blk:
                @blk.gpsimd
                def _(g_):
                    for t in range(32):
                        b = t // 8
                        p = t % 2
                        row0 = t * 128
                        if t % 8 == 0:
                            if b > 0:
                                g_.wait_ge(sRP5, 2 * b)
                            for ci in range(CCH):
                                col = ci * 4 + b
                                g_.dma_start(vrow[0:1, ci * 128:(ci + 1) * 128],
                                             gbF[:, col:col + 1]).then_inc(sVR5, 16)
                            g_.wait_ge(sVR5, 16 * CCH * (b + 1))
                        if t >= 2:
                            g_.wait_ge(sDE, 2 * (t - 1))
                        g_.dma_start(x2r[p][:], x2scr[row0:row0 + 128, :]
                                     ).then_inc(sLD5[p], 16)
                        g_.dma_start(hhqF2[p][:], hqscr[t, :, :]
                                     ).then_inc(sLD5[p], 16)
                        g_.wait_ge(sLD5[p], 32 * (t // 2 + 1))

                @blk.tensor
                def _(te):
                    for t in range(32):
                        b = t // 8
                        p = t % 2
                        if t % 8 == 0:
                            te.wait_ge(sVR5, 16 * CCH * (b + 1))
                            for half in range(2):
                                k2 = 2 * b + half
                                if k2 >= 1:
                                    te.wait_ge(sRB5, k2)
                                te.matmul(ps4b[:], ones1[0:1, :],
                                          vrow[0:1, half * 512:(half + 1) * 512],
                                          start=True, stop=True).then_inc(sRP5, 1)
                        te.wait_ge(sLD5[p], 32 * (t // 2 + 1))
                        for ch in range(2):
                            if 2 * t + ch >= 2:
                                te.wait_ge(sDE, 2 * t + ch - 1)
                            for ici in range(ICH):
                                inst = te.matmul(
                                    ps[ch % 2][:],
                                    hhqF2[p][:, ici * 128:(ici + 1) * 128],
                                    wdqT[:, ici * C + ch * 512:ici * C + ch * 512 + 512],
                                    start=(ici == 0), stop=(ici == ICH - 1))
                            inst.then_inc(sDM, 1)

                @blk.vector
                def _(ve):
                    for t in range(32):
                        b = t // 8
                        p = t % 2
                        if t % 8 == 0:
                            for half in range(2):
                                ve.wait_ge(sRP5, 2 * b + half + 1)
                                ve.tensor_copy(gb_bc[:, half * 512:(half + 1) * 512],
                                               ps4b[:]).then_inc(sRB5, 1)
                        for ch in range(2):
                            ve.wait_ge(sDM, 2 * t + ch + 1)
                            if t >= 2 and ch == 0:
                                ve.wait_ge(soo[t % 2], 16 * (t // 2))
                            drs(ve)
                            ve.tensor_tensor(dtmp[:], ps[ch % 2][:],
                                             gb_bc[:, ch * 512:(ch + 1) * 512],
                                             op=ALU.mult)
                            drs(ve)
                            ve.tensor_tensor(ot[p][:, ch * 512:(ch + 1) * 512],
                                             dtmp[:],
                                             x2r[p][:, ch * 512:(ch + 1) * 512],
                                             op=ALU.add).then_inc(sDE, 1)

                @blk.sync
                def _(sync):
                    for t in range(32):
                        p = t % 2
                        row0 = t * 128
                        sync.wait_ge(sDE, 2 * t + 2)
                        if t >= 2:
                            sync.wait_ge(soo[p], 16 * (t // 2))
                        sync.dma_start(outo[row0:row0 + 128, :], ot[p]
                                       ).then_inc(soo[p], 16)
                    sync.wait_ge(soo[0], 16 * 16)
                    sync.wait_ge(soo[1], 16 * 16)
    return nc



def _fm32(vec):
    return np.ascontiguousarray(vec.reshape(CCH, 128).T)


def kernel(L0_attn_x, L0_attn_time_decay, L0_attn_time_first, L0_attn_time_mix_k,
           L0_attn_time_mix_v, L0_attn_time_mix_r, L0_attn_Wk, L0_attn_Wv,
           L0_attn_Wr, L0_attn_Wo, L0_ffn_w_gate, L0_ffn_w_up, L0_ffn_w_down,
           L0_c, L0_wmod):
    x = np.ascontiguousarray(np.asarray(L0_attn_x, np.float32))
    td = np.asarray(L0_attn_time_decay, np.float32).reshape(C)
    tf = np.asarray(L0_attn_time_first, np.float32).reshape(C)
    tmk = np.asarray(L0_attn_time_mix_k, np.float32).reshape(C)
    tmv = np.asarray(L0_attn_time_mix_v, np.float32).reshape(C)
    tmr = np.asarray(L0_attn_time_mix_r, np.float32).reshape(C)
    Wk = np.asarray(L0_attn_Wk, np.float32)
    Wv = np.asarray(L0_attn_Wv, np.float32)
    Wr = np.asarray(L0_attn_Wr, np.float32)
    Wo = np.asarray(L0_attn_Wo, np.float32)
    wg = np.ascontiguousarray(np.asarray(L0_ffn_w_gate, np.float32))
    wu = np.ascontiguousarray(np.asarray(L0_ffn_w_up, np.float32))
    wd = np.ascontiguousarray(np.asarray(L0_ffn_w_down, np.float32))
    cv = np.asarray(L0_c, np.float32)
    wmod = np.ascontiguousarray(np.asarray(L0_wmod, np.float32))

    wkT = np.ascontiguousarray(Wk.T)
    wvT = np.ascontiguousarray(Wv.T)
    wrT = np.ascontiguousarray(Wr.T)
    woT = np.ascontiguousarray(Wo.T)
    tmH = np.ascontiguousarray(
        np.stack([_fm32(tmk), _fm32(tmv), _fm32(tmr)], axis=1).reshape(128, 3 * CCH))
    tdH = _fm32(td)
    tfH = _fm32(tf)
    cvF = np.ascontiguousarray(
        cv.reshape(B, CCH, 128).transpose(2, 1, 0).reshape(128, 4 * CCH))

    nc = build_fused()
    in_maps = []
    for j in range(NCORE):
        t0 = j * TK
        xaj = np.zeros((B, TE, C), np.float32)
        lo = max(0, t0 - W)
        xaj[:, W - (t0 - lo):W, :] = x[:, lo:t0, :]
        xaj[:, W:, :] = x[:, t0:t0 + TK, :]
        xaj = np.ascontiguousarray(xaj.reshape(B * TE, C))
        if j > 0:
            xbj = np.ascontiguousarray(x[:, t0 - W - 1, :])
            bm = np.ones((128, 1), np.float32)
        else:
            xbj = np.zeros((B, C), np.float32)
            bm = np.zeros((128, 1), np.float32)
        in_maps.append({"xa": xaj, "xb": xbj, "bmask": bm, "wkT": wkT, "wvT": wvT,
                        "wrT": wrT, "tmH": tmH, "tdH": tdH, "tfH": tfH,
                        "cvF": cvF, "wmod": wmod, "woT": woT,
                        "wg": wg, "wu": wu, "wd": wd})
    res = run_bass_kernel_spmd(nc, in_maps, core_ids=list(range(NCORE)))

    r0 = np.asarray(res.results[0]["outo"])
    out = np.empty((B, T, C), r0.dtype)
    for j in range(NCORE):
        out[:, j * TK:(j + 1) * TK, :] = res.results[j]["outo"].reshape(B, TK, C)
    return out
